# version 12
# baseline (speedup 1.0000x reference)
import contextlib
import math
import sys

import numpy as np
import ml_dtypes

import concourse.bass as bass
import concourse.mybir as mybir
from concourse.bass_utils import run_bass_kernel_spmd

F32 = mybir.dt.float32
BF16 = mybir.dt.bfloat16
F16 = mybir.dt.float16
I16 = mybir.dt.int16
I8 = mybir.dt.int8
F8E4 = mybir.dt.float8e4
M64C = 6755399441055744.0
OP = mybir.AluOpType
AF = mybir.ActivationFunctionType
BF = ml_dtypes.bfloat16
AX = mybir.AxisListType.X

S, D = 2048, 1024
NT, ND = S // 128, D // 128
NH, NPAIR = 16, 8
BLK, NBLK = 512, 4
HID = 2730
HID_T = [128] * 21 + [42]
NKH = len(HID_T)
H4 = 4096
NJ = H4 // 128
FTB, NFTB = 256, 8
EPS1, EPS2 = 1e-5, 1e-6

ENGS = ("sync", "vector", "scalar", "gpsimd", "tensor")
ESEM = {e: "E_" + e for e in ENGS}
DMACH = ("d_x", "d_w", "d_w2", "d_t", "d_b", "d_m", "d_o", "d_c", "d_q", "d_g", "d_fa", "d_fb", "d_fc", "d_fd")


class Buf:
    __slots__ = ("w", "r")

    def __init__(self):
        self.w = None
        self.r = []


class Em:
    def __init__(self):
        self.q = {e: [] for e in ENGS}
        self.cnt = {e: 0 for e in ENGS}
        self.dcnt = {ch: 0 for ch in DMACH}

    def _wait(self, eng, sem_name, val):
        def f(ctx, eng=eng, sem_name=sem_name, val=val):
            key = (eng, sem_name)
            if ctx["waited"].get(key, -1) >= val:
                return
            ctx["waited"][key] = val
            ctx["eng"][eng].wait_ge(ctx["sem"][sem_name], val)

        self.q[eng].append(f)

    def do(self, eng, build, ins=(), outs=(), ch=None, extra=(), inc=True):
        fr = sys._getframe(1)
        site = f"{fr.f_code.co_name}:{fr.f_lineno}"
        fr2 = fr.f_back
        if fr2 is not None:
            site += f" < {fr2.f_code.co_name}:{fr2.f_lineno}"
        _build0 = build

        def build(e, c, _b=_build0, _site=site):
            try:
                return _b(e, c)
            except Exception as ex:
                raise RuntimeError(f"emit failed at {_site}: {ex}") from ex

        waits = []
        for b in ins:
            if b.w is not None:
                waits.append(b.w)
        for b in outs:
            if b.w is not None:
                waits.append(b.w)
            waits.extend(b.r)
        waits.extend(t for t in extra if t is not None)
        seen = {}
        for s_, v_ in waits:
            if s_ in self.dcnt:
                v_ = self.dcnt[s_]
            if seen.get(s_, -1) < v_:
                seen[s_] = v_
        for s_, v_ in sorted(seen.items()):
            if s_ == ESEM[eng] and v_ > self.cnt[eng]:
                continue
            self._wait(eng, s_, v_)
        if ch is not None:
            if self.dcnt[ch] > 0:
                self._wait(eng, ch, self.dcnt[ch])
            self.dcnt[ch] += 16
            val = self.dcnt[ch]

            def f(ctx, build=build, eng=eng, ch=ch):
                build(ctx["eng"][eng], ctx).then_inc(ctx["sem"][ch], 16)

            self.q[eng].append(f)
            tok = (ch, val)
        elif inc:
            self.cnt[eng] += 1
            val = self.cnt[eng]

            def f(ctx, build=build, eng=eng, sem_name=ESEM[eng]):
                build(ctx["eng"][eng], ctx).then_inc(ctx["sem"][sem_name], 1)

            self.q[eng].append(f)
            tok = (ESEM[eng], val)
        else:
            def f(ctx, build=build, eng=eng):
                build(ctx["eng"][eng], ctx)

            self.q[eng].append(f)
            tok = None
        if tok is None and ch is None:
            tok_eff = (ESEM[eng], self.cnt[eng] + 1)
        else:
            tok_eff = tok
        if tok_eff is not None:
            for b in ins:
                b.r.append(tok_eff)
            for b in outs:
                b.w = tok_eff
                b.r = []
        return tok


def _pc(h):
    g = 1.0 - 2.0 ** (-5.0 - h)
    lng = math.log(g)
    hi = np.float32(lng)
    lo = np.float32(lng - float(hi))
    return float(hi), float(lo)


def build_nc(dbg=()):
    nc = bass.Bass("TRN2", target_bir_lowering=False, debug=False)
    em = Em()

    I = lambda n, sh, dt_: nc.dram_tensor(n, sh, dt_, kind="ExternalInput").ap()
    x_h = I("x", [S, D], F32)
    wq_h = I("wq", [D, D], F32)
    wk_h = I("wk", [D, D], F32)
    wv_h = I("wv", [D, D], F32)
    wg_h = I("wg", [D, D], F32)
    wo_h = I("wo", [D, D], F32)
    ln_h = {n: I(n, [D], F32) for n in ("ln1w", "ln1b", "ln2w", "ln2b")}
    win_h = I("win", [D, 2 * HID], F32)
    bin_h = I("bin", [2 * HID], F32)
    wout_h = I("wout", [HID, D], F32)
    bout_h = I("bout", [D], F32)
    w1_h = I("w1", [H4, D], F32)
    b1_h = I("b1", [H4], F32)
    w2_h = I("w2", [D, H4], F32)
    b2_h = I("b2", [D], F32)
    mask_h = I("mask", [S, D], F16)
    trel_h = I("trel", [128, 128], F32)
    tip1_h = I("tip1", [128, 128], F32)
    trev_h = I("trev", [128, 1], F32)
    ident_h = I("ident", [128, 128], F32)
    out_h = nc.dram_tensor("out", [S, D], F32, kind="ExternalOutput").ap()
    _dbg_shapes = {"hq": [H4, S], "ym": [S, D]}
    dbg_h = {n: nc.dram_tensor("dbg_" + n, _dbg_shapes.get(n, [S, D]), F32, kind="ExternalOutput").ap() for n in dbg}

    T = lambda n, sh, dt_: nc.dram_tensor(n, sh, dt_, kind="Internal").ap()
    xn_d = T("xn_d", [S, D], BF16)
    y_d = T("y_d", [S, D], BF16)
    xm_d = T("xm_d", [S, D], BF16)
    xq_d = T("xq_d", [S, D], BF16)
    x1_d = T("x1_d", [S, D], BF16)
    w1q_d = T("w1q_d", [H4, D], BF16)
    w1qT_d = T("w1qT_d", [D, H4], BF16)
    w2q_d = T("w2q_d", [D, H4], BF16)
    w2qT_d = T("w2qT_d", [H4, D], BF16)

    es = contextlib.ExitStack()
    with es:
        SBt = lambda n, sh, dt_: es.enter_context(nc.sbuf_tensor(n, sh, dt_))
        TFM = SBt("TFM", [128, ND * S], BF16)
        R0 = SBt("R0", [128, NT * D], BF16)
        PW = SBt("PW", [128, 22528], BF16)
        PB = SBt("PB", [128, 20480], BF16)
        WST = SBt("WST", [128, 2048], BF16)
        SC0 = SBt("SC0", [128, 1024], F32)
        SCF = SBt("SCF", [128, 1024], F32)
        SC1 = SBt("SC1", [128, 1024], BF16)
        SC2 = SBt("SC2", [128, 1024], BF16)
        SC3 = SBt("SC3", [128, 1024], BF16)
        SC4 = SBt("SC4", [128, 1024], BF16)
        SCI = SBt("SCI", [128, 1024], I16)
        SCI2 = SBt("SCI2", [128, 1024], I16)
        SCH1 = SBt("SCH1", [128, 512], F16)
        SCH2 = SBt("SCH2", [128, 512], F16)
        SCH3 = SBt("SCH3", [128, 512], F16)
        SCH4 = SBt("SCH4", [128, 512], F16)
        SYE = SBt("SYE", [128, 1024], F16)
        QL16 = SBt("QL16", [128, 512], I16)
        GQ1 = SBt("GQ1", [128, 512], BF16)
        F8Q = SBt("F8Q", [128, 1024], F8E4)
        L8Q = SBt("L8Q", [128, 1024], I8)
        GQI = SBt("GQI", [128, 1024], I16)
        VST = SBt("VST", [128, 66], F32)
        TRASH = SBt("TRASH", [128, 1024], BF16)
        DKT = SBt("DKT", [128, NH * 128], BF16)
        GMT = SBt("GMT", [128, NPAIR * 128], BF16)
        KGV = SBt("KGV", [128, NH], F32)
        GCV = SBt("GCV", [128, NPAIR], F32)
        SBLK = SBt("SBLK", [128, NPAIR * 128], F32)
        CST = SBt("CST", [128, 128], F32)
        IDT = SBt("IDT", [128, 128], F32)
        LNV = SBt("LNV", [128, 4 * ND], F32)
        BIV = SBt("BIV", [128, 2 * NKH], F32)
        QB1 = SBt("QB1", [128, NJ], F32)
        ROWS = SBt("ROWS", [1, 1024], F32)
        ROWB = SBt("ROWB", [1, 2048], BF16)
        ONES = SBt("ONES", [1, 512], BF16)
        ps = [es.enter_context(nc.psum_tensor(f"ps{i}", [128, 512], F32)) for i in range(8)]
        sems = {n: es.enter_context(nc.semaphore(n)) for n in list(ESEM.values()) + list(DMACH)}
        block = es.enter_context(nc.Block())

        _bufs = {}

        def B_(key):
            if key not in _bufs:
                _bufs[key] = Buf()
            return _bufs[key]

        bPS = [B_(("ps", i)) for i in range(8)]
        bSC0, bSCF, bSC1, bSC2, bSC3, bSC4 = (B_(x) for x in ("SC0", "SCF", "SC1", "SC2", "SC3", "SC4"))
        bSCI, bSCI2 = B_("SCI"), B_("SCI2")
        bSH1, bSH2, bSH3, bSH4, bSYE = B_("SCH1"), B_("SCH2"), B_("SCH3"), B_("SCH4"), B_("SYE")
        bVST, bTRASH, bCST, bROWS, bROWB = B_("VST"), B_("TRASH"), B_("CST"), B_("ROWS"), B_("ROWB")
        bWST, bTFM, bPW, bPB = B_("WST"), B_("TFM"), B_("PW"), B_("PB")
        bSBLK, bIDT, bONES = B_("SBLK"), B_("IDT"), B_("ONES")
        bLNV, bBIV, bQB1 = B_("LNV"), B_("BIV"), B_("QB1")
        bDKT, bGMT, bKGV, bGCV = B_("DKT"), B_("GMT"), B_("KGV"), B_("GCV")
        bQX, bGQ1 = B_("qx"), B_("gq1")
        bR0 = [B_(("R0", t)) for t in range(NT)]
        bSLOT = [B_(("slot", s)) for s in range(2)]

        def TT(eng, out, a, b, op, ins, outs, extra=()):
            return em.do(eng, lambda e, c: e.tensor_tensor(out, a, b, op=op), ins, outs, extra=extra)

        def TS1(eng, out, a, s1, op, ins, outs, extra=()):
            return em.do(eng, lambda e, c: e.tensor_single_scalar(out, a, s1, op=op), ins, outs, extra=extra)

        def TS2(eng, out, a, s1, s2, op0, op1, ins, outs, extra=()):
            if eng == "gpsimd":
                t = em.do(eng, lambda e, c: e.tensor_single_scalar(out, a, s1, op=op0), ins, outs, extra=extra)
                return em.do(eng, lambda e, c: e.tensor_single_scalar(out, out, s2, op=op1), list(outs), list(outs))
            return em.do(eng, lambda e, c: e.tensor_scalar(out, a, s1, s2, op0=op0, op1=op1), ins, outs, extra=extra)

        def STT(out, a, s, b, op0, op1, ins, outs, extra=()):
            return em.do("vector", lambda e, c: e.scalar_tensor_tensor(out, a, s, b, op0=op0, op1=op1), ins, outs, extra=extra)

        def CP(eng, out, a, ins, outs, extra=()):
            return em.do(eng, lambda e, c: e.tensor_copy(out, a), ins, outs, extra=extra)

        def ACT(out, a, func, ins, outs, bias=0.0, scale=1.0, accum=None, extra=()):
            return em.do("scalar", lambda e, c: e.activation(out, a, func, bias=bias, scale=scale, accum_out=accum), ins, outs, extra=extra)

        def RED(out, a, op, ins, outs, extra=()):
            return em.do("vector", lambda e, c: e.tensor_reduce(out, a, axis=AX, op=op), ins, outs, extra=extra)

        def MS(eng, ap, v, ins, outs, extra=()):
            return em.do(eng, lambda e, c: e.memset(ap, v), ins, outs, extra=extra)

        def MM(po, lhsT, rhs, start, stop, ins, outs, extra=(), inc=True, tp=None):
            return em.do("tensor", lambda e, c: e.matmul(po, lhsT, rhs, start=start, stop=stop, tile_position=tp), ins, outs, extra=extra, inc=True)

        def TRANS(po, in_, ins, outs, extra=(), P=1):
            return em.do("tensor", lambda e, c: e.transpose(po, in_, IDT[0:P, 0:P]), list(ins) + [bIDT], outs, extra=extra, inc=True)

        def DMA(eng, out, in_, ch, ins, outs, extra=(), transpose=False):
            if transpose:
                return em.do(eng, lambda e, c: e.dma_start(out=out, in_=in_, transpose=True), ins, outs, ch=ch, extra=extra)
            return em.do(eng, lambda e, c: e.dma_start(out=out, in_=in_), ins, outs, ch=ch, extra=extra)

        def RCP(out, a, ins, outs, extra=()):
            return em.do("vector", lambda e, c: e.reciprocal(out, a), ins, outs, extra=extra)

        def qx(dst, src, W, fl, kind, ins, outs, n=128):
            (Fr, br), (Fg, bg), (Fc, bc), (Fl, bl), (Fn, bn), (Fx, bx), (Fb, bb) = \
                [(a[0:n, 0:W], b) for a, b in fl]
            ti, t2, pm = SCI[0:n, :W], SCI2[0:n, :W], GQI[0:n, :W]
            f8, l8 = F8Q[0:n, :W], L8Q[0:n, :W]
            V = "vector"
            if kind == "43bf":
                TS1(V, ti, src.bitcast(I16), 0x7F80, OP.bitwise_and, list(ins), [bQX])
                TS2(V, t2, ti, 121 << 7, 384, OP.max, OP.subtract, [bQX], [bQX])
                TS1(V, Fn, ti, 15616, OP.is_ge, [bQX], [bn])
                TS2(V, pm, t2, -1, 32512, OP.mult, OP.add, [bQX], [bQX])
                CP(V, f8, src, list(ins), [bQX])
            else:
                TS1(V, ti, src.bitcast(I16), 0x7C00, OP.bitwise_and, list(ins), [bQX])
                clampv = (9 << 10) if kind == "43f16" else (1 << 10)
                TS1(V, ti, ti, clampv, OP.max, [bQX], [bQX])
                TS1(V, Fn, ti, clampv + (1 << 10), OP.is_ge, [bQX], [bn])
                TS1(V, ti, ti, 3, OP.logical_shift_right, [bQX], [bQX])
                if kind == "43f16":
                    TS2(V, pm, ti, -1, (127 + 15 + 3) << 7, OP.mult, OP.add, [bQX], [bQX])
                    TS1(V, t2, ti, (127 - 15 - 3) << 7, OP.add, [bQX], [bQX])
                    CP(V, f8, src, list(ins), [bQX])
                else:
                    TS2(V, pm, ti, -1, (127 + 15 + 10) << 7, OP.mult, OP.add, [bQX], [bQX])
                    TS1(V, t2, ti, (127 - 15 - 10) << 7, OP.add, [bQX], [bQX])
                    CP(V, QL16[0:n, :W], src.bitcast(I16), list(ins), [bQX])
            TT(V, Fr, src, pm.bitcast(BF16), OP.mult, list(ins) + [bQX], [br])
            if kind == "510f16":
                CP(V, Fb, QL16[0:n, :W].bitcast(F16), [bQX], [bb])
            else:
                CP(V, l8, f8.bitcast(I8), [bQX], [bQX])
                CP(V, Fb, l8.bitcast(F8E4), [bQX], [bb])
            TT(V, Fg, Fr, Fr, OP.mult, [br], [bg])
            hi_t = 4194000.0 if kind == "510f16" else 255.9
            lo_t = 1048000.0 if kind == "510f16" else 63.9
            TS1(V, Fc, Fg, hi_t, OP.is_ge, [bg], [bc])
            TS1(V, Fl, Fg, lo_t, OP.is_le, [bg], [bl])
            TT(V, Fl, Fl, Fn, OP.mult, [bl, bn], [bl])
            STT(Fx, Fc, -0.5, Fl, OP.mult, OP.add, [bc, bl], [bx])
            STT(Fg, Fx, 1.0, Fr, OP.add, OP.mult, [bx, br], [bg])
            TS2(V, Fg, Fg, VST[0:n, 64:65], 0.0, OP.mult, OP.add, [bg, bVST], [bg])
            TS2(V, Fr, Fg, M64C, M64C, OP.add, OP.subtract, [bg], [br])
            STT(Fx, Fl, -0.5, Fc, OP.mult, OP.add, [bl, bc], [bx])
            TT(V, Fn, Fr, t2.bitcast(BF16), OP.mult, [br, bQX], [bn])
            STT(Fn, Fx, 1.0, Fn, OP.add, OP.mult, [bx, bn], [bn])
            TS1(V, pm, t2, 6720, OP.add, [bQX], [bQX])
            TT(V, Fl, Fb, pm.bitcast(BF16), OP.add, [bb, bQX], [bl])
            TT(V, Fl, Fl, pm.bitcast(BF16), OP.subtract, [bl, bQX], [bl])
            TT(V, Fb, Fn, Fb, OP.add, [bn, bb], [bb])
            return TT(V, dst, Fb, Fl, OP.subtract, [bb, bl], list(outs))

        MS("vector", VST[:, 64:65], 2.0 ** -41, [], [bVST])
        TS2("vector", VST[:, 64:65], VST[:, 64:65], -1.0, 1.0, OP.mult, OP.add, [bVST], [bVST])

        fl_wide = [(WST[:, 0:1024], bWST), (WST[:, 1024:2048], bWST), (TRASH, bTRASH),
                   (SC0, bSC0), (SCF, bSCF), (PB[:, 0:1024], bPB), (PB[:, 1024:2048], bPB)]
        fl_ffn = [(GQ1, bGQ1), (TRASH, bTRASH), (SC1, bSC1), (SC2, bSC2),
                  (SC3, bSC3), (SC4, bSC4), (SCH2, bSH2)]
        fl_tail = [(GQ1, bGQ1), (TRASH, bTRASH), (SC1, bSC1), (SC2, bSC2),
                   (SC3, bSC3), (SC4, bSC4), (SCH3, bSH3)]
        fl_wq = [(PB[:, 2048 + i * 1024:3072 + i * 1024], bPB) for i in range(7)]

        DMA("sync", IDT[:, :], ident_h, "d_c", [], [bIDT])
        MS("vector", ONES[:, :], 1.0, [], [bONES])
        DMA("sync", CST[:, :], trel_h, "d_c", [], [bCST])
        for h in range(NH):
            hi, lo = _pc(h)
            TS1("vector", SC0[:, :128], CST[:, :], hi, OP.mult, [bCST], [bSC0])
            TS1("vector", SCF[:, :128], CST[:, :], lo, OP.mult, [bCST], [bSCF])
            TT("vector", SC0[:, :128], SC0[:, :128], SCF[:, :128], OP.add, [bSC0, bSCF], [bSC0])
            ACT(DKT[:, h * 128:(h + 1) * 128], SC0[:, :128], AF.Exp, [bSC0], [bDKT])
        DMA("sync", CST[:, :], tip1_h, "d_c", [bCST], [bCST])
        for p in range(NPAIR):
            for half, h in ((0, 2 * p), (1, 2 * p + 1)):
                hi, lo = _pc(h)
                sl = slice(64 * half, 64 * half + 64)
                TS1("vector", SC0[sl, :128], CST[sl, :], hi, OP.mult, [bCST], [bSC0])
                TS1("vector", SCF[sl, :128], CST[sl, :], lo, OP.mult, [bCST], [bSCF])
            TT("vector", SC0[:, :128], SC0[:, :128], SCF[:, :128], OP.add, [bSC0, bSCF], [bSC0])
            ACT(GMT[:, p * 128:(p + 1) * 128], SC0[:, :128], AF.Exp, [bSC0], [bGMT])
        DMA("sync", CST[:, 0:1], trev_h, "d_c", [bCST], [bCST])
        for h in range(NH):
            hi, lo = _pc(h)
            TS1("vector", SC0[:, 0:1], CST[:, 0:1], hi, OP.mult, [bCST], [bSC0])
            TS1("vector", SCF[:, 0:1], CST[:, 0:1], lo, OP.mult, [bCST], [bSCF])
            TT("vector", SC0[:, 0:1], SC0[:, 0:1], SCF[:, 0:1], OP.add, [bSC0, bSCF], [bSC0])
            ACT(KGV[:, h:h + 1], SC0[:, 0:1], AF.Exp, [bSC0], [bKGV])
        for p in range(NPAIR):
            for half, h in ((0, 2 * p), (1, 2 * p + 1)):
                hi, lo = _pc(h)
                sl = slice(64 * half, 64 * half + 64)
                MS("vector", SC0[sl, 0:1], float(np.float32(hi)) * 128.0, [], [bSC0])
                MS("vector", SCF[sl, 0:1], float(np.float32(lo)) * 128.0, [], [bSCF])
            TT("vector", SC0[:, 0:1], SC0[:, 0:1], SCF[:, 0:1], OP.add, [bSC0, bSCF], [bSC0])
            ACT(GCV[:, p:p + 1], SC0[:, 0:1], AF.Exp, [bSC0], [bGCV])
        MS("vector", SBLK[:, :], 0.0, [], [bSBLK])
        bSBLK16 = B_("sblk16")
        MS("vector", PW[:, 16384:17408], 0.0, [], [bSBLK16])
        for v, name in enumerate(("ln1w", "ln1b", "ln2w", "ln2b")):
            DMA("sync", ROWS[0:1, 0:D], ln_h[name].rearrange("(o d) -> o d", o=1), "d_c", [bROWS], [bROWS])
            for cc in range(ND):
                TRANS(ps[7][:, 0:1], ROWS[0:1, cc * 128:cc * 128 + 128], [bROWS], [bPS[7]])
                ACT(LNV[:, v * ND + cc:v * ND + cc + 1], ps[7][:, 0:1], AF.Copy, [bPS[7]], [bLNV])
        LN1W, LN1B, LN2W, LN2B = 0, ND, 2 * ND, 3 * ND
        for half in range(2):
            for k in range(NKH):
                w = HID_T[k]
                DMA("sync", ROWS[0:1, 0:w], bin_h[half * HID + k * 128:half * HID + k * 128 + w].rearrange("(o d) -> o d", o=1),
                    "d_c", [bROWS], [bROWS])
                TRANS(ps[7][0:w, 0:1], ROWS[0:1, 0:w], [bROWS], [bPS[7]])
                ACT(BIV[0:w, half * NKH + k:half * NKH + k + 1], ps[7][0:w, 0:1], AF.Copy, [bPS[7]], [bBIV])
        DMA("sync", ROWS[0:1, 0:D], bout_h.rearrange("(o d) -> o d", o=1), "d_c", [bROWS], [bROWS])
        CP("vector", ROWB[0:1, 0:D], ROWS[0:1, 0:D], [bROWS], [bROWB])
        for cchunk in range(4):
            DMA("sync", ROWS[0:1, 0:1024], b1_h[cchunk * 1024:(cchunk + 1) * 1024].rearrange("(o d) -> o d", o=1),
                "d_c", [bROWS], [bROWS])
            CP("vector", SC4[0:1, 0:1024], ROWS[0:1, 0:1024], [bROWS], [bSC4])
            qx(SC4[0:1, 0:1024], SC4[0:1, 0:1024], 1024, fl_wide, "43bf", [bSC4], [bSC4], n=1)
            CP("vector", SCF[0:1, 0:1024], SC4[0:1, 0:1024], [bSC4], [bSCF])
            for k in range(8):
                j = cchunk * 8 + k
                TRANS(ps[7][:, 0:1], SCF[0:1, k * 128:(k + 1) * 128], [bSCF], [bPS[7]])
                ACT(QB1[:, j:j + 1], ps[7][:, 0:1], AF.Copy, [bPS[7]], [bQB1])
        DMA("sync", ROWS[0:1, 0:D], b2_h.rearrange("(o d) -> o d", o=1), "d_c", [bROWS], [bROWS])
        CP("vector", SC4[0:1, 0:1024], ROWS[0:1, 0:1024], [bROWS], [bSC4])
        qx(ROWB[0:1, 1024:2048], SC4[0:1, 0:1024], 1024, fl_wide, "43bf", [bSC4], [bROWB], n=1)

        def stats_batch(tiles_fn, n, eps, cmu=0, ce2=16, ct1=32, ct2=48):
            for t in range(n):
                ap, bufs = tiles_fn(t)
                ACT(TRASH[:, :D], ap, AF.Identity, bufs, [bTRASH, bVST], accum=VST[:, cmu + t:cmu + t + 1])
                ACT(TRASH[:, :D], ap, AF.Square, bufs, [bTRASH, bVST], accum=VST[:, ce2 + t:ce2 + t + 1])
            TS1("vector", VST[:, cmu:cmu + n], VST[:, cmu:cmu + n], 1.0 / D, OP.mult, [bVST], [bVST])
            TS1("vector", VST[:, ce2:ce2 + n], VST[:, ce2:ce2 + n], 1.0 / D, OP.mult, [bVST], [bVST])
            TT("vector", VST[:, ct1:ct1 + n], VST[:, cmu:cmu + n], VST[:, cmu:cmu + n], OP.mult, [bVST], [bVST])
            TT("vector", VST[:, ce2:ce2 + n], VST[:, ce2:ce2 + n], VST[:, ct1:ct1 + n], OP.subtract, [bVST], [bVST])
            TS1("vector", VST[:, ce2:ce2 + n], VST[:, ce2:ce2 + n], eps, OP.add, [bVST], [bVST])
            ACT(VST[:, ct2:ct2 + n], VST[:, ce2:ce2 + n], AF.Sqrt, [bVST], [bVST])
            RCP(VST[:, ce2:ce2 + n], VST[:, ct2:ct2 + n], [bVST], [bVST])

        bxn_d, by_d, bxm_d, bxq_d, bx1_d = (B_(x) for x in ("xn_d", "y_d", "xm_d", "xq_d", "x1_d"))

        def ln_stats(src_ap, W, src_bufs, mu_sl, rs_sl, eps):
            ACT(TRASH[:, :W], src_ap, AF.Identity, src_bufs, [bTRASH, bVST], accum=VST[:, mu_sl:mu_sl + 1])
            ACT(TRASH[:, :W], src_ap, AF.Square, src_bufs, [bTRASH, bVST], accum=VST[:, rs_sl:rs_sl + 1])
            TS1("vector", VST[:, mu_sl:mu_sl + 1], VST[:, mu_sl:mu_sl + 1], 1.0 / W, OP.mult, [bVST], [bVST])
            TS1("vector", VST[:, rs_sl:rs_sl + 1], VST[:, rs_sl:rs_sl + 1], 1.0 / W, OP.mult, [bVST], [bVST])
            TT("vector", VST[:, 60:61], VST[:, mu_sl:mu_sl + 1], VST[:, mu_sl:mu_sl + 1], OP.mult, [bVST], [bVST])
            TT("vector", VST[:, rs_sl:rs_sl + 1], VST[:, rs_sl:rs_sl + 1], VST[:, 60:61], OP.subtract, [bVST], [bVST])
            TS1("vector", VST[:, rs_sl:rs_sl + 1], VST[:, rs_sl:rs_sl + 1], eps, OP.add, [bVST], [bVST])
            ACT(VST[:, 61:62], VST[:, rs_sl:rs_sl + 1], AF.Sqrt, [bVST], [bVST])
            RCP(VST[:, rs_sl:rs_sl + 1], VST[:, 61:62], [bVST], [bVST])

        for t in range(NT):
            DMA("gpsimd", R0[:, t * D:(t + 1) * D], x_h[t * 128:(t + 1) * 128, :], "d_x",
                [bR0[t]], [bR0[t]])
        stats_batch(lambda t: (R0[:, t * D:(t + 1) * D], [bR0[t]]), NT, EPS1)
        for t in range(NT):
            TS2("vector", R0[:, t * D:(t + 1) * D], R0[:, t * D:(t + 1) * D],
                VST[:, t:t + 1], VST[:, 16 + t:17 + t], OP.subtract, OP.mult, [bR0[t], bVST], [bR0[t]])
        stats_batch(lambda t: (R0[:, t * D:(t + 1) * D], [bR0[t]]), NT, EPS1)
        for t in range(NT):
            dst, bdst = (SC2, bSC2) if t % 2 == 0 else (SC3, bSC3)
            TS2("vector", dst[:, :], R0[:, t * D:(t + 1) * D], VST[:, t:t + 1], VST[:, 16 + t:17 + t],
                OP.subtract, OP.mult, [bR0[t], bVST], [bdst])
            DMA("sync", xn_d[t * 128:(t + 1) * 128, :], dst[:, :], "d_b", [bdst], [bxn_d])

        for cc in range(ND):
            DMA("sync", TFM[:, cc * S:(cc + 1) * S], xn_d[:, cc * 128:(cc + 1) * 128], "d_t",
                [bxn_d, bTFM], [bTFM], transpose=True)
            TS2("vector", TFM[:, cc * S:(cc + 1) * S], TFM[:, cc * S:(cc + 1) * S],
                LNV[:, LN1W + cc:LN1W + cc + 1], LNV[:, LN1B + cc:LN1B + cc + 1],
                OP.mult, OP.add, [bTFM, bLNV], [bTFM])

        bw1q_d, bw2q_d = B_("w1q_d"), B_("w2q_d")
        bWQ = [B_("wq0"), B_("wq1")]

        def emit_wquant_wide():
            for j in range(64):
                slab = PB[:, (j % 2) * 1024:(j % 2) * 1024 + 1024]
                bs = bWQ[j % 2]
                louts = [bs] + ([bQB, bKB, bKG, bVB, bGB] if j < 2 else [])
                if j < 32:
                    src = w1_h[j * 128:(j + 1) * 128, :]
                    dst = w1q_d[j * 128:(j + 1) * 128, :]
                    bqd = bw1q_d
                else:
                    j2 = j - 32
                    src = w2_h[(j2 // 4) * 128:(j2 // 4 + 1) * 128, (j2 % 4) * 1024:(j2 % 4) * 1024 + 1024]
                    dst = w2q_d[(j2 // 4) * 128:(j2 // 4 + 1) * 128, (j2 % 4) * 1024:(j2 % 4) * 1024 + 1024]
                    bqd = bw2q_d
                DMA("gpsimd", slab, src, "d_q", [bs], louts)
                qx(slab, slab, 1024, fl_wq, "43bf", [bs], [bs])
                DMA("gpsimd", dst, slab, "d_q", [bs, bPB], [bqd])

        def load_wslot(s, wh):
            DMA("gpsimd", PW[:, s * 8192:(s + 1) * 8192].rearrange("p (kd n) -> p kd n", kd=ND),
                wh.rearrange("(kd p) n -> p kd n", p=128), "d_w", [bSLOT[s]], [bSLOT[s]])

        def wslot_tile(s, k, m):
            return PW[:, s * 8192 + k * D + m * 128:s * 8192 + k * D + (m + 1) * 128]

        def wslot_mov(s, k, n):
            return PW[:, s * 8192 + k * D + n * 512:s * 8192 + k * D + n * 512 + 512]

        def xnT(k, lo, w):
            return TFM[:, k * S + lo:k * S + lo + w]

        QBo, KBo, KGo, VBo, GBo = 0, 4096, 8192, 12288, 16384
        bQB, bKB, bKG, bVB, bGB = B_("qb"), B_("kb"), B_("kg"), B_("vb"), B_("gb")

        for Bi in range(NBLK):
            lo = Bi * BLK
            load_wslot(0, wq_h)
            for m in range(ND):
                pb_ = bPS[m % 2]
                psn = ps[m % 2]
                for kk in range(ND):
                    MM(psn[:, :], wslot_tile(0, kk, m), xnT(kk, lo, BLK),
                       start=(kk == 0), stop=(kk == ND - 1),
                       ins=[bSLOT[0], bTFM], outs=[pb_], inc=(kk == ND - 1))
                ACT(PB[:, QBo + m * BLK:QBo + (m + 1) * BLK], psn[:, :], AF.Copy,
                    [pb_], [bQB], scale=0.125)
            load_wslot(1, wk_h)
            for m in range(ND):
                pb_ = bPS[2 + m % 2]
                psn = ps[2 + m % 2]
                for kk in range(ND):
                    MM(psn[:, :], wslot_tile(1, kk, m), xnT(kk, lo, BLK),
                       start=(kk == 0), stop=(kk == ND - 1),
                       ins=[bSLOT[1], bTFM], outs=[pb_], inc=(kk == ND - 1))
                ACT(PB[:, KBo + m * BLK:KBo + (m + 1) * BLK], psn[:, :], AF.Copy, [pb_], [bKB])
            for mt in range(4):
                for nn in range(2):
                    pb_ = bPS[(mt * 2 + nn) % 2]
                    psn = ps[(mt * 2 + nn) % 2]
                    for kk in range(ND):
                        MM(psn[:, :], xnT(kk, lo + mt * 128, 128), wslot_mov(1, kk, nn),
                           start=(kk == 0), stop=(kk == ND - 1),
                           ins=[bSLOT[1], bTFM], outs=[pb_], inc=(kk == ND - 1))
                    for hh in range(8):
                        h = nn * 8 + hh
                        TS1("vector",
                            PB[:, KGo + mt * 1024 + nn * 512 + hh * 64:KGo + mt * 1024 + nn * 512 + hh * 64 + 64],
                            psn[:, hh * 64:hh * 64 + 64], KGV[:, h:h + 1], OP.mult,
                            [pb_, bKGV], [bKG])
            load_wslot(0, wv_h)
            for mt in range(4):
                for nn in range(2):
                    pb_ = bPS[2 + (mt * 2 + nn) % 2]
                    psn = ps[2 + (mt * 2 + nn) % 2]
                    for kk in range(ND):
                        MM(psn[:, :], xnT(kk, lo + mt * 128, 128), wslot_mov(0, kk, nn),
                           start=(kk == 0), stop=(kk == ND - 1),
                           ins=[bSLOT[0], bTFM], outs=[pb_], inc=(kk == ND - 1))
                    ACT(PB[:, VBo + mt * 1024 + nn * 512:VBo + mt * 1024 + nn * 512 + 512],
                        psn[:, :], AF.Copy, [pb_], [bVB])
            load_wslot(1, wg_h)
            for mt in range(4):
                for nn in range(2):
                    pb_ = bPS[(mt * 2 + nn) % 2]
                    psn = ps[(mt * 2 + nn) % 2]
                    for kk in range(ND):
                        MM(psn[:, :], xnT(kk, lo + mt * 128, 128), wslot_mov(1, kk, nn),
                           start=(kk == 0), stop=(kk == ND - 1),
                           ins=[bSLOT[1], bTFM], outs=[pb_], inc=(kk == ND - 1))
                    ACT(PB[:, GBo + mt * 1024 + nn * 512:GBo + mt * 1024 + nn * 512 + 512],
                        psn[:, :], AF.Silu, [pb_], [bGB])

            for cc4 in range(4):
                g = Bi * 4 + cc4
                clo = cc4 * 128
                for p in range(NPAIR):
                    h1, h2 = 2 * p, 2 * p + 1
                    MM(ps[4][:, 0:128], PB[0:64, KBo + p * BLK + clo:KBo + p * BLK + clo + 128],
                       PB[0:64, QBo + p * BLK + clo:QBo + p * BLK + clo + 128],
                       start=True, stop=True, ins=[bKB, bQB], outs=[bPS[4]], inc=True)
                    MM(ps[7][:, 0:128], PB[64:128, KBo + p * BLK + clo:KBo + p * BLK + clo + 128],
                       PB[64:128, QBo + p * BLK + clo:QBo + p * BLK + clo + 128],
                       start=True, stop=True, ins=[bKB, bQB], outs=[bPS[7]], inc=True)
                    TT("vector", SC1[:, 0:128], ps[4][:, 0:128], DKT[:, h1 * 128:(h1 + 1) * 128],
                       OP.mult, [bPS[4], bDKT], [bSC1])
                    TT("vector", SC1[:, 128:256], ps[7][:, 0:128], DKT[:, h2 * 128:(h2 + 1) * 128],
                       OP.mult, [bPS[7], bDKT], [bSC1])
                    TT("vector", SC2[:, 0:128], PB[:, QBo + p * BLK + clo:QBo + p * BLK + clo + 128],
                       GMT[:, p * 128:(p + 1) * 128], OP.mult, [bQB, bGMT], [bSC2])
                    MM(ps[5][:, 0:64], SC1[:, 0:128],
                       PB[:, VBo + cc4 * 1024 + h1 * 64:VBo + cc4 * 1024 + h1 * 64 + 64],
                       start=True, stop=False, ins=[bSC1, bVB], outs=[bPS[5]], inc=False)
                    MM(ps[5][:, 64:128], SC1[:, 128:256],
                       PB[:, VBo + cc4 * 1024 + h2 * 64:VBo + cc4 * 1024 + h2 * 64 + 64],
                       start=False, stop=(g == 0), ins=[bSC1, bVB], outs=[bPS[5]], inc=(g == 0))
                    if g > 0:
                        MM(ps[5][:, 0:128], SC2[:, 0:128], PW[:, 16384 + p * 128:16384 + (p + 1) * 128],
                           start=False, stop=True, ins=[bSC2, bSBLK16], outs=[bPS[5]], inc=True)
                    MM(ps[6][:, 0:128], PB[:, KGo + cc4 * 1024 + p * 128:KGo + cc4 * 1024 + (p + 1) * 128],
                       PB[:, VBo + cc4 * 1024 + p * 128:VBo + cc4 * 1024 + (p + 1) * 128],
                       start=True, stop=True, ins=[bKG, bVB], outs=[bPS[6]], inc=True)
                    ACT(SC3[:, p * 128:(p + 1) * 128], ps[5][:, 0:128], AF.Copy, [bPS[5]], [bSC3])
                    STT(SBLK[0:64, p * 128:p * 128 + 64], SBLK[0:64, p * 128:p * 128 + 64],
                        GCV[0:64, p:p + 1], ps[6][0:64, 0:64], OP.mult, OP.add,
                        [bSBLK, bPS[6], bGCV], [bSBLK])
                    STT(SBLK[64:128, p * 128 + 64:p * 128 + 128], SBLK[64:128, p * 128 + 64:p * 128 + 128],
                        GCV[64:128, p:p + 1], ps[6][64:128, 64:128], OP.mult, OP.add,
                        [bSBLK, bPS[6], bGCV], [bSBLK])
                    CP("vector", PW[:, 16384 + p * 128:16384 + (p + 1) * 128],
                       SBLK[:, p * 128:(p + 1) * 128], [bSBLK], [bSBLK16])
                ACT(SC0[:, :], SC3[:, :], AF.Square, [bSC3], [bSC0])
                RED(VST[:, 0:16], SC3[:, :].rearrange("p (h d) -> p h d", h=16), OP.add, [bSC3], [bVST])
                RED(VST[:, 16:32], SC0[:, :].rearrange("p (h d) -> p h d", h=16), OP.add, [bSC0], [bVST])
                TS1("vector", VST[:, 0:16], VST[:, 0:16], 1.0 / 64, OP.mult, [bVST], [bVST])
                TS1("vector", VST[:, 16:32], VST[:, 16:32], 1.0 / 64, OP.mult, [bVST], [bVST])
                TT("vector", VST[:, 32:48], VST[:, 0:16], VST[:, 0:16], OP.mult, [bVST], [bVST])
                TT("vector", VST[:, 16:32], VST[:, 16:32], VST[:, 32:48], OP.subtract, [bVST], [bVST])
                TS1("vector", VST[:, 16:32], VST[:, 16:32], EPS2, OP.add, [bVST], [bVST])
                ACT(VST[:, 32:48], VST[:, 16:32], AF.Sqrt, [bVST], [bVST])
                RCP(VST[:, 16:32], VST[:, 32:48], [bVST], [bVST])
                for h in range(NH):
                    TS2("vector", SC3[:, h * 64:(h + 1) * 64], SC3[:, h * 64:(h + 1) * 64],
                        VST[:, h:h + 1], VST[:, 16 + h:17 + h], OP.subtract, OP.mult, [bSC3, bVST], [bSC3])
                TT("vector", SC3[:, :], SC3[:, :], PB[:, GBo + cc4 * 1024:GBo + (cc4 + 1) * 1024],
                   OP.mult, [bSC3, bGB], [bSC3])
                DMA("sync", y_d[g * 128:(g + 1) * 128, :], SC3[:, :], "d_b", [bSC3], [by_d])

        emit_wquant_wide()

        for cc in range(ND):
            DMA("sync", TFM[:, cc * S:(cc + 1) * S], y_d[:, cc * 128:(cc + 1) * 128], "d_t",
                [by_d, bTFM], [bTFM], transpose=True)
        load_wslot(0, wo_h)
        for mt in range(NT):
            for nn in range(2):
                pb_ = bPS[(mt * 2 + nn) % 4]
                psn = ps[(mt * 2 + nn) % 4]
                for kk in range(ND):
                    MM(psn[:, :], xnT(kk, mt * 128, 128), wslot_mov(0, kk, nn),
                       start=(kk == 0), stop=(kk == ND - 1),
                       ins=[bSLOT[0], bTFM], outs=[pb_], inc=(kk == ND - 1))
                TT("vector", R0[:, mt * D + nn * 512:mt * D + nn * 512 + 512],
                   psn[:, :], R0[:, mt * D + nn * 512:mt * D + nn * 512 + 512],
                   OP.add, [pb_, bR0[mt]], [bR0[mt]])

        stats_batch(lambda t: (R0[:, t * D:(t + 1) * D], [bR0[t]]), NT, EPS1)
        for t in range(NT):
            dst, bdst = (SC2, bSC2) if t % 2 == 0 else (SC3, bSC3)
            TS2("vector", dst[:, :], R0[:, t * D:(t + 1) * D], VST[:, t:t + 1], VST[:, 16 + t:17 + t],
                OP.subtract, OP.mult, [bR0[t], bVST], [bdst])
            DMA("sync", xm_d[t * 128:(t + 1) * 128, :], dst[:, :], "d_b", [bdst], [bxm_d])
        for cc in range(ND):
            DMA("sync", TFM[:, cc * S:(cc + 1) * S], xm_d[:, cc * 128:(cc + 1) * 128], "d_t",
                [bxm_d, bTFM], [bTFM], transpose=True)
            TS2("vector", TFM[:, cc * S:(cc + 1) * S], TFM[:, cc * S:(cc + 1) * S],
                LNV[:, LN2W + cc:LN2W + cc + 1], LNV[:, LN2B + cc:LN2B + cc + 1],
                OP.mult, OP.add, [bTFM, bLNV], [bTFM])

        for k in range(NKH):
            w = HID_T[k]
            if k == 0:
                DMA("gpsimd", PW[0:w, k * D:(k + 1) * D], wout_h[k * 128:k * 128 + w, :],
                    "d_w2", [bPW], [bPW, bSLOT[0], bSLOT[1], bSBLK16])
            else:
                DMA("gpsimd", PW[0:w, k * D:(k + 1) * D], wout_h[k * 128:k * 128 + w, :],
                    "d_w2", [bPW], [bPW])
        for tb in range(4):
            tlo = tb * 512
            for k in range(NKH):
                w = HID_T[k]
                for half, hofs in ((0, 0), (1, HID)):
                    DMA("gpsimd", WST[:, half * 1024:half * 1024 + 8 * w].rearrange("p (kd n) -> p kd n", kd=ND),
                        win_h[:, hofs + k * 128:hofs + k * 128 + w].rearrange("(kd p) n -> p kd n", p=128),
                        "d_w", [bWST], [bWST])
                for half, pbank in ((0, 0), (1, 1)):
                    for kk in range(ND):
                        MM(ps[pbank][0:w, :], WST[:, half * 1024 + kk * w:half * 1024 + (kk + 1) * w],
                           xnT(kk, tlo, 512), start=(kk == 0), stop=(kk == ND - 1),
                           ins=[bWST, bTFM], outs=[bPS[pbank]], inc=(kk == ND - 1))
                ACT(SC1[0:w, 0:512], ps[0][0:w, :], AF.Identity, [bPS[0], bBIV], [bSC1], bias=BIV[0:w, k:k + 1])
                ACT(SC2[0:w, 0:512], ps[1][0:w, :], AF.Silu, [bPS[1], bBIV], [bSC2], bias=BIV[0:w, NKH + k:NKH + k + 1])
                TT("vector", PB[0:w, k * 512:k * 512 + 512], SC1[0:w, 0:512], SC2[0:w, 0:512],
                   OP.mult, [bSC1, bSC2], [bPB])
            for mt in range(4):
                t_glob = tb * 4 + mt
                for nn in range(2):
                    pb_ = bPS[2 + (mt * 2 + nn) % 2]
                    psn = ps[2 + (mt * 2 + nn) % 2]
                    for k in range(NKH):
                        w = HID_T[k]
                        MM(psn[:, :], PB[0:w, k * 512 + mt * 128:k * 512 + mt * 128 + 128],
                           PW[0:w, k * D + nn * 512:k * D + nn * 512 + 512],
                           start=(k == 0), stop=False, ins=[bPB, bPW], outs=[pb_], inc=False)
                    MM(psn[:, :], ONES[0:1, 0:128], ROWB[0:1, nn * 512:nn * 512 + 512],
                       start=False, stop=True, ins=[bONES, bROWB], outs=[pb_], inc=True)
                    TT("vector", R0[:, t_glob * D + nn * 512:t_glob * D + nn * 512 + 512],
                       psn[:, :], R0[:, t_glob * D + nn * 512:t_glob * D + nn * 512 + 512],
                       OP.add, [pb_, bR0[t_glob]], [bR0[t_glob]])

        bw1qT_d, bw2qT_d = B_("w1qT_d"), B_("w2qT_d")
        for cc in range(ND):
            for hh in range(2):
                DMA("sync", WST[:, 0:2048], w1q_d[hh * 2048:(hh + 1) * 2048, cc * 128:(cc + 1) * 128],
                    "d_t", [bw1q_d, bWST], [bWST], transpose=True)
                DMA("sync", w1qT_d[cc * 128:(cc + 1) * 128, hh * 2048:(hh + 1) * 2048], WST[:, 0:2048],
                    "d_t", [bWST], [bw1qT_d])
        for jj in range(NJ):
            DMA("sync", WST[:, 0:1024], w2q_d[:, jj * 128:(jj + 1) * 128], "d_t",
                [bw2q_d, bWST], [bWST], transpose=True)
            DMA("sync", w2qT_d[jj * 128:(jj + 1) * 128, :], WST[:, 0:1024], "d_t", [bWST], [bw2qT_d])

        stats_batch(lambda t: (R0[:, t * D:(t + 1) * D], [bR0[t]]), NT, EPS1)
        for t in range(NT):
            dst, bdst = (SC2, bSC2) if t % 2 == 0 else (SC3, bSC3)
            xsrc, bxsrc = (SC4, bSC4) if t % 2 == 0 else (SC1, bSC1)
            DMA("gpsimd", xsrc[:, :], x_h[t * 128:(t + 1) * 128, :], "d_x", [bxsrc], [bxsrc])
            TS2("vector", dst[:, :], R0[:, t * D:(t + 1) * D], VST[:, t:t + 1], VST[:, 16 + t:17 + t],
                OP.subtract, OP.mult, [bR0[t], bVST], [bdst])
            TT("vector", dst[:, :], dst[:, :], xsrc[:, :], OP.add, [bdst, bxsrc], [bdst])
            DMA("sync", x1_d[t * 128:(t + 1) * 128, :], dst[:, :], "d_b", [bdst], [bx1_d])
            ACT(TRASH[:, :D], dst[:, :], AF.Identity, [bdst], [bTRASH, bVST], accum=VST[:, 32 + t:33 + t])
            ACT(TRASH[:, :D], dst[:, :], AF.Square, [bdst], [bTRASH, bVST], accum=VST[:, 48 + t:49 + t])
        TS1("vector", VST[:, 32:48], VST[:, 32:48], 1.0 / D, OP.mult, [bVST], [bVST])
        TS1("vector", VST[:, 48:64], VST[:, 48:64], 1.0 / D, OP.mult, [bVST], [bVST])
        TT("vector", VST[:, 0:16], VST[:, 32:48], VST[:, 32:48], OP.mult, [bVST], [bVST])
        TT("vector", VST[:, 48:64], VST[:, 48:64], VST[:, 0:16], OP.subtract, [bVST], [bVST])
        TS1("vector", VST[:, 48:64], VST[:, 48:64], EPS1, OP.add, [bVST], [bVST])
        ACT(VST[:, 0:16], VST[:, 48:64], AF.Sqrt, [bVST], [bVST])
        RCP(VST[:, 48:64], VST[:, 0:16], [bVST], [bVST])
        for t in range(NT):
            xsrc, bxsrc = (SC4, bSC4) if t % 2 == 0 else (SC1, bSC1)
            DMA("gpsimd", xsrc[:, :], x1_d[t * 128:(t + 1) * 128, :], "d_x", [bx1_d, bxsrc], [bxsrc])
            dst, bdst = (SC2, bSC2) if t % 2 == 0 else (SC3, bSC3)
            TS2("vector", dst[:, :], xsrc[:, :], VST[:, 32 + t:33 + t], VST[:, 48 + t:49 + t],
                OP.subtract, OP.mult, [bxsrc, bVST], [bdst])
            if "xf" in dbg:
                DMA("gpsimd", dbg_h["xf"][t * 128:(t + 1) * 128, :], dst[:, :], "d_g",
                    [bdst], [B_("xfdbg")])
            qx(dst[:, :], dst[:, :], 1024, fl_wide, "43bf", [bdst], [bdst])
            DMA("sync", xq_d[t * 128:(t + 1) * 128, :], dst[:, :], "d_b", [bdst], [bxq_d])

        bout_dma = B_("out")
        bhq = [B_("hq0"), B_("hq1")]
        bxqT = [B_("xqT0"), B_("xqT1")]

        def h_phase(tb):
            par = tb % 2
            hqo = par * 8192
            tlo = tb * FTB
            for cc in range(ND):
                DMA("sync", TFM[:, cc * S + par * FTB:cc * S + par * FTB + FTB],
                    xq_d[tlo:tlo + FTB, cc * 128:(cc + 1) * 128],
                    "d_t", [bxq_d, bxqT[par]], [bxqT[par]], transpose=True)
            for js in range(NJ):
                if js % 2 == 0:
                    wbuf, bwbuf, wch = WST[:, 0:1024], B_("wstA"), "d_fa"
                else:
                    wbuf, bwbuf, wch = PB[:, 16384:17408], B_("pbw1"), "d_fb"
                extra_outs = [bwbuf] if not (tb == 0 and js < 2) else [bwbuf, bPB if js == 1 else bWST]
                DMA("sync", wbuf.rearrange("p (kd n) -> p kd n", kd=ND),
                    w1qT_d[:, js * 128:(js + 1) * 128].rearrange("(kd p) n -> p kd n", p=128),
                    wch, [bw1qT_d], extra_outs)
                pb_ = bPS[4 + js % 2]
                psn = ps[4 + js % 2]
                for kk in range(ND):
                    MM(psn[:, 0:FTB], wbuf[:, kk * 128:(kk + 1) * 128],
                       TFM[:, kk * S + par * FTB:kk * S + par * FTB + FTB],
                       start=(kk == 0), stop=(kk == ND - 1), ins=[bwbuf, bxqT[par]],
                       outs=[pb_], inc=(kk == ND - 1))
                ACT(SCH1[:, 0:FTB], psn[:, 0:FTB], AF.Relu, [pb_, bQB1], [bSH1], bias=QB1[:, js:js + 1])
                qx(SCH1[:, 0:FTB], SCH1[:, 0:FTB], FTB, fl_ffn, "510f16", [bSH1], [bSH1])
                qx(PB[:, hqo + js * FTB:hqo + (js + 1) * FTB], SCH1[:, 0:FTB],
                   FTB, fl_ffn, "43f16", [bSH1], [bhq[par]])

        def y_phase(tb):
            par = tb % 2
            hqo = par * 8192
            for js in range(NJ):
                if js % 2 == 0:
                    ybuf, bybuf, ych = WST[:, 1024:2048], B_("wstB"), "d_fc"
                else:
                    ybuf, bybuf, ych = PB[:, 17408:18432], B_("pbw2"), "d_fd"
                DMA("sync", ybuf, w2qT_d[js * 128:(js + 1) * 128, :], ych, [bw2qT_d], [bybuf])
                for sub in range(2):
                    for ih in range(2):
                        bidx = sub * 2 + ih
                        MM(ps[bidx][:, :], PB[:, hqo + js * FTB + sub * 128:hqo + js * FTB + sub * 128 + 128],
                           ybuf[:, ih * 512:ih * 512 + 512],
                           start=(js == 0), stop=False, ins=[bhq[par], bybuf], outs=[bPS[bidx]],
                           inc=(sub == 1 and ih == 1))
            mm_last = None
            for sub in range(2):
                for ih in range(2):
                    bidx = sub * 2 + ih
                    mm_last = MM(ps[bidx][:, :], ONES[0:1, 0:128],
                                 ROWB[0:1, 1024 + ih * 512:1024 + ih * 512 + 512],
                                 start=False, stop=True, ins=[bONES, bROWB], outs=[bPS[bidx]], inc=True)
            mm_bar = em.do("tensor", lambda e, c: e.nop(), extra=[mm_last], inc=True)
            for b_ in range(4):
                bPS[b_].w = mm_bar

        def tail_phase(tb):
            for sub in range(2):
                t_glob = tb * 2 + sub
                for ih in range(2):
                    bidx = sub * 2 + ih
                    o = ih * 512
                    ACT(SYE[:, o:o + 512], ps[bidx][:, :], AF.Copy, [bPS[bidx]], [bSYE])
                    qx(SYE[:, o:o + 512], SYE[:, o:o + 512], 512, fl_tail, "510f16", [bSYE], [bSYE])
                DMA("sync", SCH1[:, :], mask_h[t_glob * 128:(t_glob + 1) * 128, 0:512], "d_m", [bSH1], [bSH1])
                DMA("sync", SCH2[:, :], mask_h[t_glob * 128:(t_glob + 1) * 128, 512:1024], "d_m", [bSH2], [bSH2])
                TT("vector", SYE[:, 0:512], SYE[:, 0:512], SCH1[:, :], OP.mult, [bSYE, bSH1], [bSYE])
                TT("vector", SYE[:, 512:1024], SYE[:, 512:1024], SCH2[:, :], OP.mult, [bSYE, bSH2], [bSYE])
                if "ym" in dbg:
                    DMA("gpsimd", dbg_h["ym"][t_glob * 128:(t_glob + 1) * 128, :], SYE[:, :],
                        "d_g", [bSYE], [B_("dbgout")])
                ACT(TRASH[:, :D], SYE[:, :], AF.Identity, [bSYE], [bTRASH, bVST], accum=VST[:, 0:1])
                ACT(TRASH[:, :D], SYE[:, :], AF.Square, [bSYE], [bTRASH, bVST], accum=VST[:, 1:2])
                TS1("vector", VST[:, 0:1], VST[:, 0:1], 1.0 / D, OP.mult, [bVST], [bVST])
                TS1("vector", VST[:, 1:2], VST[:, 1:2], 1.0 / D, OP.mult, [bVST], [bVST])
                TT("vector", VST[:, 2:3], VST[:, 0:1], VST[:, 0:1], OP.mult, [bVST], [bVST])
                TT("vector", VST[:, 1:2], VST[:, 1:2], VST[:, 2:3], OP.subtract, [bVST], [bVST])
                TS1("vector", VST[:, 1:2], VST[:, 1:2], EPS1, OP.add, [bVST], [bVST])
                ACT(VST[:, 2:3], VST[:, 1:2], AF.Sqrt, [bVST], [bVST])
                RCP(VST[:, 1:2], VST[:, 2:3], [bVST], [bVST])
                DMA("gpsimd", SC1[:, :], x1_d[t_glob * 128:(t_glob + 1) * 128, :], "d_x",
                    [bx1_d, bSC1], [bSC1])
                TS2("vector", SC2[:, :], SYE[:, :], VST[:, 0:1], VST[:, 1:2],
                    OP.subtract, OP.mult, [bSYE, bVST], [bSC2])
                TT("vector", SC0[:, :], SC2[:, :], SC1[:, :], OP.add, [bSC2, bSC1], [bSC0])
                DMA("sync", out_h[t_glob * 128:(t_glob + 1) * 128, :], SC0[:, :], "d_o", [bSC0], [bout_dma])

        h_phase(0)
        for tb in range(NFTB):
            y_phase(tb)
            if tb + 1 < NFTB:
                h_phase(tb + 1)
            tail_phase(tb)

        for n in dbg:
            src = {"xn": xn_d, "y": y_d, "xm": xm_d, "xq": xq_d, "x1": x1_d}.get(n)
            if src is None:
                continue
            DMA("gpsimd", dbg_h[n], src, "d_g", [B_(n + "_d")], [B_("dbgout")])

        em.do("tensor", lambda e, c: e.nop(), extra=[(ESEM["tensor"], em.cnt["tensor"])], inc=True)
        em.do("vector", lambda e, c: e.nop(), extra=[(ESEM["vector"], em.cnt["vector"])], inc=True)
        em.do("scalar", lambda e, c: e.nop(), extra=[(ESEM["scalar"], em.cnt["scalar"])], inc=True)
        em.do("gpsimd", lambda e, c: e.nop(), extra=[(ESEM["gpsimd"], em.cnt["gpsimd"])], inc=True)
        fin_extra = [("d_o", em.dcnt["d_o"])]
        if em.dcnt["d_g"]:
            fin_extra.append(("d_g", em.dcnt["d_g"]))
        em.do("sync", lambda e, c: e.nop(), extra=fin_extra, inc=False)

        ctx = {"waited": {}, "sem": sems, "eng": {}}

        def run_queue(name, handle):
            ctx["eng"][name] = handle
            for f in em.q[name]:
                f(ctx)

        @block.sync
        def _(e):
            run_queue("sync", e)

        @block.vector
        def _(e):
            run_queue("vector", e)

        @block.scalar
        def _(e):
            run_queue("scalar", e)

        @block.gpsimd
        def _(e):
            run_queue("gpsimd", e)

        @block.tensor
        def _(e):
            run_queue("tensor", e)

    return nc


def _host_constants():
    i = np.arange(128, dtype=np.float32)
    trel = i[None, :] - i[:, None]
    trel = np.where(trel >= 0, trel, np.float32(1e9)).astype(np.float32)
    tip1 = np.broadcast_to(i[None, :] + 1.0, (128, 128)).astype(np.float32)
    trev = (127.0 - i)[:, None].astype(np.float32)
    ident = np.eye(128, dtype=np.float32)
    return trel, tip1, trev, ident


def _dropout_mask():
    import os
    import jax
    ctx = (jax.experimental.enable_x64() if os.environ.get("BASS_SIM_FP64") == "1"
           else jax.experimental.disable_x64())
    with jax.default_device(jax.devices("cpu")[0]), ctx:
        keep = jax.random.bernoulli(jax.random.key(42, impl="threefry2x32"), 1.0 - 0.2, (8, S, D))
        keep = np.asarray(keep)
    return np.where(keep, np.float16(1.25), np.float16(0.0)).astype(np.float16)


def kernel(L0_attn_x, L0_attn_ln1_w, L0_attn_ln1_b, L0_attn_wq, L0_attn_wk, L0_attn_wv,
           L0_attn_wg, L0_attn_wo, L0_attn_ln2_w, L0_attn_ln2_b, L0_attn_w_in, L0_attn_b_in,
           L0_attn_w_out, L0_attn_b_out, L0_ffn_w1, L0_ffn_b1, L0_ffn_w2, L0_ffn_b2):
    x = np.ascontiguousarray(np.asarray(L0_attn_x, dtype=np.float32))
    trel, tip1, trev, ident = _host_constants()
    mask = _dropout_mask()
    common = {
        "wq": np.ascontiguousarray(L0_attn_wq, np.float32),
        "wk": np.ascontiguousarray(L0_attn_wk, np.float32),
        "wv": np.ascontiguousarray(L0_attn_wv, np.float32),
        "wg": np.ascontiguousarray(L0_attn_wg, np.float32),
        "wo": np.ascontiguousarray(L0_attn_wo, np.float32),
        "ln1w": np.ascontiguousarray(L0_attn_ln1_w, np.float32),
        "ln1b": np.ascontiguousarray(L0_attn_ln1_b, np.float32),
        "ln2w": np.ascontiguousarray(L0_attn_ln2_w, np.float32),
        "ln2b": np.ascontiguousarray(L0_attn_ln2_b, np.float32),
        "win": np.ascontiguousarray(L0_attn_w_in, np.float32),
        "bin": np.ascontiguousarray(L0_attn_b_in, np.float32),
        "wout": np.ascontiguousarray(L0_attn_w_out, np.float32),
        "bout": np.ascontiguousarray(L0_attn_b_out, np.float32),
        "w1": np.ascontiguousarray(L0_ffn_w1, np.float32),
        "b1": np.ascontiguousarray(L0_ffn_b1, np.float32),
        "w2": np.ascontiguousarray(L0_ffn_w2, np.float32),
        "b2": np.ascontiguousarray(L0_ffn_b2, np.float32),
        "trel": trel, "tip1": tip1, "trev": trev, "ident": ident,
    }
    nc = build_nc()
    in_maps = []
    for core in range(8):
        m = dict(common)
        m["x"] = np.ascontiguousarray(x[core])
        m["mask"] = np.ascontiguousarray(mask[core])
        in_maps.append(m)
    res = run_bass_kernel_spmd(nc, in_maps, core_ids=list(range(8)))
    return np.stack([r["out"] for r in res.results], axis=0)


# version 13
# speedup vs baseline: 1.2461x; 1.2461x over previous
import contextlib
import math
import sys

import numpy as np
import ml_dtypes

import concourse.bass as bass
import concourse.mybir as mybir
from concourse.bass_utils import run_bass_kernel_spmd

F32 = mybir.dt.float32
BF16 = mybir.dt.bfloat16
F16 = mybir.dt.float16
I16 = mybir.dt.int16
I8 = mybir.dt.int8
F8E4 = mybir.dt.float8e4
M64C = 6755399441055744.0
OP = mybir.AluOpType
AF = mybir.ActivationFunctionType
BF = ml_dtypes.bfloat16
AX = mybir.AxisListType.X

S, D = 2048, 1024
NT, ND = S // 128, D // 128
NH, NPAIR = 16, 8
BLK, NBLK = 512, 4
HID = 2730
HID_T = [128] * 21 + [42]
NKH = len(HID_T)
H4 = 4096
NJ = H4 // 128
FTB, NFTB = 256, 8
EPS1, EPS2 = 1e-5, 1e-6

ENGS = ("sync", "vector", "scalar", "gpsimd", "tensor")
ESEM = {e: "E_" + e for e in ENGS}
DMACH = ("d_x", "d_w", "d_w2", "d_t", "d_b", "d_m", "d_o", "d_c", "d_q", "d_g", "d_fa", "d_fb", "d_fc", "d_fd")


class Buf:
    __slots__ = ("w", "r")

    def __init__(self):
        self.w = None
        self.r = []


class Em:
    def __init__(self):
        self.q = {e: [] for e in ENGS}
        self.cnt = {e: 0 for e in ENGS}
        self.dcnt = {ch: 0 for ch in DMACH}

    def _wait(self, eng, sem_name, val):
        def f(ctx, eng=eng, sem_name=sem_name, val=val):
            key = (eng, sem_name)
            if ctx["waited"].get(key, -1) >= val:
                return
            ctx["waited"][key] = val
            ctx["eng"][eng].wait_ge(ctx["sem"][sem_name], val)

        self.q[eng].append(f)

    def do(self, eng, build, ins=(), outs=(), ch=None, extra=(), inc=True):
        fr = sys._getframe(1)
        site = f"{fr.f_code.co_name}:{fr.f_lineno}"
        fr2 = fr.f_back
        if fr2 is not None:
            site += f" < {fr2.f_code.co_name}:{fr2.f_lineno}"
        _build0 = build

        def build(e, c, _b=_build0, _site=site):
            try:
                return _b(e, c)
            except Exception as ex:
                raise RuntimeError(f"emit failed at {_site}: {ex}") from ex

        waits = []
        for b in ins:
            if b.w is not None:
                waits.append(b.w)
        for b in outs:
            if b.w is not None:
                waits.append(b.w)
            waits.extend(b.r)
        waits.extend(t for t in extra if t is not None)
        seen = {}
        for s_, v_ in waits:
            if s_ in self.dcnt:
                v_ = self.dcnt[s_]
            if seen.get(s_, -1) < v_:
                seen[s_] = v_
        for s_, v_ in sorted(seen.items()):
            if s_ == ESEM[eng] and v_ > self.cnt[eng]:
                continue
            self._wait(eng, s_, v_)
        if ch is not None:
            if self.dcnt[ch] > 0:
                self._wait(eng, ch, self.dcnt[ch])
            self.dcnt[ch] += 16
            val = self.dcnt[ch]

            def f(ctx, build=build, eng=eng, ch=ch):
                build(ctx["eng"][eng], ctx).then_inc(ctx["sem"][ch], 16)

            self.q[eng].append(f)
            tok = (ch, val)
        elif inc:
            self.cnt[eng] += 1
            val = self.cnt[eng]

            def f(ctx, build=build, eng=eng, sem_name=ESEM[eng]):
                build(ctx["eng"][eng], ctx).then_inc(ctx["sem"][sem_name], 1)

            self.q[eng].append(f)
            tok = (ESEM[eng], val)
        else:
            def f(ctx, build=build, eng=eng):
                build(ctx["eng"][eng], ctx)

            self.q[eng].append(f)
            tok = None
        if tok is None and ch is None:
            tok_eff = (ESEM[eng], self.cnt[eng] + 1)
        else:
            tok_eff = tok
        if tok_eff is not None:
            for b in ins:
                b.r.append(tok_eff)
            for b in outs:
                b.w = tok_eff
                b.r = []
        return tok


def _pc(h):
    g = 1.0 - 2.0 ** (-5.0 - h)
    lng = math.log(g)
    hi = np.float32(lng)
    lo = np.float32(lng - float(hi))
    return float(hi), float(lo)


def build_nc(dbg=()):
    nc = bass.Bass("TRN2", target_bir_lowering=False, debug=False)
    em = Em()

    I = lambda n, sh, dt_: nc.dram_tensor(n, sh, dt_, kind="ExternalInput").ap()
    x_h = I("x", [S, D], F32)
    wq_h = I("wq", [D, D], F32)
    wk_h = I("wk", [D, D], F32)
    wv_h = I("wv", [D, D], F32)
    wg_h = I("wg", [D, D], F32)
    wo_h = I("wo", [D, D], F32)
    ln_h = {n: I(n, [D], F32) for n in ("ln1w", "ln1b", "ln2w", "ln2b")}
    win_h = I("win", [D, 2 * HID], F32)
    bin_h = I("bin", [2 * HID], F32)
    wout_h = I("wout", [HID, D], F32)
    bout_h = I("bout", [D], F32)
    w1_h = I("w1", [H4, D], F32)
    b1_h = I("b1", [H4], F32)
    w2_h = I("w2", [D, H4], F32)
    b2_h = I("b2", [D], F32)
    mask_h = I("mask", [S, D], F16)
    trel_h = I("trel", [128, 128], F32)
    tip1_h = I("tip1", [128, 128], F32)
    trev_h = I("trev", [128, 1], F32)
    ident_h = I("ident", [128, 128], F32)
    out_h = nc.dram_tensor("out", [S, D], F32, kind="ExternalOutput").ap()
    _dbg_shapes = {"hq": [H4, S], "ym": [S, D]}
    dbg_h = {n: nc.dram_tensor("dbg_" + n, _dbg_shapes.get(n, [S, D]), F32, kind="ExternalOutput").ap() for n in dbg}

    T = lambda n, sh, dt_: nc.dram_tensor(n, sh, dt_, kind="Internal").ap()
    xn_d = T("xn_d", [S, D], BF16)
    y_d = T("y_d", [S, D], BF16)
    xm_d = T("xm_d", [S, D], BF16)
    xq_d = T("xq_d", [S, D], BF16)
    x1_d = T("x1_d", [S, D], BF16)
    w1q_d = T("w1q_d", [H4, D], BF16)
    w1qT_d = T("w1qT_d", [D, H4], BF16)
    w2q_d = T("w2q_d", [D, H4], BF16)
    w2qT_d = T("w2qT_d", [H4, D], BF16)

    es = contextlib.ExitStack()
    with es:
        SBt = lambda n, sh, dt_: es.enter_context(nc.sbuf_tensor(n, sh, dt_))
        TFM = SBt("TFM", [128, ND * S], BF16)
        R0 = SBt("R0", [128, NT * D], BF16)
        PW = SBt("PW", [128, 22528], BF16)
        PB = SBt("PB", [128, 20480], BF16)
        WST = SBt("WST", [128, 2048], BF16)
        SC0 = SBt("SC0", [128, 1024], F32)
        SCF = SBt("SCF", [128, 1024], F32)
        SC1 = SBt("SC1", [128, 1024], BF16)
        SC2 = SBt("SC2", [128, 1024], BF16)
        SC3 = SBt("SC3", [128, 1024], BF16)
        SC4 = SBt("SC4", [128, 1024], BF16)
        SCI = SBt("SCI", [128, 1024], I16)
        SCI2 = SBt("SCI2", [128, 1024], I16)
        SCH1 = SBt("SCH1", [128, 1024], F16)
        SCH2 = SBt("SCH2", [128, 512], F16)
        SCH3 = SBt("SCH3", [128, 512], F16)
        SCH4 = SBt("SCH4", [128, 512], F16)
        SYE = SBt("SYE", [128, 1024], F16)
        QL16 = SBt("QL16", [128, 512], I16)
        GQ1 = SBt("GQ1", [128, 512], BF16)
        F8Q = SBt("F8Q", [128, 1024], F8E4)
        L8Q = SBt("L8Q", [128, 1024], I8)
        GQI = SBt("GQI", [128, 1024], I16)
        VST = SBt("VST", [128, 66], F32)
        TRASH = SBt("TRASH", [128, 1024], BF16)
        DKT = SBt("DKT", [128, NH * 128], BF16)
        GMT = SBt("GMT", [128, NPAIR * 128], BF16)
        KGV = SBt("KGV", [128, NH], F32)
        GCV = SBt("GCV", [128, NPAIR], F32)
        SBLK = SBt("SBLK", [128, NPAIR * 128], F32)
        CST = SBt("CST", [128, 128], F32)
        IDT = SBt("IDT", [128, 128], F32)
        LNV = SBt("LNV", [128, 4 * ND], F32)
        BIV = SBt("BIV", [128, 2 * NKH], F32)
        QB1 = SBt("QB1", [128, NJ], F32)
        ROWS = SBt("ROWS", [1, 1024], F32)
        ROWB = SBt("ROWB", [1, 2048], BF16)
        ONES = SBt("ONES", [1, 512], BF16)
        ps = [es.enter_context(nc.psum_tensor(f"ps{i}", [128, 512], F32)) for i in range(8)]
        sems = {n: es.enter_context(nc.semaphore(n)) for n in list(ESEM.values()) + list(DMACH)}
        block = es.enter_context(nc.Block())

        _bufs = {}

        def B_(key):
            if key not in _bufs:
                _bufs[key] = Buf()
            return _bufs[key]

        bPS = [B_(("ps", i)) for i in range(8)]
        bSC0, bSCF, bSC1, bSC2, bSC3, bSC4 = (B_(x) for x in ("SC0", "SCF", "SC1", "SC2", "SC3", "SC4"))
        bSCI, bSCI2 = B_("SCI"), B_("SCI2")
        bSH1, bSH2, bSH3, bSH4, bSYE = B_("SCH1"), B_("SCH2"), B_("SCH3"), B_("SCH4"), B_("SYE")
        bVST, bTRASH, bCST, bROWS, bROWB = B_("VST"), B_("TRASH"), B_("CST"), B_("ROWS"), B_("ROWB")
        bWST, bTFM, bPW, bPB = B_("WST"), B_("TFM"), B_("PW"), B_("PB")
        bSBLK, bIDT, bONES = B_("SBLK"), B_("IDT"), B_("ONES")
        bLNV, bBIV, bQB1 = B_("LNV"), B_("BIV"), B_("QB1")
        bDKT, bGMT, bKGV, bGCV = B_("DKT"), B_("GMT"), B_("KGV"), B_("GCV")
        bQX, bGQ1 = B_("qx"), B_("gq1")
        bR0 = [B_(("R0", t)) for t in range(NT)]
        bSLOT = [B_(("slot", s)) for s in range(2)]

        def TT(eng, out, a, b, op, ins, outs, extra=()):
            return em.do(eng, lambda e, c: e.tensor_tensor(out, a, b, op=op), ins, outs, extra=extra)

        def TS1(eng, out, a, s1, op, ins, outs, extra=()):
            return em.do(eng, lambda e, c: e.tensor_single_scalar(out, a, s1, op=op), ins, outs, extra=extra)

        def TS2(eng, out, a, s1, s2, op0, op1, ins, outs, extra=()):
            if eng == "gpsimd":
                t = em.do(eng, lambda e, c: e.tensor_single_scalar(out, a, s1, op=op0), ins, outs, extra=extra)
                return em.do(eng, lambda e, c: e.tensor_single_scalar(out, out, s2, op=op1), list(outs), list(outs))
            return em.do(eng, lambda e, c: e.tensor_scalar(out, a, s1, s2, op0=op0, op1=op1), ins, outs, extra=extra)

        def STT(out, a, s, b, op0, op1, ins, outs, extra=()):
            return em.do("vector", lambda e, c: e.scalar_tensor_tensor(out, a, s, b, op0=op0, op1=op1), ins, outs, extra=extra)

        def CP(eng, out, a, ins, outs, extra=()):
            return em.do(eng, lambda e, c: e.tensor_copy(out, a), ins, outs, extra=extra)

        def ACT(out, a, func, ins, outs, bias=0.0, scale=1.0, accum=None, extra=()):
            return em.do("scalar", lambda e, c: e.activation(out, a, func, bias=bias, scale=scale, accum_out=accum), ins, outs, extra=extra)

        def RED(out, a, op, ins, outs, extra=()):
            return em.do("vector", lambda e, c: e.tensor_reduce(out, a, axis=AX, op=op), ins, outs, extra=extra)

        def MS(eng, ap, v, ins, outs, extra=()):
            return em.do(eng, lambda e, c: e.memset(ap, v), ins, outs, extra=extra)

        def MM(po, lhsT, rhs, start, stop, ins, outs, extra=(), inc=True, tp=None):
            return em.do("tensor", lambda e, c: e.matmul(po, lhsT, rhs, start=start, stop=stop, tile_position=tp), ins, outs, extra=extra, inc=True)

        def TRANS(po, in_, ins, outs, extra=(), P=1):
            return em.do("tensor", lambda e, c: e.transpose(po, in_, IDT[0:P, 0:P]), list(ins) + [bIDT], outs, extra=extra, inc=True)

        def DMA(eng, out, in_, ch, ins, outs, extra=(), transpose=False):
            if transpose:
                return em.do(eng, lambda e, c: e.dma_start(out=out, in_=in_, transpose=True), ins, outs, ch=ch, extra=extra)
            return em.do(eng, lambda e, c: e.dma_start(out=out, in_=in_), ins, outs, ch=ch, extra=extra)

        def RCP(out, a, ins, outs, extra=()):
            return em.do("vector", lambda e, c: e.reciprocal(out, a), ins, outs, extra=extra)

        def qx(dst, src, W, fl, kind, ins, outs, n=128):
            (Fr, br), (Fg, bg), (Fc, bc), (Fl, bl), (Fn, bn), (Fx, bx), (Fb, bb) = \
                [(a[0:n, 0:W], b) for a, b in fl]
            ti, t2, pm = SCI[0:n, :W], SCI2[0:n, :W], GQI[0:n, :W]
            f8, l8 = F8Q[0:n, :W], L8Q[0:n, :W]
            V = "vector"
            if kind == "43bf":
                TS1(V, ti, src.bitcast(I16), 0x7F80, OP.bitwise_and, list(ins), [bQX])
                TS2(V, t2, ti, 121 << 7, 384, OP.max, OP.subtract, [bQX], [bQX])
                TS1(V, Fn, ti, 15616, OP.is_ge, [bQX], [bn])
                TS2(V, pm, t2, -1, 32512, OP.mult, OP.add, [bQX], [bQX])
                CP(V, f8, src, list(ins), [bQX])
            else:
                TS1(V, ti, src.bitcast(I16), 0x7C00, OP.bitwise_and, list(ins), [bQX])
                clampv = (9 << 10) if kind == "43f16" else (1 << 10)
                TS1(V, ti, ti, clampv, OP.max, [bQX], [bQX])
                TS1(V, Fn, ti, clampv + (1 << 10), OP.is_ge, [bQX], [bn])
                TS1(V, ti, ti, 3, OP.logical_shift_right, [bQX], [bQX])
                if kind == "43f16":
                    TS2(V, pm, ti, -1, (127 + 15 + 3) << 7, OP.mult, OP.add, [bQX], [bQX])
                    TS1(V, t2, ti, (127 - 15 - 3) << 7, OP.add, [bQX], [bQX])
                    CP(V, f8, src, list(ins), [bQX])
                else:
                    TS2(V, pm, ti, -1, (127 + 15 + 10) << 7, OP.mult, OP.add, [bQX], [bQX])
                    TS1(V, t2, ti, (127 - 15 - 10) << 7, OP.add, [bQX], [bQX])
                    CP(V, QL16[0:n, :W], src.bitcast(I16), list(ins), [bQX])
            TT(V, Fr, src, pm.bitcast(BF16), OP.mult, list(ins) + [bQX], [br])
            if kind == "510f16":
                CP(V, Fb, QL16[0:n, :W].bitcast(F16), [bQX], [bb])
            else:
                CP(V, l8, f8.bitcast(I8), [bQX], [bQX])
                CP(V, Fb, l8.bitcast(F8E4), [bQX], [bb])
            TT(V, Fg, Fr, Fr, OP.mult, [br], [bg])
            hi_t = 4194000.0 if kind == "510f16" else 255.9
            lo_t = 1048000.0 if kind == "510f16" else 63.9
            TS1(V, Fc, Fg, hi_t, OP.is_ge, [bg], [bc])
            TS1(V, Fl, Fg, lo_t, OP.is_le, [bg], [bl])
            TT(V, Fl, Fl, Fn, OP.mult, [bl, bn], [bl])
            STT(Fx, Fc, -0.5, Fl, OP.mult, OP.add, [bc, bl], [bx])
            STT(Fg, Fx, 1.0, Fr, OP.add, OP.mult, [bx, br], [bg])
            TS2(V, Fg, Fg, VST[0:n, 64:65], 0.0, OP.mult, OP.add, [bg, bVST], [bg])
            TS2(V, Fr, Fg, M64C, M64C, OP.add, OP.subtract, [bg], [br])
            STT(Fx, Fl, -0.5, Fc, OP.mult, OP.add, [bl, bc], [bx])
            TT(V, Fn, Fr, t2.bitcast(BF16), OP.mult, [br, bQX], [bn])
            STT(Fn, Fx, 1.0, Fn, OP.add, OP.mult, [bx, bn], [bn])
            TS1(V, pm, t2, 6720, OP.add, [bQX], [bQX])
            TT(V, Fl, Fb, pm.bitcast(BF16), OP.add, [bb, bQX], [bl])
            TT(V, Fl, Fl, pm.bitcast(BF16), OP.subtract, [bl, bQX], [bl])
            TT(V, Fb, Fn, Fb, OP.add, [bn, bb], [bb])
            return TT(V, dst, Fb, Fl, OP.subtract, [bb, bl], list(outs))

        MS("vector", VST[:, 64:65], 2.0 ** -41, [], [bVST])
        TS2("vector", VST[:, 64:65], VST[:, 64:65], -1.0, 1.0, OP.mult, OP.add, [bVST], [bVST])

        fl_wide = [(WST[:, 0:1024], bWST), (WST[:, 1024:2048], bWST), (TRASH, bTRASH),
                   (SC0, bSC0), (SCF, bSCF), (PB[:, 0:1024], bPB), (PB[:, 1024:2048], bPB)]
        fl_ffn = [(GQ1, bGQ1), (TRASH, bTRASH), (SC1, bSC1), (SC2, bSC2),
                  (SC3, bSC3), (SC4, bSC4), (SCH2, bSH2)]
        fl_tail = [(GQ1, bGQ1), (TRASH, bTRASH), (SC1, bSC1), (SC2, bSC2),
                   (SC3, bSC3), (SC4, bSC4), (SCH3, bSH3)]
        fl_wq = [(PB[:, 2048 + i * 1024:3072 + i * 1024], bPB) for i in range(7)]

        def qpair_h(dst, src, W, ins, outs):
            ti, t2, pm = SCI[:, :W], SCI2[:, :W], GQI[:, :W]
            f8, l8 = F8Q[:, :W], L8Q[:, :W]
            T1, bT1 = PB[:, 18432:18432 + W], bPB
            T2, bT2 = PB[:, 19456:19456 + W], bPB
            T3, bT3 = TRASH[:, :W], bTRASH
            Tc, bTc = SC1[:, :W], bSC1
            Tn, bTn = SC2[:, :W], bSC2
            Tx, bTx = SC3[:, :W], bSC3
            Ty, bTy = SC4[:, :W], bSC4
            V = "vector"
            TS1(V, ti, src.bitcast(I16), 0x7C00, OP.bitwise_and, list(ins), [bQX])
            TS1(V, ti, ti, 1 << 10, OP.max, [bQX], [bQX])
            TS1(V, Tn, ti, 2 << 10, OP.is_ge, [bQX], [bTn])
            TS1(V, ti, ti, 3, OP.logical_shift_right, [bQX], [bQX])
            TS2(V, pm, ti, -1, 19456, OP.mult, OP.add, [bQX], [bQX])
            TS1(V, t2, ti, 13056, OP.add, [bQX], [bQX])
            TT(V, T1, src, pm.bitcast(BF16), OP.mult, list(ins) + [bQX], [bT1])
            TS1(V, Tc, T1, 2047.9, OP.is_ge, [bT1], [bTc])
            TS1(V, Tx, T1, 1023.9, OP.is_ge, [bT1], [bTx])
            TT(V, Tc, Tc, Tn, OP.mult, [bTc, bTn], [bTc])
            TS2(V, Tx, Tx, -1.0, 1.0, OP.mult, OP.add, [bTx], [bTx])
            TT(V, Tx, Tx, Tn, OP.mult, [bTx, bTn], [bTx])
            STT(Ty, Tc, -0.5, Tx, OP.mult, OP.add, [bTc, bTx], [bTy])
            STT(T1, Ty, 1.0, T1, OP.add, OP.mult, [bTy, bT1], [bT1])
            TS2(V, T1, T1, VST[:, 64:65], 0.0, OP.mult, OP.add, [bT1, bVST], [bT1])
            TS2(V, T1, T1, M64C, M64C, OP.add, OP.subtract, [bT1], [bT1])
            TT(V, T1, T1, t2.bitcast(BF16), OP.mult, [bT1, bQX], [bT1])
            STT(Tx, Tx, -0.5, Tc, OP.mult, OP.add, [bTx, bTc], [bTx])
            STT(T1, Tx, 1.0, T1, OP.add, OP.mult, [bTx, bT1], [bT1])
            TS2(V, t2, t2, 896, 15104, OP.add, OP.max, [bQX], [bQX])
            TS2(V, pm, t2, -1, 32512, OP.mult, OP.add, [bQX], [bQX])
            TT(V, T1, T1, pm.bitcast(BF16), OP.mult, [bT1, bQX], [bT1])
            STT(T1, Ty, 1.0, T1, OP.add, OP.mult, [bTy, bT1], [bT1])
            TS2(V, T1, T1, VST[:, 64:65], 0.0, OP.mult, OP.add, [bT1, bVST], [bT1])
            TS2(V, T1, T1, M64C, M64C, OP.add, OP.subtract, [bT1], [bT1])
            TT(V, T1, T1, t2.bitcast(BF16), OP.mult, [bT1, bQX], [bT1])
            STT(T1, Tx, 1.0, T1, OP.add, OP.mult, [bTx, bT1], [bT1])
            CP(V, f8, src, list(ins), [bQX])
            CP(V, l8, f8.bitcast(I8), [bQX], [bQX])
            CP(V, T2, l8.bitcast(F8E4), [bQX], [bT2])
            TS1(V, pm, t2, 6720, OP.add, [bQX], [bQX])
            TT(V, T3, T2, pm.bitcast(BF16), OP.add, [bT2, bQX], [bT3])
            TT(V, T3, T3, pm.bitcast(BF16), OP.subtract, [bT3, bQX], [bT3])
            TT(V, T2, T1, T2, OP.add, [bT1, bT2], [bT2])
            return TT(V, dst, T2, T3, OP.subtract, [bT2, bT3], list(outs))

        def qy510(dst, src, W, ins, outs):
            ti, t2, pm = SCI[:, :W], SCI2[:, :W], GQI[:, :W]
            T1, bT1 = TRASH[:, :W], bTRASH
            T2, bT2 = SCH4[:, :W], bSH4
            T3, bT3 = GQ1[:, :W], bGQ1
            Tc, bTc = SC1[:, :W], bSC1
            Tn, bTn = SC2[:, :W], bSC2
            Tx, bTx = SC3[:, :W], bSC3
            Ty, bTy = SC4[:, :W], bSC4
            V = "vector"
            TS1(V, ti, src.bitcast(I16), 0x7C00, OP.bitwise_and, list(ins), [bQX])
            TS1(V, ti, ti, 1 << 10, OP.max, [bQX], [bQX])
            TS1(V, Tn, ti, 2 << 10, OP.is_ge, [bQX], [bTn])
            TS1(V, ti, ti, 3, OP.logical_shift_right, [bQX], [bQX])
            TS2(V, pm, ti, -1, 19456, OP.mult, OP.add, [bQX], [bQX])
            TS1(V, t2, ti, 13056, OP.add, [bQX], [bQX])
            TT(V, T1, src, pm.bitcast(BF16), OP.mult, list(ins) + [bQX], [bT1])
            TT(V, Tc, T1, T1, OP.mult, [bT1], [bTc])
            TS1(V, Tx, Tc, 1048000.0, OP.is_ge, [bTc], [bTx])
            TS1(V, Tc, Tc, 4194000.0, OP.is_ge, [bTc], [bTc])
            TT(V, Tc, Tc, Tn, OP.mult, [bTc, bTn], [bTc])
            TS2(V, Tx, Tx, -1.0, 1.0, OP.mult, OP.add, [bTx], [bTx])
            TT(V, Tx, Tx, Tn, OP.mult, [bTx, bTn], [bTx])
            STT(Ty, Tc, -0.5, Tx, OP.mult, OP.add, [bTc, bTx], [bTy])
            STT(T1, Ty, 1.0, T1, OP.add, OP.mult, [bTy, bT1], [bT1])
            TS2(V, T1, T1, VST[:, 64:65], 0.0, OP.mult, OP.add, [bT1, bVST], [bT1])
            TS2(V, T1, T1, M64C, M64C, OP.add, OP.subtract, [bT1], [bT1])
            TT(V, T1, T1, t2.bitcast(BF16), OP.mult, [bT1, bQX], [bT1])
            STT(Tx, Tx, -0.5, Tc, OP.mult, OP.add, [bTx, bTc], [bTx])
            STT(T1, Tx, 1.0, T1, OP.add, OP.mult, [bTx, bT1], [bT1])
            CP(V, QL16[:, :W], src.bitcast(I16), list(ins), [bQX])
            CP(V, T2, QL16[:, :W].bitcast(F16), [bQX], [bT2])
            TS1(V, pm, t2, 6720, OP.add, [bQX], [bQX])
            TT(V, T3, T2, pm.bitcast(BF16), OP.add, [bT2, bQX], [bT3])
            TT(V, T3, T3, pm.bitcast(BF16), OP.subtract, [bT3, bQX], [bT3])
            TT(V, T2, T1, T2, OP.add, [bT1, bT2], [bT2])
            return TT(V, dst, T2, T3, OP.subtract, [bT2, bT3], list(outs))

        DMA("sync", IDT[:, :], ident_h, "d_c", [], [bIDT])
        MS("vector", ONES[:, :], 1.0, [], [bONES])
        DMA("sync", CST[:, :], trel_h, "d_c", [], [bCST])
        for h in range(NH):
            hi, lo = _pc(h)
            TS1("vector", SC0[:, :128], CST[:, :], hi, OP.mult, [bCST], [bSC0])
            TS1("vector", SCF[:, :128], CST[:, :], lo, OP.mult, [bCST], [bSCF])
            TT("vector", SC0[:, :128], SC0[:, :128], SCF[:, :128], OP.add, [bSC0, bSCF], [bSC0])
            ACT(DKT[:, h * 128:(h + 1) * 128], SC0[:, :128], AF.Exp, [bSC0], [bDKT])
        DMA("sync", CST[:, :], tip1_h, "d_c", [bCST], [bCST])
        for p in range(NPAIR):
            for half, h in ((0, 2 * p), (1, 2 * p + 1)):
                hi, lo = _pc(h)
                sl = slice(64 * half, 64 * half + 64)
                TS1("vector", SC0[sl, :128], CST[sl, :], hi, OP.mult, [bCST], [bSC0])
                TS1("vector", SCF[sl, :128], CST[sl, :], lo, OP.mult, [bCST], [bSCF])
            TT("vector", SC0[:, :128], SC0[:, :128], SCF[:, :128], OP.add, [bSC0, bSCF], [bSC0])
            ACT(GMT[:, p * 128:(p + 1) * 128], SC0[:, :128], AF.Exp, [bSC0], [bGMT])
        DMA("sync", CST[:, 0:1], trev_h, "d_c", [bCST], [bCST])
        for h in range(NH):
            hi, lo = _pc(h)
            TS1("vector", SC0[:, 0:1], CST[:, 0:1], hi, OP.mult, [bCST], [bSC0])
            TS1("vector", SCF[:, 0:1], CST[:, 0:1], lo, OP.mult, [bCST], [bSCF])
            TT("vector", SC0[:, 0:1], SC0[:, 0:1], SCF[:, 0:1], OP.add, [bSC0, bSCF], [bSC0])
            ACT(KGV[:, h:h + 1], SC0[:, 0:1], AF.Exp, [bSC0], [bKGV])
        for p in range(NPAIR):
            for half, h in ((0, 2 * p), (1, 2 * p + 1)):
                hi, lo = _pc(h)
                sl = slice(64 * half, 64 * half + 64)
                MS("vector", SC0[sl, 0:1], float(np.float32(hi)) * 128.0, [], [bSC0])
                MS("vector", SCF[sl, 0:1], float(np.float32(lo)) * 128.0, [], [bSCF])
            TT("vector", SC0[:, 0:1], SC0[:, 0:1], SCF[:, 0:1], OP.add, [bSC0, bSCF], [bSC0])
            ACT(GCV[:, p:p + 1], SC0[:, 0:1], AF.Exp, [bSC0], [bGCV])
        MS("vector", SBLK[:, :], 0.0, [], [bSBLK])
        bSBLK16 = B_("sblk16")
        MS("vector", PW[:, 16384:17408], 0.0, [], [bSBLK16])
        for v, name in enumerate(("ln1w", "ln1b", "ln2w", "ln2b")):
            DMA("sync", ROWS[0:1, 0:D], ln_h[name].rearrange("(o d) -> o d", o=1), "d_c", [bROWS], [bROWS])
            for cc in range(ND):
                TRANS(ps[7][:, 0:1], ROWS[0:1, cc * 128:cc * 128 + 128], [bROWS], [bPS[7]])
                ACT(LNV[:, v * ND + cc:v * ND + cc + 1], ps[7][:, 0:1], AF.Copy, [bPS[7]], [bLNV])
        LN1W, LN1B, LN2W, LN2B = 0, ND, 2 * ND, 3 * ND
        for half in range(2):
            for k in range(NKH):
                w = HID_T[k]
                DMA("sync", ROWS[0:1, 0:w], bin_h[half * HID + k * 128:half * HID + k * 128 + w].rearrange("(o d) -> o d", o=1),
                    "d_c", [bROWS], [bROWS])
                TRANS(ps[7][0:w, 0:1], ROWS[0:1, 0:w], [bROWS], [bPS[7]])
                ACT(BIV[0:w, half * NKH + k:half * NKH + k + 1], ps[7][0:w, 0:1], AF.Copy, [bPS[7]], [bBIV])
        DMA("sync", ROWS[0:1, 0:D], bout_h.rearrange("(o d) -> o d", o=1), "d_c", [bROWS], [bROWS])
        CP("vector", ROWB[0:1, 0:D], ROWS[0:1, 0:D], [bROWS], [bROWB])
        for cchunk in range(4):
            DMA("sync", ROWS[0:1, 0:1024], b1_h[cchunk * 1024:(cchunk + 1) * 1024].rearrange("(o d) -> o d", o=1),
                "d_c", [bROWS], [bROWS])
            CP("vector", SC4[0:1, 0:1024], ROWS[0:1, 0:1024], [bROWS], [bSC4])
            qx(SC4[0:1, 0:1024], SC4[0:1, 0:1024], 1024, fl_wide, "43bf", [bSC4], [bSC4], n=1)
            CP("vector", SCF[0:1, 0:1024], SC4[0:1, 0:1024], [bSC4], [bSCF])
            for k in range(8):
                j = cchunk * 8 + k
                TRANS(ps[7][:, 0:1], SCF[0:1, k * 128:(k + 1) * 128], [bSCF], [bPS[7]])
                ACT(QB1[:, j:j + 1], ps[7][:, 0:1], AF.Copy, [bPS[7]], [bQB1])
        DMA("sync", ROWS[0:1, 0:D], b2_h.rearrange("(o d) -> o d", o=1), "d_c", [bROWS], [bROWS])
        CP("vector", SC4[0:1, 0:1024], ROWS[0:1, 0:1024], [bROWS], [bSC4])
        qx(ROWB[0:1, 1024:2048], SC4[0:1, 0:1024], 1024, fl_wide, "43bf", [bSC4], [bROWB], n=1)

        def stats_batch(tiles_fn, n, eps, cmu=0, ce2=16, ct1=32, ct2=48):
            for t in range(n):
                ap, bufs = tiles_fn(t)
                ACT(TRASH[:, :D], ap, AF.Identity, bufs, [bTRASH, bVST], accum=VST[:, cmu + t:cmu + t + 1])
                ACT(TRASH[:, :D], ap, AF.Square, bufs, [bTRASH, bVST], accum=VST[:, ce2 + t:ce2 + t + 1])
            TS1("vector", VST[:, cmu:cmu + n], VST[:, cmu:cmu + n], 1.0 / D, OP.mult, [bVST], [bVST])
            TS1("vector", VST[:, ce2:ce2 + n], VST[:, ce2:ce2 + n], 1.0 / D, OP.mult, [bVST], [bVST])
            TT("vector", VST[:, ct1:ct1 + n], VST[:, cmu:cmu + n], VST[:, cmu:cmu + n], OP.mult, [bVST], [bVST])
            TT("vector", VST[:, ce2:ce2 + n], VST[:, ce2:ce2 + n], VST[:, ct1:ct1 + n], OP.subtract, [bVST], [bVST])
            TS1("vector", VST[:, ce2:ce2 + n], VST[:, ce2:ce2 + n], eps, OP.add, [bVST], [bVST])
            ACT(VST[:, ct2:ct2 + n], VST[:, ce2:ce2 + n], AF.Sqrt, [bVST], [bVST])
            RCP(VST[:, ce2:ce2 + n], VST[:, ct2:ct2 + n], [bVST], [bVST])

        bxn_d, by_d, bxm_d, bxq_d, bx1_d = (B_(x) for x in ("xn_d", "y_d", "xm_d", "xq_d", "x1_d"))

        def ln_stats(src_ap, W, src_bufs, mu_sl, rs_sl, eps):
            ACT(TRASH[:, :W], src_ap, AF.Identity, src_bufs, [bTRASH, bVST], accum=VST[:, mu_sl:mu_sl + 1])
            ACT(TRASH[:, :W], src_ap, AF.Square, src_bufs, [bTRASH, bVST], accum=VST[:, rs_sl:rs_sl + 1])
            TS1("vector", VST[:, mu_sl:mu_sl + 1], VST[:, mu_sl:mu_sl + 1], 1.0 / W, OP.mult, [bVST], [bVST])
            TS1("vector", VST[:, rs_sl:rs_sl + 1], VST[:, rs_sl:rs_sl + 1], 1.0 / W, OP.mult, [bVST], [bVST])
            TT("vector", VST[:, 60:61], VST[:, mu_sl:mu_sl + 1], VST[:, mu_sl:mu_sl + 1], OP.mult, [bVST], [bVST])
            TT("vector", VST[:, rs_sl:rs_sl + 1], VST[:, rs_sl:rs_sl + 1], VST[:, 60:61], OP.subtract, [bVST], [bVST])
            TS1("vector", VST[:, rs_sl:rs_sl + 1], VST[:, rs_sl:rs_sl + 1], eps, OP.add, [bVST], [bVST])
            ACT(VST[:, 61:62], VST[:, rs_sl:rs_sl + 1], AF.Sqrt, [bVST], [bVST])
            RCP(VST[:, rs_sl:rs_sl + 1], VST[:, 61:62], [bVST], [bVST])

        for t in range(NT):
            DMA("gpsimd", R0[:, t * D:(t + 1) * D], x_h[t * 128:(t + 1) * 128, :], "d_x",
                [bR0[t]], [bR0[t]])
        stats_batch(lambda t: (R0[:, t * D:(t + 1) * D], [bR0[t]]), NT, EPS1)
        for t in range(NT):
            TS2("vector", R0[:, t * D:(t + 1) * D], R0[:, t * D:(t + 1) * D],
                VST[:, t:t + 1], VST[:, 16 + t:17 + t], OP.subtract, OP.mult, [bR0[t], bVST], [bR0[t]])
        stats_batch(lambda t: (R0[:, t * D:(t + 1) * D], [bR0[t]]), NT, EPS1)
        for t in range(NT):
            dst, bdst = (SC2, bSC2) if t % 2 == 0 else (SC3, bSC3)
            TS2("vector", dst[:, :], R0[:, t * D:(t + 1) * D], VST[:, t:t + 1], VST[:, 16 + t:17 + t],
                OP.subtract, OP.mult, [bR0[t], bVST], [bdst])
            DMA("sync", xn_d[t * 128:(t + 1) * 128, :], dst[:, :], "d_b", [bdst], [bxn_d])

        for cc in range(ND):
            DMA("sync", TFM[:, cc * S:(cc + 1) * S], xn_d[:, cc * 128:(cc + 1) * 128], "d_t",
                [bxn_d, bTFM], [bTFM], transpose=True)
            TS2("vector", TFM[:, cc * S:(cc + 1) * S], TFM[:, cc * S:(cc + 1) * S],
                LNV[:, LN1W + cc:LN1W + cc + 1], LNV[:, LN1B + cc:LN1B + cc + 1],
                OP.mult, OP.add, [bTFM, bLNV], [bTFM])

        bw1q_d, bw2q_d = B_("w1q_d"), B_("w2q_d")
        bWQ = [B_("wq0"), B_("wq1")]

        def emit_wquant_wide():
            for j in range(64):
                slab = PB[:, (j % 2) * 1024:(j % 2) * 1024 + 1024]
                bs = bWQ[j % 2]
                louts = [bs] + ([bQB, bKB, bKG, bVB, bGB] if j < 2 else [])
                if j < 32:
                    src = w1_h[j * 128:(j + 1) * 128, :]
                    dst = w1q_d[j * 128:(j + 1) * 128, :]
                    bqd = bw1q_d
                else:
                    j2 = j - 32
                    src = w2_h[(j2 // 4) * 128:(j2 // 4 + 1) * 128, (j2 % 4) * 1024:(j2 % 4) * 1024 + 1024]
                    dst = w2q_d[(j2 // 4) * 128:(j2 // 4 + 1) * 128, (j2 % 4) * 1024:(j2 % 4) * 1024 + 1024]
                    bqd = bw2q_d
                DMA("gpsimd", slab, src, "d_q", [bs], louts)
                qx(slab, slab, 1024, fl_wq, "43bf", [bs], [bs])
                DMA("gpsimd", dst, slab, "d_q", [bs, bPB], [bqd])

        def load_wslot(s, wh):
            DMA("gpsimd", PW[:, s * 8192:(s + 1) * 8192].rearrange("p (kd n) -> p kd n", kd=ND),
                wh.rearrange("(kd p) n -> p kd n", p=128), "d_w", [bSLOT[s]], [bSLOT[s]])

        def wslot_tile(s, k, m):
            return PW[:, s * 8192 + k * D + m * 128:s * 8192 + k * D + (m + 1) * 128]

        def wslot_mov(s, k, n):
            return PW[:, s * 8192 + k * D + n * 512:s * 8192 + k * D + n * 512 + 512]

        def xnT(k, lo, w):
            return TFM[:, k * S + lo:k * S + lo + w]

        QBo, KBo, KGo, VBo, GBo = 0, 4096, 8192, 12288, 16384
        bQB, bKB, bKG, bVB, bGB = B_("qb"), B_("kb"), B_("kg"), B_("vb"), B_("gb")

        for Bi in range(NBLK):
            lo = Bi * BLK
            load_wslot(0, wq_h)
            for m in range(ND):
                pb_ = bPS[m % 2]
                psn = ps[m % 2]
                for kk in range(ND):
                    MM(psn[:, :], wslot_tile(0, kk, m), xnT(kk, lo, BLK),
                       start=(kk == 0), stop=(kk == ND - 1),
                       ins=[bSLOT[0], bTFM], outs=[pb_], inc=(kk == ND - 1))
                ACT(PB[:, QBo + m * BLK:QBo + (m + 1) * BLK], psn[:, :], AF.Copy,
                    [pb_], [bQB], scale=0.125)
            load_wslot(1, wk_h)
            for m in range(ND):
                pb_ = bPS[2 + m % 2]
                psn = ps[2 + m % 2]
                for kk in range(ND):
                    MM(psn[:, :], wslot_tile(1, kk, m), xnT(kk, lo, BLK),
                       start=(kk == 0), stop=(kk == ND - 1),
                       ins=[bSLOT[1], bTFM], outs=[pb_], inc=(kk == ND - 1))
                ACT(PB[:, KBo + m * BLK:KBo + (m + 1) * BLK], psn[:, :], AF.Copy, [pb_], [bKB])
            for mt in range(4):
                for nn in range(2):
                    pb_ = bPS[(mt * 2 + nn) % 2]
                    psn = ps[(mt * 2 + nn) % 2]
                    for kk in range(ND):
                        MM(psn[:, :], xnT(kk, lo + mt * 128, 128), wslot_mov(1, kk, nn),
                           start=(kk == 0), stop=(kk == ND - 1),
                           ins=[bSLOT[1], bTFM], outs=[pb_], inc=(kk == ND - 1))
                    for hh in range(8):
                        h = nn * 8 + hh
                        TS1("vector",
                            PB[:, KGo + mt * 1024 + nn * 512 + hh * 64:KGo + mt * 1024 + nn * 512 + hh * 64 + 64],
                            psn[:, hh * 64:hh * 64 + 64], KGV[:, h:h + 1], OP.mult,
                            [pb_, bKGV], [bKG])
            load_wslot(0, wv_h)
            for mt in range(4):
                for nn in range(2):
                    pb_ = bPS[2 + (mt * 2 + nn) % 2]
                    psn = ps[2 + (mt * 2 + nn) % 2]
                    for kk in range(ND):
                        MM(psn[:, :], xnT(kk, lo + mt * 128, 128), wslot_mov(0, kk, nn),
                           start=(kk == 0), stop=(kk == ND - 1),
                           ins=[bSLOT[0], bTFM], outs=[pb_], inc=(kk == ND - 1))
                    ACT(PB[:, VBo + mt * 1024 + nn * 512:VBo + mt * 1024 + nn * 512 + 512],
                        psn[:, :], AF.Copy, [pb_], [bVB])
            load_wslot(1, wg_h)
            for mt in range(4):
                for nn in range(2):
                    pb_ = bPS[(mt * 2 + nn) % 2]
                    psn = ps[(mt * 2 + nn) % 2]
                    for kk in range(ND):
                        MM(psn[:, :], xnT(kk, lo + mt * 128, 128), wslot_mov(1, kk, nn),
                           start=(kk == 0), stop=(kk == ND - 1),
                           ins=[bSLOT[1], bTFM], outs=[pb_], inc=(kk == ND - 1))
                    ACT(PB[:, GBo + mt * 1024 + nn * 512:GBo + mt * 1024 + nn * 512 + 512],
                        psn[:, :], AF.Silu, [pb_], [bGB])

            for cc4 in range(4):
                g = Bi * 4 + cc4
                clo = cc4 * 128
                for p in range(NPAIR):
                    h1, h2 = 2 * p, 2 * p + 1
                    MM(ps[4][:, 0:128], PB[0:64, KBo + p * BLK + clo:KBo + p * BLK + clo + 128],
                       PB[0:64, QBo + p * BLK + clo:QBo + p * BLK + clo + 128],
                       start=True, stop=True, ins=[bKB, bQB], outs=[bPS[4]], inc=True)
                    MM(ps[7][:, 0:128], PB[64:128, KBo + p * BLK + clo:KBo + p * BLK + clo + 128],
                       PB[64:128, QBo + p * BLK + clo:QBo + p * BLK + clo + 128],
                       start=True, stop=True, ins=[bKB, bQB], outs=[bPS[7]], inc=True)
                    TT("vector", SC1[:, 0:128], ps[4][:, 0:128], DKT[:, h1 * 128:(h1 + 1) * 128],
                       OP.mult, [bPS[4], bDKT], [bSC1])
                    TT("vector", SC1[:, 128:256], ps[7][:, 0:128], DKT[:, h2 * 128:(h2 + 1) * 128],
                       OP.mult, [bPS[7], bDKT], [bSC1])
                    TT("vector", SC2[:, 0:128], PB[:, QBo + p * BLK + clo:QBo + p * BLK + clo + 128],
                       GMT[:, p * 128:(p + 1) * 128], OP.mult, [bQB, bGMT], [bSC2])
                    MM(ps[5][:, 0:64], SC1[:, 0:128],
                       PB[:, VBo + cc4 * 1024 + h1 * 64:VBo + cc4 * 1024 + h1 * 64 + 64],
                       start=True, stop=False, ins=[bSC1, bVB], outs=[bPS[5]], inc=False)
                    MM(ps[5][:, 64:128], SC1[:, 128:256],
                       PB[:, VBo + cc4 * 1024 + h2 * 64:VBo + cc4 * 1024 + h2 * 64 + 64],
                       start=False, stop=(g == 0), ins=[bSC1, bVB], outs=[bPS[5]], inc=(g == 0))
                    if g > 0:
                        MM(ps[5][:, 0:128], SC2[:, 0:128], PW[:, 16384 + p * 128:16384 + (p + 1) * 128],
                           start=False, stop=True, ins=[bSC2, bSBLK16], outs=[bPS[5]], inc=True)
                    MM(ps[6][:, 0:128], PB[:, KGo + cc4 * 1024 + p * 128:KGo + cc4 * 1024 + (p + 1) * 128],
                       PB[:, VBo + cc4 * 1024 + p * 128:VBo + cc4 * 1024 + (p + 1) * 128],
                       start=True, stop=True, ins=[bKG, bVB], outs=[bPS[6]], inc=True)
                    ACT(SC3[:, p * 128:(p + 1) * 128], ps[5][:, 0:128], AF.Copy, [bPS[5]], [bSC3])
                    STT(SBLK[0:64, p * 128:p * 128 + 64], SBLK[0:64, p * 128:p * 128 + 64],
                        GCV[0:64, p:p + 1], ps[6][0:64, 0:64], OP.mult, OP.add,
                        [bSBLK, bPS[6], bGCV], [bSBLK])
                    STT(SBLK[64:128, p * 128 + 64:p * 128 + 128], SBLK[64:128, p * 128 + 64:p * 128 + 128],
                        GCV[64:128, p:p + 1], ps[6][64:128, 64:128], OP.mult, OP.add,
                        [bSBLK, bPS[6], bGCV], [bSBLK])
                    CP("vector", PW[:, 16384 + p * 128:16384 + (p + 1) * 128],
                       SBLK[:, p * 128:(p + 1) * 128], [bSBLK], [bSBLK16])
                ACT(SC0[:, :], SC3[:, :], AF.Square, [bSC3], [bSC0])
                RED(VST[:, 0:16], SC3[:, :].rearrange("p (h d) -> p h d", h=16), OP.add, [bSC3], [bVST])
                RED(VST[:, 16:32], SC0[:, :].rearrange("p (h d) -> p h d", h=16), OP.add, [bSC0], [bVST])
                TS1("vector", VST[:, 0:16], VST[:, 0:16], 1.0 / 64, OP.mult, [bVST], [bVST])
                TS1("vector", VST[:, 16:32], VST[:, 16:32], 1.0 / 64, OP.mult, [bVST], [bVST])
                TT("vector", VST[:, 32:48], VST[:, 0:16], VST[:, 0:16], OP.mult, [bVST], [bVST])
                TT("vector", VST[:, 16:32], VST[:, 16:32], VST[:, 32:48], OP.subtract, [bVST], [bVST])
                TS1("vector", VST[:, 16:32], VST[:, 16:32], EPS2, OP.add, [bVST], [bVST])
                ACT(VST[:, 32:48], VST[:, 16:32], AF.Sqrt, [bVST], [bVST])
                RCP(VST[:, 16:32], VST[:, 32:48], [bVST], [bVST])
                for h in range(NH):
                    TS2("vector", SC3[:, h * 64:(h + 1) * 64], SC3[:, h * 64:(h + 1) * 64],
                        VST[:, h:h + 1], VST[:, 16 + h:17 + h], OP.subtract, OP.mult, [bSC3, bVST], [bSC3])
                TT("vector", SC3[:, :], SC3[:, :], PB[:, GBo + cc4 * 1024:GBo + (cc4 + 1) * 1024],
                   OP.mult, [bSC3, bGB], [bSC3])
                DMA("sync", y_d[g * 128:(g + 1) * 128, :], SC3[:, :], "d_b", [bSC3], [by_d])

        emit_wquant_wide()

        for cc in range(ND):
            DMA("sync", TFM[:, cc * S:(cc + 1) * S], y_d[:, cc * 128:(cc + 1) * 128], "d_t",
                [by_d, bTFM], [bTFM], transpose=True)
        load_wslot(0, wo_h)
        for mt in range(NT):
            for nn in range(2):
                pb_ = bPS[(mt * 2 + nn) % 4]
                psn = ps[(mt * 2 + nn) % 4]
                for kk in range(ND):
                    MM(psn[:, :], xnT(kk, mt * 128, 128), wslot_mov(0, kk, nn),
                       start=(kk == 0), stop=(kk == ND - 1),
                       ins=[bSLOT[0], bTFM], outs=[pb_], inc=(kk == ND - 1))
                TT("vector", R0[:, mt * D + nn * 512:mt * D + nn * 512 + 512],
                   psn[:, :], R0[:, mt * D + nn * 512:mt * D + nn * 512 + 512],
                   OP.add, [pb_, bR0[mt]], [bR0[mt]])

        stats_batch(lambda t: (R0[:, t * D:(t + 1) * D], [bR0[t]]), NT, EPS1)
        for t in range(NT):
            dst, bdst = (SC2, bSC2) if t % 2 == 0 else (SC3, bSC3)
            TS2("vector", dst[:, :], R0[:, t * D:(t + 1) * D], VST[:, t:t + 1], VST[:, 16 + t:17 + t],
                OP.subtract, OP.mult, [bR0[t], bVST], [bdst])
            DMA("sync", xm_d[t * 128:(t + 1) * 128, :], dst[:, :], "d_b", [bdst], [bxm_d])
        for cc in range(ND):
            DMA("sync", TFM[:, cc * S:(cc + 1) * S], xm_d[:, cc * 128:(cc + 1) * 128], "d_t",
                [bxm_d, bTFM], [bTFM], transpose=True)
            TS2("vector", TFM[:, cc * S:(cc + 1) * S], TFM[:, cc * S:(cc + 1) * S],
                LNV[:, LN2W + cc:LN2W + cc + 1], LNV[:, LN2B + cc:LN2B + cc + 1],
                OP.mult, OP.add, [bTFM, bLNV], [bTFM])

        for k in range(NKH):
            w = HID_T[k]
            if k == 0:
                DMA("gpsimd", PW[0:w, k * D:(k + 1) * D], wout_h[k * 128:k * 128 + w, :],
                    "d_w2", [bPW], [bPW, bSLOT[0], bSLOT[1], bSBLK16])
            else:
                DMA("gpsimd", PW[0:w, k * D:(k + 1) * D], wout_h[k * 128:k * 128 + w, :],
                    "d_w2", [bPW], [bPW])
        for tb in range(4):
            tlo = tb * 512
            for k in range(NKH):
                w = HID_T[k]
                for half, hofs in ((0, 0), (1, HID)):
                    DMA("gpsimd", WST[:, half * 1024:half * 1024 + 8 * w].rearrange("p (kd n) -> p kd n", kd=ND),
                        win_h[:, hofs + k * 128:hofs + k * 128 + w].rearrange("(kd p) n -> p kd n", p=128),
                        "d_w", [bWST], [bWST])
                for half, pbank in ((0, 0), (1, 1)):
                    for kk in range(ND):
                        MM(ps[pbank][0:w, :], WST[:, half * 1024 + kk * w:half * 1024 + (kk + 1) * w],
                           xnT(kk, tlo, 512), start=(kk == 0), stop=(kk == ND - 1),
                           ins=[bWST, bTFM], outs=[bPS[pbank]], inc=(kk == ND - 1))
                ACT(SC1[0:w, 0:512], ps[0][0:w, :], AF.Identity, [bPS[0], bBIV], [bSC1], bias=BIV[0:w, k:k + 1])
                ACT(SC2[0:w, 0:512], ps[1][0:w, :], AF.Silu, [bPS[1], bBIV], [bSC2], bias=BIV[0:w, NKH + k:NKH + k + 1])
                TT("vector", PB[0:w, k * 512:k * 512 + 512], SC1[0:w, 0:512], SC2[0:w, 0:512],
                   OP.mult, [bSC1, bSC2], [bPB])
            for mt in range(4):
                t_glob = tb * 4 + mt
                for nn in range(2):
                    pb_ = bPS[2 + (mt * 2 + nn) % 2]
                    psn = ps[2 + (mt * 2 + nn) % 2]
                    for k in range(NKH):
                        w = HID_T[k]
                        MM(psn[:, :], PB[0:w, k * 512 + mt * 128:k * 512 + mt * 128 + 128],
                           PW[0:w, k * D + nn * 512:k * D + nn * 512 + 512],
                           start=(k == 0), stop=False, ins=[bPB, bPW], outs=[pb_], inc=False)
                    MM(psn[:, :], ONES[0:1, 0:128], ROWB[0:1, nn * 512:nn * 512 + 512],
                       start=False, stop=True, ins=[bONES, bROWB], outs=[pb_], inc=True)
                    TT("vector", R0[:, t_glob * D + nn * 512:t_glob * D + nn * 512 + 512],
                       psn[:, :], R0[:, t_glob * D + nn * 512:t_glob * D + nn * 512 + 512],
                       OP.add, [pb_, bR0[t_glob]], [bR0[t_glob]])

        bw1qT_d, bw2qT_d = B_("w1qT_d"), B_("w2qT_d")
        for cc in range(ND):
            for hh in range(2):
                DMA("sync", WST[:, 0:2048], w1q_d[hh * 2048:(hh + 1) * 2048, cc * 128:(cc + 1) * 128],
                    "d_t", [bw1q_d, bWST], [bWST], transpose=True)
                DMA("sync", w1qT_d[cc * 128:(cc + 1) * 128, hh * 2048:(hh + 1) * 2048], WST[:, 0:2048],
                    "d_t", [bWST], [bw1qT_d])
        for jj in range(NJ):
            DMA("sync", WST[:, 0:1024], w2q_d[:, jj * 128:(jj + 1) * 128], "d_t",
                [bw2q_d, bWST], [bWST], transpose=True)
            DMA("sync", w2qT_d[jj * 128:(jj + 1) * 128, :], WST[:, 0:1024], "d_t", [bWST], [bw2qT_d])

        stats_batch(lambda t: (R0[:, t * D:(t + 1) * D], [bR0[t]]), NT, EPS1)
        for t in range(NT):
            dst, bdst = (SC2, bSC2) if t % 2 == 0 else (SC3, bSC3)
            xsrc, bxsrc = (SC4, bSC4) if t % 2 == 0 else (SC1, bSC1)
            DMA("gpsimd", xsrc[:, :], x_h[t * 128:(t + 1) * 128, :], "d_x", [bxsrc], [bxsrc])
            TS2("vector", dst[:, :], R0[:, t * D:(t + 1) * D], VST[:, t:t + 1], VST[:, 16 + t:17 + t],
                OP.subtract, OP.mult, [bR0[t], bVST], [bdst])
            TT("vector", dst[:, :], dst[:, :], xsrc[:, :], OP.add, [bdst, bxsrc], [bdst])
            DMA("sync", x1_d[t * 128:(t + 1) * 128, :], dst[:, :], "d_b", [bdst], [bx1_d])
            ACT(TRASH[:, :D], dst[:, :], AF.Identity, [bdst], [bTRASH, bVST], accum=VST[:, 32 + t:33 + t])
            ACT(TRASH[:, :D], dst[:, :], AF.Square, [bdst], [bTRASH, bVST], accum=VST[:, 48 + t:49 + t])
        TS1("vector", VST[:, 32:48], VST[:, 32:48], 1.0 / D, OP.mult, [bVST], [bVST])
        TS1("vector", VST[:, 48:64], VST[:, 48:64], 1.0 / D, OP.mult, [bVST], [bVST])
        TT("vector", VST[:, 0:16], VST[:, 32:48], VST[:, 32:48], OP.mult, [bVST], [bVST])
        TT("vector", VST[:, 48:64], VST[:, 48:64], VST[:, 0:16], OP.subtract, [bVST], [bVST])
        TS1("vector", VST[:, 48:64], VST[:, 48:64], EPS1, OP.add, [bVST], [bVST])
        ACT(VST[:, 0:16], VST[:, 48:64], AF.Sqrt, [bVST], [bVST])
        RCP(VST[:, 48:64], VST[:, 0:16], [bVST], [bVST])
        for t in range(NT):
            xsrc, bxsrc = (SC4, bSC4) if t % 2 == 0 else (SC1, bSC1)
            DMA("gpsimd", xsrc[:, :], x1_d[t * 128:(t + 1) * 128, :], "d_x", [bx1_d, bxsrc], [bxsrc])
            dst, bdst = (SC2, bSC2) if t % 2 == 0 else (SC3, bSC3)
            TS2("vector", dst[:, :], xsrc[:, :], VST[:, 32 + t:33 + t], VST[:, 48 + t:49 + t],
                OP.subtract, OP.mult, [bxsrc, bVST], [bdst])
            if "xf" in dbg:
                DMA("gpsimd", dbg_h["xf"][t * 128:(t + 1) * 128, :], dst[:, :], "d_g",
                    [bdst], [B_("xfdbg")])
            qx(dst[:, :], dst[:, :], 1024, fl_wide, "43bf", [bdst], [bdst])
            DMA("sync", xq_d[t * 128:(t + 1) * 128, :], dst[:, :], "d_b", [bdst], [bxq_d])

        bout_dma = B_("out")
        bhq = [B_("hq0"), B_("hq1")]
        bxqT = [B_("xqT0"), B_("xqT1")]

        def h_phase(tb):
            par = tb % 2
            hqo = par * 8192
            tlo = tb * FTB
            for cc in range(ND):
                DMA("sync", TFM[:, cc * S + par * FTB:cc * S + par * FTB + FTB],
                    xq_d[tlo:tlo + FTB, cc * 128:(cc + 1) * 128],
                    "d_t", [bxq_d, bxqT[par]], [bxqT[par]], transpose=True)
            for js in range(NJ):
                if js % 2 == 0:
                    wbuf, bwbuf, wch = WST[:, 0:1024], B_("wstA"), "d_fa"
                else:
                    wbuf, bwbuf, wch = PB[:, 16384:17408], B_("pbw1"), "d_fb"
                extra_outs = [bwbuf] if not (tb == 0 and js < 2) else [bwbuf, bPB if js == 1 else bWST]
                DMA("sync", wbuf.rearrange("p (kd n) -> p kd n", kd=ND),
                    w1qT_d[:, js * 128:(js + 1) * 128].rearrange("(kd p) n -> p kd n", p=128),
                    wch, [bw1qT_d], extra_outs)
                pb_ = bPS[4 + js % 2]
                psn = ps[4 + js % 2]
                for kk in range(ND):
                    MM(psn[:, 0:FTB], wbuf[:, kk * 128:(kk + 1) * 128],
                       TFM[:, kk * S + par * FTB:kk * S + par * FTB + FTB],
                       start=(kk == 0), stop=(kk == ND - 1), ins=[bwbuf, bxqT[par]],
                       outs=[pb_], inc=(kk == ND - 1))
                ACT(SCH1[:, (js % 4) * FTB:(js % 4 + 1) * FTB], psn[:, 0:FTB], AF.Relu,
                    [pb_, bQB1], [bSH1], bias=QB1[:, js:js + 1])
                if js % 4 == 3:
                    qpair_h(PB[:, hqo + (js - 3) * FTB:hqo + (js + 1) * FTB], SCH1[:, 0:1024],
                            1024, [bSH1], [bhq[par]])
                    if "hq" in dbg:
                        DMA("gpsimd",
                            dbg_h["hq"][(js - 3) * 128:(js + 1) * 128, tlo:tlo + FTB].rearrange("(a p) n -> p a n", p=128),
                            PB[:, hqo + (js - 3) * FTB:hqo + (js + 1) * FTB].rearrange("p (a n) -> p a n", a=4),
                            "d_g", [bhq[par]], [B_("dbgout")])

        def y_phase(tb):
            par = tb % 2
            hqo = par * 8192
            for js in range(NJ):
                if js % 2 == 0:
                    ybuf, bybuf, ych = WST[:, 1024:2048], B_("wstB"), "d_fc"
                else:
                    ybuf, bybuf, ych = PB[:, 17408:18432], B_("pbw2"), "d_fd"
                DMA("sync", ybuf, w2qT_d[js * 128:(js + 1) * 128, :], ych, [bw2qT_d], [bybuf])
                for sub in range(2):
                    for ih in range(2):
                        bidx = sub * 2 + ih
                        MM(ps[bidx][:, :], PB[:, hqo + js * FTB + sub * 128:hqo + js * FTB + sub * 128 + 128],
                           ybuf[:, ih * 512:ih * 512 + 512],
                           start=(js == 0), stop=False, ins=[bhq[par], bybuf], outs=[bPS[bidx]],
                           inc=(sub == 1 and ih == 1))
            mm_last = None
            for sub in range(2):
                for ih in range(2):
                    bidx = sub * 2 + ih
                    mm_last = MM(ps[bidx][:, :], ONES[0:1, 0:128],
                                 ROWB[0:1, 1024 + ih * 512:1024 + ih * 512 + 512],
                                 start=False, stop=True, ins=[bONES, bROWB], outs=[bPS[bidx]], inc=True)
            mm_bar = em.do("tensor", lambda e, c: e.nop(), extra=[mm_last], inc=True)
            for b_ in range(4):
                bPS[b_].w = mm_bar

        def tail_phase(tb):
            for sub in range(2):
                t_glob = tb * 2 + sub
                for ih in range(2):
                    bidx = sub * 2 + ih
                    o = ih * 512
                    ACT(SYE[:, o:o + 512], ps[bidx][:, :], AF.Copy, [bPS[bidx]], [bSYE])
                    qy510(SYE[:, o:o + 512], SYE[:, o:o + 512], 512, [bSYE], [bSYE])
                DMA("sync", SCH2[:, :], mask_h[t_glob * 128:(t_glob + 1) * 128, 0:512], "d_m", [bSH2], [bSH2])
                DMA("sync", SCH3[:, :], mask_h[t_glob * 128:(t_glob + 1) * 128, 512:1024], "d_m", [bSH3], [bSH3])
                TT("vector", SYE[:, 0:512], SYE[:, 0:512], SCH2[:, :], OP.mult, [bSYE, bSH2], [bSYE])
                TT("vector", SYE[:, 512:1024], SYE[:, 512:1024], SCH3[:, :], OP.mult, [bSYE, bSH3], [bSYE])
                if "ym" in dbg:
                    DMA("gpsimd", dbg_h["ym"][t_glob * 128:(t_glob + 1) * 128, :], SYE[:, :],
                        "d_g", [bSYE], [B_("dbgout")])
                ACT(TRASH[:, :D], SYE[:, :], AF.Identity, [bSYE], [bTRASH, bVST], accum=VST[:, 0:1])
                ACT(TRASH[:, :D], SYE[:, :], AF.Square, [bSYE], [bTRASH, bVST], accum=VST[:, 1:2])
                TS1("vector", VST[:, 0:1], VST[:, 0:1], 1.0 / D, OP.mult, [bVST], [bVST])
                TS1("vector", VST[:, 1:2], VST[:, 1:2], 1.0 / D, OP.mult, [bVST], [bVST])
                TT("vector", VST[:, 2:3], VST[:, 0:1], VST[:, 0:1], OP.mult, [bVST], [bVST])
                TT("vector", VST[:, 1:2], VST[:, 1:2], VST[:, 2:3], OP.subtract, [bVST], [bVST])
                TS1("vector", VST[:, 1:2], VST[:, 1:2], EPS1, OP.add, [bVST], [bVST])
                ACT(VST[:, 2:3], VST[:, 1:2], AF.Sqrt, [bVST], [bVST])
                RCP(VST[:, 1:2], VST[:, 2:3], [bVST], [bVST])
                DMA("gpsimd", SC1[:, :], x1_d[t_glob * 128:(t_glob + 1) * 128, :], "d_x",
                    [bx1_d, bSC1], [bSC1])
                TS2("vector", SC2[:, :], SYE[:, :], VST[:, 0:1], VST[:, 1:2],
                    OP.subtract, OP.mult, [bSYE, bVST], [bSC2])
                TT("vector", SC0[:, :], SC2[:, :], SC1[:, :], OP.add, [bSC2, bSC1], [bSC0])
                DMA("sync", out_h[t_glob * 128:(t_glob + 1) * 128, :], SC0[:, :], "d_o", [bSC0], [bout_dma])

        h_phase(0)
        for tb in range(NFTB):
            y_phase(tb)
            if tb + 1 < NFTB:
                h_phase(tb + 1)
            tail_phase(tb)

        for n in dbg:
            src = {"xn": xn_d, "y": y_d, "xm": xm_d, "xq": xq_d, "x1": x1_d}.get(n)
            if src is None:
                continue
            DMA("gpsimd", dbg_h[n], src, "d_g", [B_(n + "_d")], [B_("dbgout")])

        em.do("tensor", lambda e, c: e.nop(), extra=[(ESEM["tensor"], em.cnt["tensor"])], inc=True)
        em.do("vector", lambda e, c: e.nop(), extra=[(ESEM["vector"], em.cnt["vector"])], inc=True)
        em.do("scalar", lambda e, c: e.nop(), extra=[(ESEM["scalar"], em.cnt["scalar"])], inc=True)
        em.do("gpsimd", lambda e, c: e.nop(), extra=[(ESEM["gpsimd"], em.cnt["gpsimd"])], inc=True)
        fin_extra = [("d_o", em.dcnt["d_o"])]
        if em.dcnt["d_g"]:
            fin_extra.append(("d_g", em.dcnt["d_g"]))
        em.do("sync", lambda e, c: e.nop(), extra=fin_extra, inc=False)

        ctx = {"waited": {}, "sem": sems, "eng": {}}

        def run_queue(name, handle):
            ctx["eng"][name] = handle
            for f in em.q[name]:
                f(ctx)

        @block.sync
        def _(e):
            run_queue("sync", e)

        @block.vector
        def _(e):
            run_queue("vector", e)

        @block.scalar
        def _(e):
            run_queue("scalar", e)

        @block.gpsimd
        def _(e):
            run_queue("gpsimd", e)

        @block.tensor
        def _(e):
            run_queue("tensor", e)

    return nc


def _host_constants():
    i = np.arange(128, dtype=np.float32)
    trel = i[None, :] - i[:, None]
    trel = np.where(trel >= 0, trel, np.float32(1e9)).astype(np.float32)
    tip1 = np.broadcast_to(i[None, :] + 1.0, (128, 128)).astype(np.float32)
    trev = (127.0 - i)[:, None].astype(np.float32)
    ident = np.eye(128, dtype=np.float32)
    return trel, tip1, trev, ident


def _dropout_mask():
    import os
    import jax
    ctx = (jax.experimental.enable_x64() if os.environ.get("BASS_SIM_FP64") == "1"
           else jax.experimental.disable_x64())
    with jax.default_device(jax.devices("cpu")[0]), ctx:
        keep = jax.random.bernoulli(jax.random.key(42, impl="threefry2x32"), 1.0 - 0.2, (8, S, D))
        keep = np.asarray(keep)
    return np.where(keep, np.float16(1.25), np.float16(0.0)).astype(np.float16)


def kernel(L0_attn_x, L0_attn_ln1_w, L0_attn_ln1_b, L0_attn_wq, L0_attn_wk, L0_attn_wv,
           L0_attn_wg, L0_attn_wo, L0_attn_ln2_w, L0_attn_ln2_b, L0_attn_w_in, L0_attn_b_in,
           L0_attn_w_out, L0_attn_b_out, L0_ffn_w1, L0_ffn_b1, L0_ffn_w2, L0_ffn_b2):
    x = np.ascontiguousarray(np.asarray(L0_attn_x, dtype=np.float32))
    trel, tip1, trev, ident = _host_constants()
    mask = _dropout_mask()
    common = {
        "wq": np.ascontiguousarray(L0_attn_wq, np.float32),
        "wk": np.ascontiguousarray(L0_attn_wk, np.float32),
        "wv": np.ascontiguousarray(L0_attn_wv, np.float32),
        "wg": np.ascontiguousarray(L0_attn_wg, np.float32),
        "wo": np.ascontiguousarray(L0_attn_wo, np.float32),
        "ln1w": np.ascontiguousarray(L0_attn_ln1_w, np.float32),
        "ln1b": np.ascontiguousarray(L0_attn_ln1_b, np.float32),
        "ln2w": np.ascontiguousarray(L0_attn_ln2_w, np.float32),
        "ln2b": np.ascontiguousarray(L0_attn_ln2_b, np.float32),
        "win": np.ascontiguousarray(L0_attn_w_in, np.float32),
        "bin": np.ascontiguousarray(L0_attn_b_in, np.float32),
        "wout": np.ascontiguousarray(L0_attn_w_out, np.float32),
        "bout": np.ascontiguousarray(L0_attn_b_out, np.float32),
        "w1": np.ascontiguousarray(L0_ffn_w1, np.float32),
        "b1": np.ascontiguousarray(L0_ffn_b1, np.float32),
        "w2": np.ascontiguousarray(L0_ffn_w2, np.float32),
        "b2": np.ascontiguousarray(L0_ffn_b2, np.float32),
        "trel": trel, "tip1": tip1, "trev": trev, "ident": ident,
    }
    nc = build_nc()
    in_maps = []
    for core in range(8):
        m = dict(common)
        m["x"] = np.ascontiguousarray(x[core])
        m["mask"] = np.ascontiguousarray(mask[core])
        in_maps.append(m)
    res = run_bass_kernel_spmd(nc, in_maps, core_ids=list(range(8)))
    return np.stack([r["out"] for r in res.results], axis=0)


# version 14
# speedup vs baseline: 1.4873x; 1.1935x over previous
import contextlib
import math
import sys

import numpy as np
import ml_dtypes

import concourse.bass as bass
import concourse.mybir as mybir
from concourse.bass_utils import run_bass_kernel_spmd

F32 = mybir.dt.float32
BF16 = mybir.dt.bfloat16
F16 = mybir.dt.float16
I16 = mybir.dt.int16
I8 = mybir.dt.int8
F8E4 = mybir.dt.float8e4
M64C = 6755399441055744.0
OP = mybir.AluOpType
AF = mybir.ActivationFunctionType
BF = ml_dtypes.bfloat16
AX = mybir.AxisListType.X

S, D = 2048, 1024
NT, ND = S // 128, D // 128
NH, NPAIR = 16, 8
BLK, NBLK = 512, 4
HID = 2730
HID_T = [128] * 21 + [42]
NKH = len(HID_T)
H4 = 4096
NJ = H4 // 128
FTB, NFTB = 256, 8
EPS1, EPS2 = 1e-5, 1e-6

ENGS = ("sync", "vector", "scalar", "gpsimd", "tensor")
ESEM = {e: "E_" + e for e in ENGS}
DMACH = ("d_x", "d_w", "d_w2", "d_t", "d_b", "d_m", "d_o", "d_c", "d_q", "d_g", "d_fa", "d_fb", "d_fc", "d_fd", "d_wq")


class Buf:
    __slots__ = ("w", "r")

    def __init__(self):
        self.w = None
        self.r = []


class Em:
    def __init__(self):
        self.q = {e: [] for e in ENGS}
        self.cnt = {e: 0 for e in ENGS}
        self.dcnt = {ch: 0 for ch in DMACH}

    def _wait(self, eng, sem_name, val):
        def f(ctx, eng=eng, sem_name=sem_name, val=val):
            key = (eng, sem_name)
            if ctx["waited"].get(key, -1) >= val:
                return
            ctx["waited"][key] = val
            ctx["eng"][eng].wait_ge(ctx["sem"][sem_name], val)

        self.q[eng].append(f)

    def do(self, eng, build, ins=(), outs=(), ch=None, extra=(), inc=True):
        fr = sys._getframe(1)
        site = f"{fr.f_code.co_name}:{fr.f_lineno}"
        fr2 = fr.f_back
        if fr2 is not None:
            site += f" < {fr2.f_code.co_name}:{fr2.f_lineno}"
        _build0 = build

        def build(e, c, _b=_build0, _site=site):
            try:
                return _b(e, c)
            except Exception as ex:
                raise RuntimeError(f"emit failed at {_site}: {ex}") from ex

        waits = []
        for b in ins:
            if b.w is not None:
                waits.append(b.w)
        for b in outs:
            if b.w is not None:
                waits.append(b.w)
            waits.extend(b.r)
        waits.extend(t for t in extra if t is not None)
        seen = {}
        for s_, v_ in waits:
            if s_ in self.dcnt:
                v_ = self.dcnt[s_]
            if seen.get(s_, -1) < v_:
                seen[s_] = v_
        for s_, v_ in sorted(seen.items()):
            if s_ == ESEM[eng] and v_ > self.cnt[eng]:
                continue
            self._wait(eng, s_, v_)
        if ch is not None:
            if self.dcnt[ch] > 0:
                self._wait(eng, ch, self.dcnt[ch])
            self.dcnt[ch] += 16
            val = self.dcnt[ch]

            def f(ctx, build=build, eng=eng, ch=ch):
                build(ctx["eng"][eng], ctx).then_inc(ctx["sem"][ch], 16)

            self.q[eng].append(f)
            tok = (ch, val)
        elif inc:
            self.cnt[eng] += 1
            val = self.cnt[eng]

            def f(ctx, build=build, eng=eng, sem_name=ESEM[eng]):
                build(ctx["eng"][eng], ctx).then_inc(ctx["sem"][sem_name], 1)

            self.q[eng].append(f)
            tok = (ESEM[eng], val)
        else:
            def f(ctx, build=build, eng=eng):
                build(ctx["eng"][eng], ctx)

            self.q[eng].append(f)
            tok = None
        if tok is None and ch is None:
            tok_eff = (ESEM[eng], self.cnt[eng] + 1)
        else:
            tok_eff = tok
        if tok_eff is not None:
            for b in ins:
                b.r.append(tok_eff)
            for b in outs:
                b.w = tok_eff
                b.r = []
        return tok


def _pc(h):
    g = 1.0 - 2.0 ** (-5.0 - h)
    lng = math.log(g)
    hi = np.float32(lng)
    lo = np.float32(lng - float(hi))
    return float(hi), float(lo)


def build_nc(dbg=()):
    nc = bass.Bass("TRN2", target_bir_lowering=False, debug=False)
    em = Em()

    I = lambda n, sh, dt_: nc.dram_tensor(n, sh, dt_, kind="ExternalInput").ap()
    x_h = I("x", [S, D], F32)
    wq_h = I("wq", [D, D], F32)
    wk_h = I("wk", [D, D], F32)
    wv_h = I("wv", [D, D], F32)
    wg_h = I("wg", [D, D], F32)
    wo_h = I("wo", [D, D], F32)
    ln_h = {n: I(n, [D], F32) for n in ("ln1w", "ln1b", "ln2w", "ln2b")}
    win_h = I("win", [D, 2 * HID], F32)
    bin_h = I("bin", [2 * HID], F32)
    wout_h = I("wout", [HID, D], F32)
    bout_h = I("bout", [D], F32)
    w1_h = I("w1", [H4, D], F32)
    b1_h = I("b1", [H4], F32)
    w2_h = I("w2", [D, H4], F32)
    b2_h = I("b2", [D], F32)
    mask_h = I("mask", [S, D], F16)
    trel_h = I("trel", [128, 128], F32)
    tip1_h = I("tip1", [128, 128], F32)
    trev_h = I("trev", [128, 1], F32)
    ident_h = I("ident", [128, 128], F32)
    out_h = nc.dram_tensor("out", [S, D], F32, kind="ExternalOutput").ap()
    _dbg_shapes = {"hq": [H4, S], "ym": [S, D]}
    dbg_h = {n: nc.dram_tensor("dbg_" + n, _dbg_shapes.get(n, [S, D]), F32, kind="ExternalOutput").ap() for n in dbg}

    T = lambda n, sh, dt_: nc.dram_tensor(n, sh, dt_, kind="Internal").ap()
    xn_d = T("xn_d", [S, D], BF16)
    y_d = T("y_d", [S, D], BF16)
    xm_d = T("xm_d", [S, D], BF16)
    xq_d = T("xq_d", [S, D], BF16)
    x1_d = T("x1_d", [S, D], BF16)
    w1q_d = T("w1q_d", [H4, D], BF16)
    w1qT_d = T("w1qT_d", [D, H4], BF16)
    w2q_d = T("w2q_d", [D, H4], BF16)
    w2qT_d = T("w2qT_d", [H4, D], BF16)

    es = contextlib.ExitStack()
    with es:
        SBt = lambda n, sh, dt_: es.enter_context(nc.sbuf_tensor(n, sh, dt_))
        TFM = SBt("TFM", [128, ND * S], BF16)
        R0 = SBt("R0", [128, NT * D], BF16)
        PW = SBt("PW", [128, 22528], BF16)
        PB = SBt("PB", [128, 20480], BF16)
        WST = SBt("WST", [128, 2048], BF16)
        SC0 = SBt("SC0", [128, 1024], F32)
        SCF = SBt("SCF", [128, 1024], F32)
        SC1 = SBt("SC1", [128, 1024], BF16)
        SC2 = SBt("SC2", [128, 1024], BF16)
        SC3 = SBt("SC3", [128, 1024], BF16)
        SC4 = SBt("SC4", [128, 1024], BF16)
        SCI = SBt("SCI", [128, 1024], I16)
        SCI2 = SBt("SCI2", [128, 1024], I16)
        SCH1 = SBt("SCH1", [128, 1024], F16)
        SCH2 = SBt("SCH2", [128, 512], F16)
        SCH3 = SBt("SCH3", [128, 512], F16)
        SCH4 = SBt("SCH4", [128, 512], F16)
        SYE = SBt("SYE", [128, 1024], F16)
        QL16 = SBt("QL16", [128, 512], I16)
        GQ1 = SBt("GQ1", [128, 512], BF16)
        F8Q = SBt("F8Q", [128, 1024], F8E4)
        L8Q = SBt("L8Q", [128, 1024], I8)
        GQI = SBt("GQI", [128, 1024], I16)
        VST = SBt("VST", [128, 66], F32)
        TRASH = SBt("TRASH", [128, 1024], BF16)
        DKT = SBt("DKT", [128, NH * 128], BF16)
        GMT = SBt("GMT", [128, NPAIR * 128], BF16)
        KGV = SBt("KGV", [128, NH], F32)
        GCV = SBt("GCV", [128, NPAIR], F32)
        SBLK = SBt("SBLK", [128, NPAIR * 128], F32)
        CST = SBt("CST", [128, 128], F32)
        IDT = SBt("IDT", [128, 128], F32)
        LNV = SBt("LNV", [128, 4 * ND], F32)
        BIV = SBt("BIV", [128, 2 * NKH], F32)
        QB1 = SBt("QB1", [128, NJ], F32)
        ROWS = SBt("ROWS", [1, 1024], F32)
        ROWB = SBt("ROWB", [1, 2048], BF16)
        ONES = SBt("ONES", [1, 512], BF16)
        ps = [es.enter_context(nc.psum_tensor(f"ps{i}", [128, 512], F32)) for i in range(8)]
        sems = {n: es.enter_context(nc.semaphore(n)) for n in list(ESEM.values()) + list(DMACH)}
        block = es.enter_context(nc.Block())

        _bufs = {}

        def B_(key):
            if key not in _bufs:
                _bufs[key] = Buf()
            return _bufs[key]

        bPS = [B_(("ps", i)) for i in range(8)]
        bSC0, bSCF, bSC1, bSC2, bSC3, bSC4 = (B_(x) for x in ("SC0", "SCF", "SC1", "SC2", "SC3", "SC4"))
        bSCI, bSCI2 = B_("SCI"), B_("SCI2")
        bSH1, bSH2, bSH3, bSH4, bSYE = B_("SCH1"), B_("SCH2"), B_("SCH3"), B_("SCH4"), B_("SYE")
        bVST, bTRASH, bCST, bROWS, bROWB = B_("VST"), B_("TRASH"), B_("CST"), B_("ROWS"), B_("ROWB")
        bWST, bTFM, bPW, bPB = B_("WST"), B_("TFM"), B_("PW"), B_("PB")
        bSBLK, bIDT, bONES = B_("SBLK"), B_("IDT"), B_("ONES")
        bLNV, bBIV, bQB1 = B_("LNV"), B_("BIV"), B_("QB1")
        bDKT, bGMT, bKGV, bGCV = B_("DKT"), B_("GMT"), B_("KGV"), B_("GCV")
        bQX, bGQ1 = B_("qx"), B_("gq1")
        bR0 = [B_(("R0", t)) for t in range(NT)]
        bSLOT = [B_(("slot", s)) for s in range(2)]

        def TT(eng, out, a, b, op, ins, outs, extra=()):
            return em.do(eng, lambda e, c: e.tensor_tensor(out, a, b, op=op), ins, outs, extra=extra)

        def TS1(eng, out, a, s1, op, ins, outs, extra=()):
            return em.do(eng, lambda e, c: e.tensor_single_scalar(out, a, s1, op=op), ins, outs, extra=extra)

        def TS2(eng, out, a, s1, s2, op0, op1, ins, outs, extra=()):
            if eng == "gpsimd":
                t = em.do(eng, lambda e, c: e.tensor_single_scalar(out, a, s1, op=op0), ins, outs, extra=extra)
                return em.do(eng, lambda e, c: e.tensor_single_scalar(out, out, s2, op=op1), list(outs), list(outs))
            return em.do(eng, lambda e, c: e.tensor_scalar(out, a, s1, s2, op0=op0, op1=op1), ins, outs, extra=extra)

        def STT(out, a, s, b, op0, op1, ins, outs, extra=()):
            return em.do("vector", lambda e, c: e.scalar_tensor_tensor(out, a, s, b, op0=op0, op1=op1), ins, outs, extra=extra)

        def CP(eng, out, a, ins, outs, extra=()):
            return em.do(eng, lambda e, c: e.tensor_copy(out, a), ins, outs, extra=extra)

        def ACT(out, a, func, ins, outs, bias=0.0, scale=1.0, accum=None, extra=()):
            return em.do("scalar", lambda e, c: e.activation(out, a, func, bias=bias, scale=scale, accum_out=accum), ins, outs, extra=extra)

        def RED(out, a, op, ins, outs, extra=()):
            return em.do("vector", lambda e, c: e.tensor_reduce(out, a, axis=AX, op=op), ins, outs, extra=extra)

        def MS(eng, ap, v, ins, outs, extra=()):
            return em.do(eng, lambda e, c: e.memset(ap, v), ins, outs, extra=extra)

        def MM(po, lhsT, rhs, start, stop, ins, outs, extra=(), inc=True, tp=None):
            return em.do("tensor", lambda e, c: e.matmul(po, lhsT, rhs, start=start, stop=stop, tile_position=tp), ins, outs, extra=extra, inc=True)

        def TRANS(po, in_, ins, outs, extra=(), P=1):
            return em.do("tensor", lambda e, c: e.transpose(po, in_, IDT[0:P, 0:P]), list(ins) + [bIDT], outs, extra=extra, inc=True)

        def DMA(eng, out, in_, ch, ins, outs, extra=(), transpose=False):
            if transpose:
                return em.do(eng, lambda e, c: e.dma_start(out=out, in_=in_, transpose=True), ins, outs, ch=ch, extra=extra)
            return em.do(eng, lambda e, c: e.dma_start(out=out, in_=in_), ins, outs, ch=ch, extra=extra)

        def RCP(out, a, ins, outs, extra=()):
            return em.do("vector", lambda e, c: e.reciprocal(out, a), ins, outs, extra=extra)

        def qx(dst, src, W, fl, kind, ins, outs, n=128):
            (Fr, br), (Fg, bg), (Fc, bc), (Fl, bl), (Fn, bn), (Fx, bx), (Fb, bb) = \
                [(a[0:n, 0:W], b) for a, b in fl]
            ti, t2, pm = SCI[0:n, :W], SCI2[0:n, :W], GQI[0:n, :W]
            f8, l8 = F8Q[0:n, :W], L8Q[0:n, :W]
            V = "vector"
            if kind == "43bf":
                TS1(V, ti, src.bitcast(I16), 0x7F80, OP.bitwise_and, list(ins), [bQX])
                TS2(V, t2, ti, 121 << 7, 384, OP.max, OP.subtract, [bQX], [bQX])
                TS1(V, Fn, ti, 15616, OP.is_ge, [bQX], [bn])
                TS2(V, pm, t2, -1, 32512, OP.mult, OP.add, [bQX], [bQX])
                CP(V, f8, src, list(ins), [bQX])
            else:
                TS1(V, ti, src.bitcast(I16), 0x7C00, OP.bitwise_and, list(ins), [bQX])
                clampv = (9 << 10) if kind == "43f16" else (1 << 10)
                TS1(V, ti, ti, clampv, OP.max, [bQX], [bQX])
                TS1(V, Fn, ti, clampv + (1 << 10), OP.is_ge, [bQX], [bn])
                TS1(V, ti, ti, 3, OP.logical_shift_right, [bQX], [bQX])
                if kind == "43f16":
                    TS2(V, pm, ti, -1, (127 + 15 + 3) << 7, OP.mult, OP.add, [bQX], [bQX])
                    TS1(V, t2, ti, (127 - 15 - 3) << 7, OP.add, [bQX], [bQX])
                    CP(V, f8, src, list(ins), [bQX])
                else:
                    TS2(V, pm, ti, -1, (127 + 15 + 10) << 7, OP.mult, OP.add, [bQX], [bQX])
                    TS1(V, t2, ti, (127 - 15 - 10) << 7, OP.add, [bQX], [bQX])
                    CP(V, QL16[0:n, :W], src.bitcast(I16), list(ins), [bQX])
            TT(V, Fr, src, pm.bitcast(BF16), OP.mult, list(ins) + [bQX], [br])
            if kind == "510f16":
                CP(V, Fb, QL16[0:n, :W].bitcast(F16), [bQX], [bb])
            else:
                CP(V, l8, f8.bitcast(I8), [bQX], [bQX])
                CP(V, Fb, l8.bitcast(F8E4), [bQX], [bb])
            TT(V, Fg, Fr, Fr, OP.mult, [br], [bg])
            hi_t = 4194000.0 if kind == "510f16" else 255.9
            lo_t = 1048000.0 if kind == "510f16" else 63.9
            TS1(V, Fc, Fg, hi_t, OP.is_ge, [bg], [bc])
            TS1(V, Fl, Fg, lo_t, OP.is_le, [bg], [bl])
            TT(V, Fl, Fl, Fn, OP.mult, [bl, bn], [bl])
            STT(Fx, Fc, -0.5, Fl, OP.mult, OP.add, [bc, bl], [bx])
            STT(Fg, Fx, 1.0, Fr, OP.add, OP.mult, [bx, br], [bg])
            TS2(V, Fg, Fg, VST[0:n, 64:65], 0.0, OP.mult, OP.add, [bg, bVST], [bg])
            TS2(V, Fr, Fg, M64C, M64C, OP.add, OP.subtract, [bg], [br])
            STT(Fx, Fl, -0.5, Fc, OP.mult, OP.add, [bl, bc], [bx])
            TT(V, Fn, Fr, t2.bitcast(BF16), OP.mult, [br, bQX], [bn])
            STT(Fn, Fx, 1.0, Fn, OP.add, OP.mult, [bx, bn], [bn])
            TS1(V, pm, t2, 6720, OP.add, [bQX], [bQX])
            TT(V, Fl, Fb, pm.bitcast(BF16), OP.add, [bb, bQX], [bl])
            TT(V, Fl, Fl, pm.bitcast(BF16), OP.subtract, [bl, bQX], [bl])
            TT(V, Fb, Fn, Fb, OP.add, [bn, bb], [bb])
            return TT(V, dst, Fb, Fl, OP.subtract, [bb, bl], list(outs))

        MS("vector", VST[:, 64:65], 2.0 ** -41, [], [bVST])
        TS2("vector", VST[:, 64:65], VST[:, 64:65], -1.0, 1.0, OP.mult, OP.add, [bVST], [bVST])

        fl_wide = [(WST[:, 0:1024], bWST), (WST[:, 1024:2048], bWST), (TRASH, bTRASH),
                   (SC0, bSC0), (SCF, bSCF), (PB[:, 0:1024], bPB), (PB[:, 1024:2048], bPB)]
        fl_ffn = [(GQ1, bGQ1), (TRASH, bTRASH), (SC1, bSC1), (SC2, bSC2),
                  (SC3, bSC3), (SC4, bSC4), (SCH2, bSH2)]
        fl_tail = [(GQ1, bGQ1), (TRASH, bTRASH), (SC1, bSC1), (SC2, bSC2),
                   (SC3, bSC3), (SC4, bSC4), (SCH3, bSH3)]
        fl_wq = [(PB[:, 2048 + i * 1024:3072 + i * 1024], bPB) for i in range(7)]

        def qpair_h(dst, src, W, ins, outs):
            ti, t2, pm = SCI[:, :W], SCI2[:, :W], GQI[:, :W]
            f8, l8 = F8Q[:, :W], L8Q[:, :W]
            T1, bT1 = PB[:, 18432:18432 + W], bPB
            T2, bT2 = PB[:, 19456:19456 + W], bPB
            T3, bT3 = TRASH[:, :W], bTRASH
            Tc, bTc = SC1[:, :W], bSC1
            Tn, bTn = SC2[:, :W], bSC2
            Tx, bTx = SC3[:, :W], bSC3
            Ty, bTy = SC4[:, :W], bSC4
            V = "vector"
            TS1(V, ti, src.bitcast(I16), 0x7C00, OP.bitwise_and, list(ins), [bQX])
            TS1(V, ti, ti, 1 << 10, OP.max, [bQX], [bQX])
            TS1(V, Tn, ti, 2 << 10, OP.is_ge, [bQX], [bTn])
            TS1(V, ti, ti, 3, OP.logical_shift_right, [bQX], [bQX])
            TS2(V, pm, ti, -1, 19456, OP.mult, OP.add, [bQX], [bQX])
            TS1(V, t2, ti, 13056, OP.add, [bQX], [bQX])
            TT(V, T1, src, pm.bitcast(BF16), OP.mult, list(ins) + [bQX], [bT1])
            TS1(V, Tc, T1, 2047.9, OP.is_ge, [bT1], [bTc])
            TS1(V, Tx, T1, 1023.9, OP.is_ge, [bT1], [bTx])
            TT(V, Tc, Tc, Tn, OP.mult, [bTc, bTn], [bTc])
            TS2(V, Tx, Tx, -1.0, 1.0, OP.mult, OP.add, [bTx], [bTx])
            TT(V, Tx, Tx, Tn, OP.mult, [bTx, bTn], [bTx])
            STT(Ty, Tc, -0.5, Tx, OP.mult, OP.add, [bTc, bTx], [bTy])
            STT(T1, Ty, 1.0, T1, OP.add, OP.mult, [bTy, bT1], [bT1])
            TS2(V, T1, T1, VST[:, 64:65], 0.0, OP.mult, OP.add, [bT1, bVST], [bT1])
            TS2(V, T1, T1, M64C, M64C, OP.add, OP.subtract, [bT1], [bT1])
            TT(V, T1, T1, t2.bitcast(BF16), OP.mult, [bT1, bQX], [bT1])
            STT(Tx, Tx, -0.5, Tc, OP.mult, OP.add, [bTx, bTc], [bTx])
            STT(T1, Tx, 1.0, T1, OP.add, OP.mult, [bTx, bT1], [bT1])
            TS2(V, t2, t2, 896, 15104, OP.add, OP.max, [bQX], [bQX])
            TS2(V, pm, t2, -1, 32512, OP.mult, OP.add, [bQX], [bQX])
            TT(V, T1, T1, pm.bitcast(BF16), OP.mult, [bT1, bQX], [bT1])
            STT(T1, Ty, 1.0, T1, OP.add, OP.mult, [bTy, bT1], [bT1])
            TS2(V, T1, T1, VST[:, 64:65], 0.0, OP.mult, OP.add, [bT1, bVST], [bT1])
            TS2(V, T1, T1, M64C, M64C, OP.add, OP.subtract, [bT1], [bT1])
            TT(V, T1, T1, t2.bitcast(BF16), OP.mult, [bT1, bQX], [bT1])
            STT(T1, Tx, 1.0, T1, OP.add, OP.mult, [bTx, bT1], [bT1])
            CP(V, f8, src, list(ins), [bQX])
            CP(V, l8, f8.bitcast(I8), [bQX], [bQX])
            CP(V, T2, l8.bitcast(F8E4), [bQX], [bT2])
            TS1(V, pm, t2, 6720, OP.add, [bQX], [bQX])
            TT(V, T3, T2, pm.bitcast(BF16), OP.add, [bT2, bQX], [bT3])
            TT(V, T3, T3, pm.bitcast(BF16), OP.subtract, [bT3, bQX], [bT3])
            TT(V, T2, T1, T2, OP.add, [bT1, bT2], [bT2])
            return TT(V, dst, T2, T3, OP.subtract, [bT2, bT3], list(outs))

        def qy510(dst, src, W, ins, outs):
            ti, t2, pm = SCI[:, :W], SCI2[:, :W], GQI[:, :W]
            T1, bT1 = TRASH[:, :W], bTRASH
            T2, bT2 = SCH4[:, :W], bSH4
            T3, bT3 = GQ1[:, :W], bGQ1
            Tc, bTc = SC1[:, :W], bSC1
            Tn, bTn = SC2[:, :W], bSC2
            Tx, bTx = SC3[:, :W], bSC3
            Ty, bTy = SC4[:, :W], bSC4
            V = "vector"
            TS1(V, ti, src.bitcast(I16), 0x7C00, OP.bitwise_and, list(ins), [bQX])
            TS1(V, ti, ti, 1 << 10, OP.max, [bQX], [bQX])
            TS1(V, Tn, ti, 2 << 10, OP.is_ge, [bQX], [bTn])
            TS1(V, ti, ti, 3, OP.logical_shift_right, [bQX], [bQX])
            TS2(V, pm, ti, -1, 19456, OP.mult, OP.add, [bQX], [bQX])
            TS1(V, t2, ti, 13056, OP.add, [bQX], [bQX])
            TT(V, T1, src, pm.bitcast(BF16), OP.mult, list(ins) + [bQX], [bT1])
            TT(V, Tc, T1, T1, OP.mult, [bT1], [bTc])
            TS1(V, Tx, Tc, 1048000.0, OP.is_ge, [bTc], [bTx])
            TS1(V, Tc, Tc, 4194000.0, OP.is_ge, [bTc], [bTc])
            TT(V, Tc, Tc, Tn, OP.mult, [bTc, bTn], [bTc])
            TS2(V, Tx, Tx, -1.0, 1.0, OP.mult, OP.add, [bTx], [bTx])
            TT(V, Tx, Tx, Tn, OP.mult, [bTx, bTn], [bTx])
            STT(Ty, Tc, -0.5, Tx, OP.mult, OP.add, [bTc, bTx], [bTy])
            STT(T1, Ty, 1.0, T1, OP.add, OP.mult, [bTy, bT1], [bT1])
            TS2(V, T1, T1, VST[:, 64:65], 0.0, OP.mult, OP.add, [bT1, bVST], [bT1])
            TS2(V, T1, T1, M64C, M64C, OP.add, OP.subtract, [bT1], [bT1])
            TT(V, T1, T1, t2.bitcast(BF16), OP.mult, [bT1, bQX], [bT1])
            STT(Tx, Tx, -0.5, Tc, OP.mult, OP.add, [bTx, bTc], [bTx])
            STT(T1, Tx, 1.0, T1, OP.add, OP.mult, [bTx, bT1], [bT1])
            CP(V, QL16[:, :W], src.bitcast(I16), list(ins), [bQX])
            CP(V, T2, QL16[:, :W].bitcast(F16), [bQX], [bT2])
            TS1(V, pm, t2, 6720, OP.add, [bQX], [bQX])
            TT(V, T3, T2, pm.bitcast(BF16), OP.add, [bT2, bQX], [bT3])
            TT(V, T3, T3, pm.bitcast(BF16), OP.subtract, [bT3, bQX], [bT3])
            TT(V, T2, T1, T2, OP.add, [bT1, bT2], [bT2])
            return TT(V, dst, T2, T3, OP.subtract, [bT2, bT3], list(outs))

        DMA("sync", IDT[:, :], ident_h, "d_c", [], [bIDT])
        MS("vector", ONES[:, :], 1.0, [], [bONES])
        DMA("sync", CST[:, :], trel_h, "d_c", [], [bCST])
        for h in range(NH):
            hi, lo = _pc(h)
            TS1("vector", SC0[:, :128], CST[:, :], hi, OP.mult, [bCST], [bSC0])
            TS1("vector", SCF[:, :128], CST[:, :], lo, OP.mult, [bCST], [bSCF])
            TT("vector", SC0[:, :128], SC0[:, :128], SCF[:, :128], OP.add, [bSC0, bSCF], [bSC0])
            ACT(DKT[:, h * 128:(h + 1) * 128], SC0[:, :128], AF.Exp, [bSC0], [bDKT])
        DMA("sync", CST[:, :], tip1_h, "d_c", [bCST], [bCST])
        for p in range(NPAIR):
            for half, h in ((0, 2 * p), (1, 2 * p + 1)):
                hi, lo = _pc(h)
                sl = slice(64 * half, 64 * half + 64)
                TS1("vector", SC0[sl, :128], CST[sl, :], hi, OP.mult, [bCST], [bSC0])
                TS1("vector", SCF[sl, :128], CST[sl, :], lo, OP.mult, [bCST], [bSCF])
            TT("vector", SC0[:, :128], SC0[:, :128], SCF[:, :128], OP.add, [bSC0, bSCF], [bSC0])
            ACT(GMT[:, p * 128:(p + 1) * 128], SC0[:, :128], AF.Exp, [bSC0], [bGMT])
        DMA("sync", CST[:, 0:1], trev_h, "d_c", [bCST], [bCST])
        for h in range(NH):
            hi, lo = _pc(h)
            TS1("vector", SC0[:, 0:1], CST[:, 0:1], hi, OP.mult, [bCST], [bSC0])
            TS1("vector", SCF[:, 0:1], CST[:, 0:1], lo, OP.mult, [bCST], [bSCF])
            TT("vector", SC0[:, 0:1], SC0[:, 0:1], SCF[:, 0:1], OP.add, [bSC0, bSCF], [bSC0])
            ACT(KGV[:, h:h + 1], SC0[:, 0:1], AF.Exp, [bSC0], [bKGV])
        for p in range(NPAIR):
            for half, h in ((0, 2 * p), (1, 2 * p + 1)):
                hi, lo = _pc(h)
                sl = slice(64 * half, 64 * half + 64)
                MS("vector", SC0[sl, 0:1], float(np.float32(hi)) * 128.0, [], [bSC0])
                MS("vector", SCF[sl, 0:1], float(np.float32(lo)) * 128.0, [], [bSCF])
            TT("vector", SC0[:, 0:1], SC0[:, 0:1], SCF[:, 0:1], OP.add, [bSC0, bSCF], [bSC0])
            ACT(GCV[:, p:p + 1], SC0[:, 0:1], AF.Exp, [bSC0], [bGCV])
        MS("vector", SBLK[:, :], 0.0, [], [bSBLK])
        bSBLK16 = B_("sblk16")
        MS("vector", PW[:, 16384:17408], 0.0, [], [bSBLK16])
        for v, name in enumerate(("ln1w", "ln1b", "ln2w", "ln2b")):
            DMA("sync", ROWS[0:1, 0:D], ln_h[name].rearrange("(o d) -> o d", o=1), "d_c", [bROWS], [bROWS])
            for cc in range(ND):
                TRANS(ps[7][:, 0:1], ROWS[0:1, cc * 128:cc * 128 + 128], [bROWS], [bPS[7]])
                ACT(LNV[:, v * ND + cc:v * ND + cc + 1], ps[7][:, 0:1], AF.Copy, [bPS[7]], [bLNV])
        LN1W, LN1B, LN2W, LN2B = 0, ND, 2 * ND, 3 * ND
        for half in range(2):
            for k in range(NKH):
                w = HID_T[k]
                DMA("sync", ROWS[0:1, 0:w], bin_h[half * HID + k * 128:half * HID + k * 128 + w].rearrange("(o d) -> o d", o=1),
                    "d_c", [bROWS], [bROWS])
                TRANS(ps[7][0:w, 0:1], ROWS[0:1, 0:w], [bROWS], [bPS[7]])
                ACT(BIV[0:w, half * NKH + k:half * NKH + k + 1], ps[7][0:w, 0:1], AF.Copy, [bPS[7]], [bBIV])
        DMA("sync", ROWS[0:1, 0:D], bout_h.rearrange("(o d) -> o d", o=1), "d_c", [bROWS], [bROWS])
        CP("vector", ROWB[0:1, 0:D], ROWS[0:1, 0:D], [bROWS], [bROWB])
        for cchunk in range(4):
            DMA("sync", ROWS[0:1, 0:1024], b1_h[cchunk * 1024:(cchunk + 1) * 1024].rearrange("(o d) -> o d", o=1),
                "d_c", [bROWS], [bROWS])
            CP("vector", SC4[0:1, 0:1024], ROWS[0:1, 0:1024], [bROWS], [bSC4])
            qx(SC4[0:1, 0:1024], SC4[0:1, 0:1024], 1024, fl_wide, "43bf", [bSC4], [bSC4], n=1)
            CP("vector", SCF[0:1, 0:1024], SC4[0:1, 0:1024], [bSC4], [bSCF])
            for k in range(8):
                j = cchunk * 8 + k
                TRANS(ps[7][:, 0:1], SCF[0:1, k * 128:(k + 1) * 128], [bSCF], [bPS[7]])
                ACT(QB1[:, j:j + 1], ps[7][:, 0:1], AF.Copy, [bPS[7]], [bQB1])
        DMA("sync", ROWS[0:1, 0:D], b2_h.rearrange("(o d) -> o d", o=1), "d_c", [bROWS], [bROWS])
        CP("vector", SC4[0:1, 0:1024], ROWS[0:1, 0:1024], [bROWS], [bSC4])
        qx(ROWB[0:1, 1024:2048], SC4[0:1, 0:1024], 1024, fl_wide, "43bf", [bSC4], [bROWB], n=1)

        def stats_batch(tiles_fn, n, eps, cmu=0, ce2=16, ct1=32, ct2=48):
            for t in range(n):
                ap, bufs = tiles_fn(t)
                ACT(TRASH[:, :D], ap, AF.Identity, bufs, [bTRASH, bVST], accum=VST[:, cmu + t:cmu + t + 1])
                ACT(TRASH[:, :D], ap, AF.Square, bufs, [bTRASH, bVST], accum=VST[:, ce2 + t:ce2 + t + 1])
            TS1("vector", VST[:, cmu:cmu + n], VST[:, cmu:cmu + n], 1.0 / D, OP.mult, [bVST], [bVST])
            TS1("vector", VST[:, ce2:ce2 + n], VST[:, ce2:ce2 + n], 1.0 / D, OP.mult, [bVST], [bVST])
            TT("vector", VST[:, ct1:ct1 + n], VST[:, cmu:cmu + n], VST[:, cmu:cmu + n], OP.mult, [bVST], [bVST])
            TT("vector", VST[:, ce2:ce2 + n], VST[:, ce2:ce2 + n], VST[:, ct1:ct1 + n], OP.subtract, [bVST], [bVST])
            TS1("vector", VST[:, ce2:ce2 + n], VST[:, ce2:ce2 + n], eps, OP.add, [bVST], [bVST])
            ACT(VST[:, ct2:ct2 + n], VST[:, ce2:ce2 + n], AF.Sqrt, [bVST], [bVST])
            RCP(VST[:, ce2:ce2 + n], VST[:, ct2:ct2 + n], [bVST], [bVST])

        bxn_d, by_d, bxm_d, bxq_d, bx1_d = (B_(x) for x in ("xn_d", "y_d", "xm_d", "xq_d", "x1_d"))

        def ln_stats(src_ap, W, src_bufs, mu_sl, rs_sl, eps):
            ACT(TRASH[:, :W], src_ap, AF.Identity, src_bufs, [bTRASH, bVST], accum=VST[:, mu_sl:mu_sl + 1])
            ACT(TRASH[:, :W], src_ap, AF.Square, src_bufs, [bTRASH, bVST], accum=VST[:, rs_sl:rs_sl + 1])
            TS1("vector", VST[:, mu_sl:mu_sl + 1], VST[:, mu_sl:mu_sl + 1], 1.0 / W, OP.mult, [bVST], [bVST])
            TS1("vector", VST[:, rs_sl:rs_sl + 1], VST[:, rs_sl:rs_sl + 1], 1.0 / W, OP.mult, [bVST], [bVST])
            TT("vector", VST[:, 60:61], VST[:, mu_sl:mu_sl + 1], VST[:, mu_sl:mu_sl + 1], OP.mult, [bVST], [bVST])
            TT("vector", VST[:, rs_sl:rs_sl + 1], VST[:, rs_sl:rs_sl + 1], VST[:, 60:61], OP.subtract, [bVST], [bVST])
            TS1("vector", VST[:, rs_sl:rs_sl + 1], VST[:, rs_sl:rs_sl + 1], eps, OP.add, [bVST], [bVST])
            ACT(VST[:, 61:62], VST[:, rs_sl:rs_sl + 1], AF.Sqrt, [bVST], [bVST])
            RCP(VST[:, rs_sl:rs_sl + 1], VST[:, 61:62], [bVST], [bVST])

        for t in range(NT):
            DMA("gpsimd", R0[:, t * D:(t + 1) * D], x_h[t * 128:(t + 1) * 128, :], "d_x",
                [bR0[t]], [bR0[t]])
        stats_batch(lambda t: (R0[:, t * D:(t + 1) * D], [bR0[t]]), NT, EPS1)
        for t in range(NT):
            TS2("vector", R0[:, t * D:(t + 1) * D], R0[:, t * D:(t + 1) * D],
                VST[:, t:t + 1], VST[:, 16 + t:17 + t], OP.subtract, OP.mult, [bR0[t], bVST], [bR0[t]])
        stats_batch(lambda t: (R0[:, t * D:(t + 1) * D], [bR0[t]]), NT, EPS1)
        for t in range(NT):
            dst, bdst = (SC2, bSC2) if t % 2 == 0 else (SC3, bSC3)
            TS2("vector", dst[:, :], R0[:, t * D:(t + 1) * D], VST[:, t:t + 1], VST[:, 16 + t:17 + t],
                OP.subtract, OP.mult, [bR0[t], bVST], [bdst])
            DMA("sync", xn_d[t * 128:(t + 1) * 128, :], dst[:, :], "d_b", [bdst], [bxn_d])

        for cc in range(ND):
            DMA("sync", TFM[:, cc * S:(cc + 1) * S], xn_d[:, cc * 128:(cc + 1) * 128], "d_t",
                [bxn_d, bTFM], [bTFM], transpose=True)
            TS2("vector", TFM[:, cc * S:(cc + 1) * S], TFM[:, cc * S:(cc + 1) * S],
                LNV[:, LN1W + cc:LN1W + cc + 1], LNV[:, LN1B + cc:LN1B + cc + 1],
                OP.mult, OP.add, [bTFM, bLNV], [bTFM])

        bw1q_d, bw2q_d = B_("w1q_d"), B_("w2q_d")
        bWQ = [B_("wq0"), B_("wq1")]

        def qw43f32(dst, src, W, ins, outs):
            ti, t2, pm = SCI[:, :W], SCI2[:, :W], GQI[:, :W]
            f8, l8 = F8Q[:, :W], L8Q[:, :W]
            T1, bT1 = SC1[:, :W], bSC1
            T2, bT2 = SC2[:, :W], bSC2
            T3, bT3 = SC3[:, :W], bSC3
            V = "vector"
            hi16 = src.bitcast(I16).rearrange("p (w two) -> p two w", two=2)[:, 1:2, :].rearrange("p o w -> p (o w)")
            TS1(V, ti, hi16, 0x7F80, OP.bitwise_and, list(ins), [bQX])
            TS2(V, t2, ti, 121 << 7, 384, OP.max, OP.subtract, [bQX], [bQX])
            TS2(V, pm, t2, -1, 32512, OP.mult, OP.add, [bQX], [bQX])
            TT(V, T1, src, pm.bitcast(BF16), OP.mult, list(ins) + [bQX], [bT1])
            TS2(V, T1, T1, VST[:, 64:65], 0.0, OP.mult, OP.add, [bT1, bVST], [bT1])
            TS2(V, T1, T1, M64C, M64C, OP.add, OP.subtract, [bT1], [bT1])
            TT(V, T1, T1, t2.bitcast(BF16), OP.mult, [bT1, bQX], [bT1])
            CP(V, f8, src, list(ins), [bQX])
            CP(V, l8, f8.bitcast(I8), [bQX], [bQX])
            CP(V, T2, l8.bitcast(F8E4), [bQX], [bT2])
            TS1(V, pm, t2, 6720, OP.add, [bQX], [bQX])
            TT(V, T3, T2, pm.bitcast(BF16), OP.add, [bT2, bQX], [bT3])
            TT(V, T3, T3, pm.bitcast(BF16), OP.subtract, [bT3, bQX], [bT3])
            TT(V, T2, T1, T2, OP.add, [bT1, bT2], [bT2])
            return TT(V, dst, T2, T3, OP.subtract, [bT2, bT3], list(outs))

        def wq_src_dst(j):
            if j < 32:
                return (w1_h[j * 128:(j + 1) * 128, :],
                        w1q_d[j * 128:(j + 1) * 128, :], bw1q_d)
            j2 = j - 32
            return (w2_h[(j2 // 4) * 128:(j2 // 4 + 1) * 128, (j2 % 4) * 1024:(j2 % 4) * 1024 + 1024],
                    w2q_d[(j2 // 4) * 128:(j2 // 4 + 1) * 128, (j2 % 4) * 1024:(j2 % 4) * 1024 + 1024], bw2q_d)

        def emit_wq_load(j, extra_outs=()):
            src, _, _ = wq_src_dst(j)
            slab, bs = (SC0, bSC0) if j % 2 == 0 else (SCF, bSCF)
            DMA("sync", slab[:, :], src, "d_wq", [bs], [bs] + list(extra_outs))

        def emit_wq_quant(j):
            _, dst, bqd = wq_src_dst(j)
            slab, bs = (SC0, bSC0) if j % 2 == 0 else (SCF, bSCF)
            qw43f32(SC4[:, :], slab[:, :], 1024, [bs], [bSC4])
            DMA("sync", dst, SC4[:, :], "d_wq", [bSC4, bs], [bqd])

        def load_wslot(s, wh):
            DMA("gpsimd", PW[:, s * 8192:(s + 1) * 8192].rearrange("p (kd n) -> p kd n", kd=ND),
                wh.rearrange("(kd p) n -> p kd n", p=128), "d_w", [bSLOT[s]], [bSLOT[s]])

        def wslot_tile(s, k, m):
            return PW[:, s * 8192 + k * D + m * 128:s * 8192 + k * D + (m + 1) * 128]

        def wslot_mov(s, k, n):
            return PW[:, s * 8192 + k * D + n * 512:s * 8192 + k * D + n * 512 + 512]

        def xnT(k, lo, w):
            return TFM[:, k * S + lo:k * S + lo + w]

        QBo, KBo, KGo, VBo, GBo = 0, 4096, 8192, 12288, 16384
        bQB, bKB, bKG, bVB, bGB = B_("qb"), B_("kb"), B_("kg"), B_("vb"), B_("gb")

        for Bi in range(NBLK):
            lo = Bi * BLK
            load_wslot(0, wq_h)
            for m in range(ND):
                pb_ = bPS[m % 2]
                psn = ps[m % 2]
                for kk in range(ND):
                    MM(psn[:, :], wslot_tile(0, kk, m), xnT(kk, lo, BLK),
                       start=(kk == 0), stop=(kk == ND - 1),
                       ins=[bSLOT[0], bTFM], outs=[pb_], inc=(kk == ND - 1))
                ACT(PB[:, QBo + m * BLK:QBo + (m + 1) * BLK], psn[:, :], AF.Copy,
                    [pb_], [bQB], scale=0.125)
            load_wslot(1, wk_h)
            for m in range(ND):
                pb_ = bPS[2 + m % 2]
                psn = ps[2 + m % 2]
                for kk in range(ND):
                    MM(psn[:, :], wslot_tile(1, kk, m), xnT(kk, lo, BLK),
                       start=(kk == 0), stop=(kk == ND - 1),
                       ins=[bSLOT[1], bTFM], outs=[pb_], inc=(kk == ND - 1))
                ACT(PB[:, KBo + m * BLK:KBo + (m + 1) * BLK], psn[:, :], AF.Copy, [pb_], [bKB])
            for mt in range(4):
                for nn in range(2):
                    pb_ = bPS[(mt * 2 + nn) % 2]
                    psn = ps[(mt * 2 + nn) % 2]
                    for kk in range(ND):
                        MM(psn[:, :], xnT(kk, lo + mt * 128, 128), wslot_mov(1, kk, nn),
                           start=(kk == 0), stop=(kk == ND - 1),
                           ins=[bSLOT[1], bTFM], outs=[pb_], inc=(kk == ND - 1))
                    for hh in range(8):
                        h = nn * 8 + hh
                        TS1("vector",
                            PB[:, KGo + mt * 1024 + nn * 512 + hh * 64:KGo + mt * 1024 + nn * 512 + hh * 64 + 64],
                            psn[:, hh * 64:hh * 64 + 64], KGV[:, h:h + 1], OP.mult,
                            [pb_, bKGV], [bKG])
            load_wslot(0, wv_h)
            for mt in range(4):
                for nn in range(2):
                    pb_ = bPS[2 + (mt * 2 + nn) % 2]
                    psn = ps[2 + (mt * 2 + nn) % 2]
                    for kk in range(ND):
                        MM(psn[:, :], xnT(kk, lo + mt * 128, 128), wslot_mov(0, kk, nn),
                           start=(kk == 0), stop=(kk == ND - 1),
                           ins=[bSLOT[0], bTFM], outs=[pb_], inc=(kk == ND - 1))
                    ACT(PB[:, VBo + mt * 1024 + nn * 512:VBo + mt * 1024 + nn * 512 + 512],
                        psn[:, :], AF.Copy, [pb_], [bVB])
            load_wslot(1, wg_h)
            for mt in range(4):
                for nn in range(2):
                    pb_ = bPS[(mt * 2 + nn) % 2]
                    psn = ps[(mt * 2 + nn) % 2]
                    for kk in range(ND):
                        MM(psn[:, :], xnT(kk, lo + mt * 128, 128), wslot_mov(1, kk, nn),
                           start=(kk == 0), stop=(kk == ND - 1),
                           ins=[bSLOT[1], bTFM], outs=[pb_], inc=(kk == ND - 1))
                    ACT(PB[:, GBo + mt * 1024 + nn * 512:GBo + mt * 1024 + nn * 512 + 512],
                        psn[:, :], AF.Silu, [pb_], [bGB])

            for cc4 in range(4):
                g = Bi * 4 + cc4
                clo = cc4 * 128
                for p in range(NPAIR):
                    h1, h2 = 2 * p, 2 * p + 1
                    MM(ps[4][:, 0:128], PB[0:64, KBo + p * BLK + clo:KBo + p * BLK + clo + 128],
                       PB[0:64, QBo + p * BLK + clo:QBo + p * BLK + clo + 128],
                       start=True, stop=True, ins=[bKB, bQB], outs=[bPS[4]], inc=True)
                    MM(ps[7][:, 0:128], PB[64:128, KBo + p * BLK + clo:KBo + p * BLK + clo + 128],
                       PB[64:128, QBo + p * BLK + clo:QBo + p * BLK + clo + 128],
                       start=True, stop=True, ins=[bKB, bQB], outs=[bPS[7]], inc=True)
                    TT("vector", SC1[:, 0:128], ps[4][:, 0:128], DKT[:, h1 * 128:(h1 + 1) * 128],
                       OP.mult, [bPS[4], bDKT], [bSC1])
                    TT("vector", SC1[:, 128:256], ps[7][:, 0:128], DKT[:, h2 * 128:(h2 + 1) * 128],
                       OP.mult, [bPS[7], bDKT], [bSC1])
                    TT("vector", SC2[:, 0:128], PB[:, QBo + p * BLK + clo:QBo + p * BLK + clo + 128],
                       GMT[:, p * 128:(p + 1) * 128], OP.mult, [bQB, bGMT], [bSC2])
                    MM(ps[5][:, 0:64], SC1[:, 0:128],
                       PB[:, VBo + cc4 * 1024 + h1 * 64:VBo + cc4 * 1024 + h1 * 64 + 64],
                       start=True, stop=False, ins=[bSC1, bVB], outs=[bPS[5]], inc=False)
                    MM(ps[5][:, 64:128], SC1[:, 128:256],
                       PB[:, VBo + cc4 * 1024 + h2 * 64:VBo + cc4 * 1024 + h2 * 64 + 64],
                       start=False, stop=(g == 0), ins=[bSC1, bVB], outs=[bPS[5]], inc=(g == 0))
                    if g > 0:
                        MM(ps[5][:, 0:128], SC2[:, 0:128], PW[:, 16384 + p * 128:16384 + (p + 1) * 128],
                           start=False, stop=True, ins=[bSC2, bSBLK16], outs=[bPS[5]], inc=True)
                    MM(ps[6][:, 0:128], PB[:, KGo + cc4 * 1024 + p * 128:KGo + cc4 * 1024 + (p + 1) * 128],
                       PB[:, VBo + cc4 * 1024 + p * 128:VBo + cc4 * 1024 + (p + 1) * 128],
                       start=True, stop=True, ins=[bKG, bVB], outs=[bPS[6]], inc=True)
                    ACT(SC3[:, p * 128:(p + 1) * 128], ps[5][:, 0:128], AF.Copy, [bPS[5]], [bSC3])
                    STT(SBLK[0:64, p * 128:p * 128 + 64], SBLK[0:64, p * 128:p * 128 + 64],
                        GCV[0:64, p:p + 1], ps[6][0:64, 0:64], OP.mult, OP.add,
                        [bSBLK, bPS[6], bGCV], [bSBLK])
                    STT(SBLK[64:128, p * 128 + 64:p * 128 + 128], SBLK[64:128, p * 128 + 64:p * 128 + 128],
                        GCV[64:128, p:p + 1], ps[6][64:128, 64:128], OP.mult, OP.add,
                        [bSBLK, bPS[6], bGCV], [bSBLK])
                    CP("vector", PW[:, 16384 + p * 128:16384 + (p + 1) * 128],
                       SBLK[:, p * 128:(p + 1) * 128], [bSBLK], [bSBLK16])
                ACT(SC0[:, :], SC3[:, :], AF.Square, [bSC3], [bSC0])
                RED(VST[:, 0:16], SC3[:, :].rearrange("p (h d) -> p h d", h=16), OP.add, [bSC3], [bVST])
                RED(VST[:, 16:32], SC0[:, :].rearrange("p (h d) -> p h d", h=16), OP.add, [bSC0], [bVST])
                TS1("vector", VST[:, 0:16], VST[:, 0:16], 1.0 / 64, OP.mult, [bVST], [bVST])
                TS1("vector", VST[:, 16:32], VST[:, 16:32], 1.0 / 64, OP.mult, [bVST], [bVST])
                TT("vector", VST[:, 32:48], VST[:, 0:16], VST[:, 0:16], OP.mult, [bVST], [bVST])
                TT("vector", VST[:, 16:32], VST[:, 16:32], VST[:, 32:48], OP.subtract, [bVST], [bVST])
                TS1("vector", VST[:, 16:32], VST[:, 16:32], EPS2, OP.add, [bVST], [bVST])
                ACT(VST[:, 32:48], VST[:, 16:32], AF.Sqrt, [bVST], [bVST])
                RCP(VST[:, 16:32], VST[:, 32:48], [bVST], [bVST])
                for h in range(NH):
                    TS2("vector", SC3[:, h * 64:(h + 1) * 64], SC3[:, h * 64:(h + 1) * 64],
                        VST[:, h:h + 1], VST[:, 16 + h:17 + h], OP.subtract, OP.mult, [bSC3, bVST], [bSC3])
                TT("vector", SC3[:, :], SC3[:, :], PB[:, GBo + cc4 * 1024:GBo + (cc4 + 1) * 1024],
                   OP.mult, [bSC3, bGB], [bSC3])
                DMA("sync", y_d[g * 128:(g + 1) * 128, :], SC3[:, :], "d_b", [bSC3], [by_d])

        for cc in range(ND):
            DMA("sync", TFM[:, cc * S:(cc + 1) * S], y_d[:, cc * 128:(cc + 1) * 128], "d_t",
                [by_d, bTFM], [bTFM], transpose=True)
        load_wslot(0, wo_h)
        for mt in range(NT):
            for nn in range(2):
                pb_ = bPS[(mt * 2 + nn) % 4]
                psn = ps[(mt * 2 + nn) % 4]
                for kk in range(ND):
                    MM(psn[:, :], xnT(kk, mt * 128, 128), wslot_mov(0, kk, nn),
                       start=(kk == 0), stop=(kk == ND - 1),
                       ins=[bSLOT[0], bTFM], outs=[pb_], inc=(kk == ND - 1))
                TT("vector", R0[:, mt * D + nn * 512:mt * D + nn * 512 + 512],
                   psn[:, :], R0[:, mt * D + nn * 512:mt * D + nn * 512 + 512],
                   OP.add, [pb_, bR0[mt]], [bR0[mt]])

        stats_batch(lambda t: (R0[:, t * D:(t + 1) * D], [bR0[t]]), NT, EPS1)
        for t in range(NT):
            dst, bdst = (SC2, bSC2) if t % 2 == 0 else (SC3, bSC3)
            TS2("vector", dst[:, :], R0[:, t * D:(t + 1) * D], VST[:, t:t + 1], VST[:, 16 + t:17 + t],
                OP.subtract, OP.mult, [bR0[t], bVST], [bdst])
            DMA("sync", xm_d[t * 128:(t + 1) * 128, :], dst[:, :], "d_b", [bdst], [bxm_d])
        for cc in range(ND):
            DMA("sync", TFM[:, cc * S:(cc + 1) * S], xm_d[:, cc * 128:(cc + 1) * 128], "d_t",
                [bxm_d, bTFM], [bTFM], transpose=True)
            TS2("vector", TFM[:, cc * S:(cc + 1) * S], TFM[:, cc * S:(cc + 1) * S],
                LNV[:, LN2W + cc:LN2W + cc + 1], LNV[:, LN2B + cc:LN2B + cc + 1],
                OP.mult, OP.add, [bTFM, bLNV], [bTFM])

        for k in range(NKH):
            w = HID_T[k]
            if k == 0:
                DMA("gpsimd", PW[0:w, k * D:(k + 1) * D], wout_h[k * 128:k * 128 + w, :],
                    "d_w2", [bPW], [bPW, bSLOT[0], bSLOT[1], bSBLK16])
            else:
                DMA("gpsimd", PW[0:w, k * D:(k + 1) * D], wout_h[k * 128:k * 128 + w, :],
                    "d_w2", [bPW], [bPW])
        emit_wq_load(0)
        for tb in range(4):
            tlo = tb * 512
            for k in range(NKH):
                wqj = tb * NKH + k
                if wqj < 64:
                    emit_wq_load(wqj + 1) if wqj + 1 < 64 else None
                    emit_wq_quant(wqj)
                w = HID_T[k]
                for half, hofs in ((0, 0), (1, HID)):
                    DMA("gpsimd", WST[:, half * 1024:half * 1024 + 8 * w].rearrange("p (kd n) -> p kd n", kd=ND),
                        win_h[:, hofs + k * 128:hofs + k * 128 + w].rearrange("(kd p) n -> p kd n", p=128),
                        "d_w", [bWST], [bWST])
                for half, pbank in ((0, 0), (1, 1)):
                    for kk in range(ND):
                        MM(ps[pbank][0:w, :], WST[:, half * 1024 + kk * w:half * 1024 + (kk + 1) * w],
                           xnT(kk, tlo, 512), start=(kk == 0), stop=(kk == ND - 1),
                           ins=[bWST, bTFM], outs=[bPS[pbank]], inc=(kk == ND - 1))
                ACT(SC1[0:w, 0:512], ps[0][0:w, :], AF.Identity, [bPS[0], bBIV], [bSC1], bias=BIV[0:w, k:k + 1])
                ACT(SC2[0:w, 0:512], ps[1][0:w, :], AF.Silu, [bPS[1], bBIV], [bSC2], bias=BIV[0:w, NKH + k:NKH + k + 1])
                TT("vector", PB[0:w, k * 512:k * 512 + 512], SC1[0:w, 0:512], SC2[0:w, 0:512],
                   OP.mult, [bSC1, bSC2], [bPB])
            for mt in range(4):
                t_glob = tb * 4 + mt
                for nn in range(2):
                    pb_ = bPS[2 + (mt * 2 + nn) % 2]
                    psn = ps[2 + (mt * 2 + nn) % 2]
                    for k in range(NKH):
                        w = HID_T[k]
                        MM(psn[:, :], PB[0:w, k * 512 + mt * 128:k * 512 + mt * 128 + 128],
                           PW[0:w, k * D + nn * 512:k * D + nn * 512 + 512],
                           start=(k == 0), stop=False, ins=[bPB, bPW], outs=[pb_], inc=False)
                    MM(psn[:, :], ONES[0:1, 0:128], ROWB[0:1, nn * 512:nn * 512 + 512],
                       start=False, stop=True, ins=[bONES, bROWB], outs=[pb_], inc=True)
                    TT("vector", R0[:, t_glob * D + nn * 512:t_glob * D + nn * 512 + 512],
                       psn[:, :], R0[:, t_glob * D + nn * 512:t_glob * D + nn * 512 + 512],
                       OP.add, [pb_, bR0[t_glob]], [bR0[t_glob]])

        bw1qT_d, bw2qT_d = B_("w1qT_d"), B_("w2qT_d")
        for cc in range(ND):
            for hh in range(2):
                DMA("sync", WST[:, 0:2048], w1q_d[hh * 2048:(hh + 1) * 2048, cc * 128:(cc + 1) * 128],
                    "d_t", [bw1q_d, bWST], [bWST], transpose=True)
                DMA("sync", w1qT_d[cc * 128:(cc + 1) * 128, hh * 2048:(hh + 1) * 2048], WST[:, 0:2048],
                    "d_t", [bWST], [bw1qT_d])
        for jj in range(NJ):
            DMA("sync", WST[:, 0:1024], w2q_d[:, jj * 128:(jj + 1) * 128], "d_t",
                [bw2q_d, bWST], [bWST], transpose=True)
            DMA("sync", w2qT_d[jj * 128:(jj + 1) * 128, :], WST[:, 0:1024], "d_t", [bWST], [bw2qT_d])

        stats_batch(lambda t: (R0[:, t * D:(t + 1) * D], [bR0[t]]), NT, EPS1)
        for t in range(NT):
            dst, bdst = (SC2, bSC2) if t % 2 == 0 else (SC3, bSC3)
            xsrc, bxsrc = (SC4, bSC4) if t % 2 == 0 else (SC1, bSC1)
            DMA("gpsimd", xsrc[:, :], x_h[t * 128:(t + 1) * 128, :], "d_x", [bxsrc], [bxsrc])
            TS2("vector", dst[:, :], R0[:, t * D:(t + 1) * D], VST[:, t:t + 1], VST[:, 16 + t:17 + t],
                OP.subtract, OP.mult, [bR0[t], bVST], [bdst])
            TT("vector", dst[:, :], dst[:, :], xsrc[:, :], OP.add, [bdst, bxsrc], [bdst])
            DMA("sync", x1_d[t * 128:(t + 1) * 128, :], dst[:, :], "d_b", [bdst], [bx1_d])
            ACT(TRASH[:, :D], dst[:, :], AF.Identity, [bdst], [bTRASH, bVST], accum=VST[:, 32 + t:33 + t])
            ACT(TRASH[:, :D], dst[:, :], AF.Square, [bdst], [bTRASH, bVST], accum=VST[:, 48 + t:49 + t])
        TS1("vector", VST[:, 32:48], VST[:, 32:48], 1.0 / D, OP.mult, [bVST], [bVST])
        TS1("vector", VST[:, 48:64], VST[:, 48:64], 1.0 / D, OP.mult, [bVST], [bVST])
        TT("vector", VST[:, 0:16], VST[:, 32:48], VST[:, 32:48], OP.mult, [bVST], [bVST])
        TT("vector", VST[:, 48:64], VST[:, 48:64], VST[:, 0:16], OP.subtract, [bVST], [bVST])
        TS1("vector", VST[:, 48:64], VST[:, 48:64], EPS1, OP.add, [bVST], [bVST])
        ACT(VST[:, 0:16], VST[:, 48:64], AF.Sqrt, [bVST], [bVST])
        RCP(VST[:, 48:64], VST[:, 0:16], [bVST], [bVST])
        for t in range(NT):
            xsrc, bxsrc = (SC4, bSC4) if t % 2 == 0 else (SC1, bSC1)
            DMA("gpsimd", xsrc[:, :], x1_d[t * 128:(t + 1) * 128, :], "d_x", [bx1_d, bxsrc], [bxsrc])
            dst, bdst = (SC2, bSC2) if t % 2 == 0 else (SC3, bSC3)
            TS2("vector", dst[:, :], xsrc[:, :], VST[:, 32 + t:33 + t], VST[:, 48 + t:49 + t],
                OP.subtract, OP.mult, [bxsrc, bVST], [bdst])
            if "xf" in dbg:
                DMA("gpsimd", dbg_h["xf"][t * 128:(t + 1) * 128, :], dst[:, :], "d_g",
                    [bdst], [B_("xfdbg")])
            qx(dst[:, :], dst[:, :], 1024, fl_wide, "43bf", [bdst], [bdst])
            DMA("sync", xq_d[t * 128:(t + 1) * 128, :], dst[:, :], "d_b", [bdst], [bxq_d])

        bout_dma = B_("out")
        bhq = [B_("hq0"), B_("hq1")]
        bxqT = [B_("xqT0"), B_("xqT1")]

        def h_phase(tb):
            par = tb % 2
            hqo = par * 8192
            tlo = tb * FTB
            for cc in range(ND):
                DMA("sync", TFM[:, cc * S + par * FTB:cc * S + par * FTB + FTB],
                    xq_d[tlo:tlo + FTB, cc * 128:(cc + 1) * 128],
                    "d_t", [bxq_d, bxqT[par]], [bxqT[par]], transpose=True)
            for js in range(NJ):
                if js % 2 == 0:
                    wbuf, bwbuf, wch = WST[:, 0:1024], B_("wstA"), "d_fa"
                else:
                    wbuf, bwbuf, wch = PB[:, 16384:17408], B_("pbw1"), "d_fb"
                extra_outs = [bwbuf] if not (tb == 0 and js < 2) else [bwbuf, bPB if js == 1 else bWST]
                DMA("sync", wbuf.rearrange("p (kd n) -> p kd n", kd=ND),
                    w1qT_d[:, js * 128:(js + 1) * 128].rearrange("(kd p) n -> p kd n", p=128),
                    wch, [bw1qT_d], extra_outs)
                pb_ = bPS[4 + js % 2]
                psn = ps[4 + js % 2]
                for kk in range(ND):
                    MM(psn[:, 0:FTB], wbuf[:, kk * 128:(kk + 1) * 128],
                       TFM[:, kk * S + par * FTB:kk * S + par * FTB + FTB],
                       start=(kk == 0), stop=(kk == ND - 1), ins=[bwbuf, bxqT[par]],
                       outs=[pb_], inc=(kk == ND - 1))
                ACT(SCH1[:, (js % 4) * FTB:(js % 4 + 1) * FTB], psn[:, 0:FTB], AF.Relu,
                    [pb_, bQB1], [bSH1], bias=QB1[:, js:js + 1])
                if js % 4 == 3:
                    qpair_h(PB[:, hqo + (js - 3) * FTB:hqo + (js + 1) * FTB], SCH1[:, 0:1024],
                            1024, [bSH1], [bhq[par]])
                    if "hq" in dbg:
                        DMA("gpsimd",
                            dbg_h["hq"][(js - 3) * 128:(js + 1) * 128, tlo:tlo + FTB].rearrange("(a p) n -> p a n", p=128),
                            PB[:, hqo + (js - 3) * FTB:hqo + (js + 1) * FTB].rearrange("p (a n) -> p a n", a=4),
                            "d_g", [bhq[par]], [B_("dbgout")])

        def y_phase(tb):
            par = tb % 2
            hqo = par * 8192
            for js in range(NJ):
                if js % 2 == 0:
                    ybuf, bybuf, ych = WST[:, 1024:2048], B_("wstB"), "d_fc"
                else:
                    ybuf, bybuf, ych = PB[:, 17408:18432], B_("pbw2"), "d_fd"
                DMA("sync", ybuf, w2qT_d[js * 128:(js + 1) * 128, :], ych, [bw2qT_d], [bybuf])
                for sub in range(2):
                    for ih in range(2):
                        bidx = sub * 2 + ih
                        MM(ps[bidx][:, :], PB[:, hqo + js * FTB + sub * 128:hqo + js * FTB + sub * 128 + 128],
                           ybuf[:, ih * 512:ih * 512 + 512],
                           start=(js == 0), stop=False, ins=[bhq[par], bybuf], outs=[bPS[bidx]],
                           inc=(sub == 1 and ih == 1))
            mm_last = None
            for sub in range(2):
                for ih in range(2):
                    bidx = sub * 2 + ih
                    mm_last = MM(ps[bidx][:, :], ONES[0:1, 0:128],
                                 ROWB[0:1, 1024 + ih * 512:1024 + ih * 512 + 512],
                                 start=False, stop=True, ins=[bONES, bROWB], outs=[bPS[bidx]], inc=True)
            mm_bar = em.do("tensor", lambda e, c: e.nop(), extra=[mm_last], inc=True)
            for b_ in range(4):
                bPS[b_].w = mm_bar

        def tail_phase(tb):
            for sub in range(2):
                t_glob = tb * 2 + sub
                for ih in range(2):
                    bidx = sub * 2 + ih
                    o = ih * 512
                    ACT(SYE[:, o:o + 512], ps[bidx][:, :], AF.Copy, [bPS[bidx]], [bSYE])
                    qy510(SYE[:, o:o + 512], SYE[:, o:o + 512], 512, [bSYE], [bSYE])
                DMA("sync", SCH2[:, :], mask_h[t_glob * 128:(t_glob + 1) * 128, 0:512], "d_m", [bSH2], [bSH2])
                DMA("sync", SCH3[:, :], mask_h[t_glob * 128:(t_glob + 1) * 128, 512:1024], "d_m", [bSH3], [bSH3])
                TT("vector", SYE[:, 0:512], SYE[:, 0:512], SCH2[:, :], OP.mult, [bSYE, bSH2], [bSYE])
                TT("vector", SYE[:, 512:1024], SYE[:, 512:1024], SCH3[:, :], OP.mult, [bSYE, bSH3], [bSYE])
                if "ym" in dbg:
                    DMA("gpsimd", dbg_h["ym"][t_glob * 128:(t_glob + 1) * 128, :], SYE[:, :],
                        "d_g", [bSYE], [B_("dbgout")])
                ACT(TRASH[:, :D], SYE[:, :], AF.Identity, [bSYE], [bTRASH, bVST], accum=VST[:, 0:1])
                ACT(TRASH[:, :D], SYE[:, :], AF.Square, [bSYE], [bTRASH, bVST], accum=VST[:, 1:2])
                TS1("vector", VST[:, 0:1], VST[:, 0:1], 1.0 / D, OP.mult, [bVST], [bVST])
                TS1("vector", VST[:, 1:2], VST[:, 1:2], 1.0 / D, OP.mult, [bVST], [bVST])
                TT("vector", VST[:, 2:3], VST[:, 0:1], VST[:, 0:1], OP.mult, [bVST], [bVST])
                TT("vector", VST[:, 1:2], VST[:, 1:2], VST[:, 2:3], OP.subtract, [bVST], [bVST])
                TS1("vector", VST[:, 1:2], VST[:, 1:2], EPS1, OP.add, [bVST], [bVST])
                ACT(VST[:, 2:3], VST[:, 1:2], AF.Sqrt, [bVST], [bVST])
                RCP(VST[:, 1:2], VST[:, 2:3], [bVST], [bVST])
                DMA("gpsimd", SC1[:, :], x1_d[t_glob * 128:(t_glob + 1) * 128, :], "d_x",
                    [bx1_d, bSC1], [bSC1])
                TS2("vector", SC2[:, :], SYE[:, :], VST[:, 0:1], VST[:, 1:2],
                    OP.subtract, OP.mult, [bSYE, bVST], [bSC2])
                TT("vector", SC0[:, :], SC2[:, :], SC1[:, :], OP.add, [bSC2, bSC1], [bSC0])
                DMA("sync", out_h[t_glob * 128:(t_glob + 1) * 128, :], SC0[:, :], "d_o", [bSC0], [bout_dma])

        h_phase(0)
        for tb in range(NFTB):
            y_phase(tb)
            if tb + 1 < NFTB:
                h_phase(tb + 1)
            tail_phase(tb)

        for n in dbg:
            src = {"xn": xn_d, "y": y_d, "xm": xm_d, "xq": xq_d, "x1": x1_d}.get(n)
            if src is None:
                continue
            DMA("gpsimd", dbg_h[n], src, "d_g", [B_(n + "_d")], [B_("dbgout")])

        em.do("tensor", lambda e, c: e.nop(), extra=[(ESEM["tensor"], em.cnt["tensor"])], inc=True)
        em.do("vector", lambda e, c: e.nop(), extra=[(ESEM["vector"], em.cnt["vector"])], inc=True)
        em.do("scalar", lambda e, c: e.nop(), extra=[(ESEM["scalar"], em.cnt["scalar"])], inc=True)
        em.do("gpsimd", lambda e, c: e.nop(), extra=[(ESEM["gpsimd"], em.cnt["gpsimd"])], inc=True)
        fin_extra = [("d_o", em.dcnt["d_o"])]
        if em.dcnt["d_g"]:
            fin_extra.append(("d_g", em.dcnt["d_g"]))
        em.do("sync", lambda e, c: e.nop(), extra=fin_extra, inc=False)

        ctx = {"waited": {}, "sem": sems, "eng": {}}

        def run_queue(name, handle):
            ctx["eng"][name] = handle
            for f in em.q[name]:
                f(ctx)

        @block.sync
        def _(e):
            run_queue("sync", e)

        @block.vector
        def _(e):
            run_queue("vector", e)

        @block.scalar
        def _(e):
            run_queue("scalar", e)

        @block.gpsimd
        def _(e):
            run_queue("gpsimd", e)

        @block.tensor
        def _(e):
            run_queue("tensor", e)

    return nc


def _host_constants():
    i = np.arange(128, dtype=np.float32)
    trel = i[None, :] - i[:, None]
    trel = np.where(trel >= 0, trel, np.float32(1e9)).astype(np.float32)
    tip1 = np.broadcast_to(i[None, :] + 1.0, (128, 128)).astype(np.float32)
    trev = (127.0 - i)[:, None].astype(np.float32)
    ident = np.eye(128, dtype=np.float32)
    return trel, tip1, trev, ident


def _dropout_mask():
    import os
    import jax
    ctx = (jax.experimental.enable_x64() if os.environ.get("BASS_SIM_FP64") == "1"
           else jax.experimental.disable_x64())
    with jax.default_device(jax.devices("cpu")[0]), ctx:
        keep = jax.random.bernoulli(jax.random.key(42, impl="threefry2x32"), 1.0 - 0.2, (8, S, D))
        keep = np.asarray(keep)
    return np.where(keep, np.float16(1.25), np.float16(0.0)).astype(np.float16)


def kernel(L0_attn_x, L0_attn_ln1_w, L0_attn_ln1_b, L0_attn_wq, L0_attn_wk, L0_attn_wv,
           L0_attn_wg, L0_attn_wo, L0_attn_ln2_w, L0_attn_ln2_b, L0_attn_w_in, L0_attn_b_in,
           L0_attn_w_out, L0_attn_b_out, L0_ffn_w1, L0_ffn_b1, L0_ffn_w2, L0_ffn_b2):
    x = np.ascontiguousarray(np.asarray(L0_attn_x, dtype=np.float32))
    trel, tip1, trev, ident = _host_constants()
    mask = _dropout_mask()
    common = {
        "wq": np.ascontiguousarray(L0_attn_wq, np.float32),
        "wk": np.ascontiguousarray(L0_attn_wk, np.float32),
        "wv": np.ascontiguousarray(L0_attn_wv, np.float32),
        "wg": np.ascontiguousarray(L0_attn_wg, np.float32),
        "wo": np.ascontiguousarray(L0_attn_wo, np.float32),
        "ln1w": np.ascontiguousarray(L0_attn_ln1_w, np.float32),
        "ln1b": np.ascontiguousarray(L0_attn_ln1_b, np.float32),
        "ln2w": np.ascontiguousarray(L0_attn_ln2_w, np.float32),
        "ln2b": np.ascontiguousarray(L0_attn_ln2_b, np.float32),
        "win": np.ascontiguousarray(L0_attn_w_in, np.float32),
        "bin": np.ascontiguousarray(L0_attn_b_in, np.float32),
        "wout": np.ascontiguousarray(L0_attn_w_out, np.float32),
        "bout": np.ascontiguousarray(L0_attn_b_out, np.float32),
        "w1": np.ascontiguousarray(L0_ffn_w1, np.float32),
        "b1": np.ascontiguousarray(L0_ffn_b1, np.float32),
        "w2": np.ascontiguousarray(L0_ffn_w2, np.float32),
        "b2": np.ascontiguousarray(L0_ffn_b2, np.float32),
        "trel": trel, "tip1": tip1, "trev": trev, "ident": ident,
    }
    nc = build_nc()
    in_maps = []
    for core in range(8):
        m = dict(common)
        m["x"] = np.ascontiguousarray(x[core])
        m["mask"] = np.ascontiguousarray(mask[core])
        in_maps.append(m)
    res = run_bass_kernel_spmd(nc, in_maps, core_ids=list(range(8)))
    return np.stack([r["out"] for r in res.results], axis=0)


# version 15
# speedup vs baseline: 1.4991x; 1.0079x over previous
import contextlib
import math
import sys

import numpy as np
import ml_dtypes

import concourse.bass as bass
import concourse.mybir as mybir
from concourse.bass_utils import run_bass_kernel_spmd

F32 = mybir.dt.float32
BF16 = mybir.dt.bfloat16
F16 = mybir.dt.float16
I16 = mybir.dt.int16
I8 = mybir.dt.int8
F8E4 = mybir.dt.float8e4
M64C = 6755399441055744.0
OP = mybir.AluOpType
AF = mybir.ActivationFunctionType
BF = ml_dtypes.bfloat16
AX = mybir.AxisListType.X

S, D = 2048, 1024
NT, ND = S // 128, D // 128
NH, NPAIR = 16, 8
BLK, NBLK = 512, 4
HID = 2730
HID_T = [128] * 21 + [42]
NKH = len(HID_T)
H4 = 4096
NJ = H4 // 128
FTB, NFTB = 256, 8
EPS1, EPS2 = 1e-5, 1e-6

ENGS = ("sync", "vector", "scalar", "gpsimd", "tensor")
ESEM = {e: "E_" + e for e in ENGS}
DMACH = ("d_x", "d_w", "d_w2", "d_t", "d_b", "d_m", "d_o", "d_c", "d_q", "d_g", "d_fa", "d_fb", "d_fc", "d_fd", "d_wq")


class Buf:
    __slots__ = ("w", "r")

    def __init__(self):
        self.w = None
        self.r = []


class Em:
    def __init__(self):
        self.q = {e: [] for e in ENGS}
        self.cnt = {e: 0 for e in ENGS}
        self.dcnt = {ch: 0 for ch in DMACH}

    def _wait(self, eng, sem_name, val):
        def f(ctx, eng=eng, sem_name=sem_name, val=val):
            key = (eng, sem_name)
            if ctx["waited"].get(key, -1) >= val:
                return
            ctx["waited"][key] = val
            ctx["eng"][eng].wait_ge(ctx["sem"][sem_name], val)

        self.q[eng].append(f)

    def do(self, eng, build, ins=(), outs=(), ch=None, extra=(), inc=True):
        fr = sys._getframe(1)
        site = f"{fr.f_code.co_name}:{fr.f_lineno}"
        fr2 = fr.f_back
        if fr2 is not None:
            site += f" < {fr2.f_code.co_name}:{fr2.f_lineno}"
        _build0 = build

        def build(e, c, _b=_build0, _site=site):
            try:
                return _b(e, c)
            except Exception as ex:
                raise RuntimeError(f"emit failed at {_site}: {ex}") from ex

        waits = []
        for b in ins:
            if b.w is not None:
                waits.append(b.w)
        for b in outs:
            if b.w is not None:
                waits.append(b.w)
            waits.extend(b.r)
        waits.extend(t for t in extra if t is not None)
        seen = {}
        for s_, v_ in waits:
            if s_ in self.dcnt:
                v_ = self.dcnt[s_]
            if seen.get(s_, -1) < v_:
                seen[s_] = v_
        for s_, v_ in sorted(seen.items()):
            if s_ == ESEM[eng] and v_ > self.cnt[eng]:
                continue
            self._wait(eng, s_, v_)
        if ch is not None:
            if self.dcnt[ch] > 0:
                self._wait(eng, ch, self.dcnt[ch])
            self.dcnt[ch] += 16
            val = self.dcnt[ch]

            def f(ctx, build=build, eng=eng, ch=ch):
                build(ctx["eng"][eng], ctx).then_inc(ctx["sem"][ch], 16)

            self.q[eng].append(f)
            tok = (ch, val)
        elif inc:
            self.cnt[eng] += 1
            val = self.cnt[eng]

            def f(ctx, build=build, eng=eng, sem_name=ESEM[eng]):
                build(ctx["eng"][eng], ctx).then_inc(ctx["sem"][sem_name], 1)

            self.q[eng].append(f)
            tok = (ESEM[eng], val)
        else:
            def f(ctx, build=build, eng=eng):
                build(ctx["eng"][eng], ctx)

            self.q[eng].append(f)
            tok = None
        if tok is None and ch is None:
            tok_eff = (ESEM[eng], self.cnt[eng] + 1)
        else:
            tok_eff = tok
        if tok_eff is not None:
            for b in ins:
                b.r.append(tok_eff)
            for b in outs:
                b.w = tok_eff
                b.r = []
        return tok


def _pc(h):
    g = 1.0 - 2.0 ** (-5.0 - h)
    lng = math.log(g)
    hi = np.float32(lng)
    lo = np.float32(lng - float(hi))
    return float(hi), float(lo)


def build_nc(dbg=()):
    nc = bass.Bass("TRN2", target_bir_lowering=False, debug=False)
    em = Em()

    I = lambda n, sh, dt_: nc.dram_tensor(n, sh, dt_, kind="ExternalInput").ap()
    x_h = I("x", [S, D], F32)
    wq_h = I("wq", [D, D], F32)
    wk_h = I("wk", [D, D], F32)
    wv_h = I("wv", [D, D], F32)
    wg_h = I("wg", [D, D], F32)
    wo_h = I("wo", [D, D], F32)
    ln_h = {n: I(n, [D], F32) for n in ("ln1w", "ln1b", "ln2w", "ln2b")}
    win_h = I("win", [D, 2 * HID], F32)
    bin_h = I("bin", [2 * HID], F32)
    wout_h = I("wout", [HID, D], F32)
    bout_h = I("bout", [D], F32)
    w1_h = I("w1", [H4, D], F32)
    b1_h = I("b1", [H4], F32)
    w2_h = I("w2", [D, H4], F32)
    b2_h = I("b2", [D], F32)
    mask_h = I("mask", [S, D], F16)
    trel_h = I("trel", [128, 128], F32)
    tip1_h = I("tip1", [128, 128], F32)
    trev_h = I("trev", [128, 1], F32)
    ident_h = I("ident", [128, 128], F32)
    out_h = nc.dram_tensor("out", [S, D], F32, kind="ExternalOutput").ap()
    _dbg_shapes = {"hq": [H4, S], "ym": [S, D]}
    dbg_h = {n: nc.dram_tensor("dbg_" + n, _dbg_shapes.get(n, [S, D]), F32, kind="ExternalOutput").ap() for n in dbg}

    T = lambda n, sh, dt_: nc.dram_tensor(n, sh, dt_, kind="Internal").ap()
    xn_d = T("xn_d", [S, D], BF16)
    y_d = T("y_d", [S, D], BF16)
    xm_d = T("xm_d", [S, D], BF16)
    xq_d = T("xq_d", [S, D], BF16)
    x1_d = T("x1_d", [S, D], BF16)
    w1q_d = T("w1q_d", [H4, D], BF16)
    w1qT_d = T("w1qT_d", [D, H4], BF16)
    w2q_d = T("w2q_d", [D, H4], BF16)
    w2qT_d = T("w2qT_d", [H4, D], BF16)

    es = contextlib.ExitStack()
    with es:
        SBt = lambda n, sh, dt_: es.enter_context(nc.sbuf_tensor(n, sh, dt_))
        TFM = SBt("TFM", [128, ND * S], BF16)
        R0 = SBt("R0", [128, NT * D], BF16)
        PW = SBt("PW", [128, 22528], BF16)
        PB = SBt("PB", [128, 20480], BF16)
        WST = SBt("WST", [128, 2048], BF16)
        SC0 = SBt("SC0", [128, 1024], F32)
        SCF = SBt("SCF", [128, 1024], F32)
        SC1 = SBt("SC1", [128, 1024], BF16)
        SC2 = SBt("SC2", [128, 1024], BF16)
        SC3 = SBt("SC3", [128, 1024], BF16)
        SC4 = SBt("SC4", [128, 1024], BF16)
        SCI = SBt("SCI", [128, 1024], I16)
        SCI2 = SBt("SCI2", [128, 1024], I16)
        SCH1 = SBt("SCH1", [128, 1024], F16)
        SCH2 = SBt("SCH2", [128, 512], F16)
        SCH3 = SBt("SCH3", [128, 512], F16)
        SCH4 = SBt("SCH4", [128, 512], F16)
        SYE = SBt("SYE", [128, 1024], F16)
        QL16 = SBt("QL16", [128, 512], I16)
        GQ1 = SBt("GQ1", [128, 512], BF16)
        F8Q = SBt("F8Q", [128, 1024], F8E4)
        L8Q = SBt("L8Q", [128, 1024], I8)
        GQI = SBt("GQI", [128, 1024], I16)
        VST = SBt("VST", [128, 66], F32)
        TRASH = SBt("TRASH", [128, 1024], BF16)
        DKT = SBt("DKT", [128, NH * 128], BF16)
        GMT = SBt("GMT", [128, NPAIR * 128], BF16)
        KGV = SBt("KGV", [128, NH], F32)
        GCV = SBt("GCV", [128, NPAIR], F32)
        SBLK = SBt("SBLK", [128, NPAIR * 128], F32)
        CST = SBt("CST", [128, 128], F32)
        IDT = SBt("IDT", [128, 128], F32)
        LNV = SBt("LNV", [128, 4 * ND], F32)
        BIV = SBt("BIV", [128, 2 * NKH], F32)
        QB1 = SBt("QB1", [128, NJ], F32)
        ROWS = SBt("ROWS", [1, 1024], F32)
        ROWB = SBt("ROWB", [1, 2048], BF16)
        ONES = SBt("ONES", [1, 512], BF16)
        ps = [es.enter_context(nc.psum_tensor(f"ps{i}", [128, 512], F32)) for i in range(8)]
        sems = {n: es.enter_context(nc.semaphore(n)) for n in list(ESEM.values()) + list(DMACH)}
        block = es.enter_context(nc.Block())

        _bufs = {}

        def B_(key):
            if key not in _bufs:
                _bufs[key] = Buf()
            return _bufs[key]

        bPS = [B_(("ps", i)) for i in range(8)]
        bSC0, bSCF, bSC1, bSC2, bSC3, bSC4 = (B_(x) for x in ("SC0", "SCF", "SC1", "SC2", "SC3", "SC4"))
        bSCI, bSCI2 = B_("SCI"), B_("SCI2")
        bSH1, bSH2, bSH3, bSH4, bSYE = B_("SCH1"), B_("SCH2"), B_("SCH3"), B_("SCH4"), B_("SYE")
        bVST, bTRASH, bCST, bROWS, bROWB = B_("VST"), B_("TRASH"), B_("CST"), B_("ROWS"), B_("ROWB")
        bWST, bTFM, bPW, bPB = B_("WST"), B_("TFM"), B_("PW"), B_("PB")
        bSBLK, bIDT, bONES = B_("SBLK"), B_("IDT"), B_("ONES")
        bLNV, bBIV, bQB1 = B_("LNV"), B_("BIV"), B_("QB1")
        bDKT, bGMT, bKGV, bGCV = B_("DKT"), B_("GMT"), B_("KGV"), B_("GCV")
        bQX, bGQ1 = B_("qx"), B_("gq1")
        bR0 = [B_(("R0", t)) for t in range(NT)]
        bSLOT = [B_(("slot", s)) for s in range(2)]

        def TT(eng, out, a, b, op, ins, outs, extra=()):
            return em.do(eng, lambda e, c: e.tensor_tensor(out, a, b, op=op), ins, outs, extra=extra)

        def TS1(eng, out, a, s1, op, ins, outs, extra=()):
            return em.do(eng, lambda e, c: e.tensor_single_scalar(out, a, s1, op=op), ins, outs, extra=extra)

        def TS2(eng, out, a, s1, s2, op0, op1, ins, outs, extra=()):
            if eng == "gpsimd":
                t = em.do(eng, lambda e, c: e.tensor_single_scalar(out, a, s1, op=op0), ins, outs, extra=extra)
                return em.do(eng, lambda e, c: e.tensor_single_scalar(out, out, s2, op=op1), list(outs), list(outs))
            return em.do(eng, lambda e, c: e.tensor_scalar(out, a, s1, s2, op0=op0, op1=op1), ins, outs, extra=extra)

        def STT(out, a, s, b, op0, op1, ins, outs, extra=()):
            return em.do("vector", lambda e, c: e.scalar_tensor_tensor(out, a, s, b, op0=op0, op1=op1), ins, outs, extra=extra)

        def CP(eng, out, a, ins, outs, extra=()):
            return em.do(eng, lambda e, c: e.tensor_copy(out, a), ins, outs, extra=extra)

        def ACT(out, a, func, ins, outs, bias=0.0, scale=1.0, accum=None, extra=()):
            return em.do("scalar", lambda e, c: e.activation(out, a, func, bias=bias, scale=scale, accum_out=accum), ins, outs, extra=extra)

        def RED(out, a, op, ins, outs, extra=()):
            return em.do("vector", lambda e, c: e.tensor_reduce(out, a, axis=AX, op=op), ins, outs, extra=extra)

        def MS(eng, ap, v, ins, outs, extra=()):
            return em.do(eng, lambda e, c: e.memset(ap, v), ins, outs, extra=extra)

        def MM(po, lhsT, rhs, start, stop, ins, outs, extra=(), inc=True, tp=None):
            return em.do("tensor", lambda e, c: e.matmul(po, lhsT, rhs, start=start, stop=stop, tile_position=tp), ins, outs, extra=extra, inc=True)

        def TRANS(po, in_, ins, outs, extra=(), P=1):
            return em.do("tensor", lambda e, c: e.transpose(po, in_, IDT[0:P, 0:P]), list(ins) + [bIDT], outs, extra=extra, inc=True)

        def DMA(eng, out, in_, ch, ins, outs, extra=(), transpose=False):
            if transpose:
                return em.do(eng, lambda e, c: e.dma_start(out=out, in_=in_, transpose=True), ins, outs, ch=ch, extra=extra)
            return em.do(eng, lambda e, c: e.dma_start(out=out, in_=in_), ins, outs, ch=ch, extra=extra)

        def RCP(out, a, ins, outs, extra=()):
            return em.do("vector", lambda e, c: e.reciprocal(out, a), ins, outs, extra=extra)

        def qx(dst, src, W, fl, kind, ins, outs, n=128):
            (Fr, br), (Fg, bg), (Fc, bc), (Fl, bl), (Fn, bn), (Fx, bx), (Fb, bb) = \
                [(a[0:n, 0:W], b) for a, b in fl]
            ti, t2, pm = SCI[0:n, :W], SCI2[0:n, :W], GQI[0:n, :W]
            f8, l8 = F8Q[0:n, :W], L8Q[0:n, :W]
            V = "vector"
            if kind == "43bf":
                TS1(V, ti, src.bitcast(I16), 0x7F80, OP.bitwise_and, list(ins), [bQX])
                TS2(V, t2, ti, 121 << 7, 384, OP.max, OP.subtract, [bQX], [bQX])
                TS1(V, Fn, ti, 15616, OP.is_ge, [bQX], [bn])
                TS2(V, pm, t2, -1, 32512, OP.mult, OP.add, [bQX], [bQX])
                CP(V, f8, src, list(ins), [bQX])
            else:
                TS1(V, ti, src.bitcast(I16), 0x7C00, OP.bitwise_and, list(ins), [bQX])
                clampv = (9 << 10) if kind == "43f16" else (1 << 10)
                TS1(V, ti, ti, clampv, OP.max, [bQX], [bQX])
                TS1(V, Fn, ti, clampv + (1 << 10), OP.is_ge, [bQX], [bn])
                TS1(V, ti, ti, 3, OP.logical_shift_right, [bQX], [bQX])
                if kind == "43f16":
                    TS2(V, pm, ti, -1, (127 + 15 + 3) << 7, OP.mult, OP.add, [bQX], [bQX])
                    TS1(V, t2, ti, (127 - 15 - 3) << 7, OP.add, [bQX], [bQX])
                    CP(V, f8, src, list(ins), [bQX])
                else:
                    TS2(V, pm, ti, -1, (127 + 15 + 10) << 7, OP.mult, OP.add, [bQX], [bQX])
                    TS1(V, t2, ti, (127 - 15 - 10) << 7, OP.add, [bQX], [bQX])
                    CP(V, QL16[0:n, :W], src.bitcast(I16), list(ins), [bQX])
            TT(V, Fr, src, pm.bitcast(BF16), OP.mult, list(ins) + [bQX], [br])
            if kind == "510f16":
                CP(V, Fb, QL16[0:n, :W].bitcast(F16), [bQX], [bb])
            else:
                CP(V, l8, f8.bitcast(I8), [bQX], [bQX])
                CP(V, Fb, l8.bitcast(F8E4), [bQX], [bb])
            TT(V, Fg, Fr, Fr, OP.mult, [br], [bg])
            hi_t = 4194000.0 if kind == "510f16" else 255.9
            lo_t = 1048000.0 if kind == "510f16" else 63.9
            TS1(V, Fc, Fg, hi_t, OP.is_ge, [bg], [bc])
            TS1(V, Fl, Fg, lo_t, OP.is_le, [bg], [bl])
            TT(V, Fl, Fl, Fn, OP.mult, [bl, bn], [bl])
            STT(Fx, Fc, -0.5, Fl, OP.mult, OP.add, [bc, bl], [bx])
            STT(Fg, Fx, 1.0, Fr, OP.add, OP.mult, [bx, br], [bg])
            TS2(V, Fg, Fg, VST[0:n, 64:65], 0.0, OP.mult, OP.add, [bg, bVST], [bg])
            TS2(V, Fr, Fg, M64C, M64C, OP.add, OP.subtract, [bg], [br])
            STT(Fx, Fl, -0.5, Fc, OP.mult, OP.add, [bl, bc], [bx])
            TT(V, Fn, Fr, t2.bitcast(BF16), OP.mult, [br, bQX], [bn])
            STT(Fn, Fx, 1.0, Fn, OP.add, OP.mult, [bx, bn], [bn])
            TS1(V, pm, t2, 6720, OP.add, [bQX], [bQX])
            TT(V, Fl, Fb, pm.bitcast(BF16), OP.add, [bb, bQX], [bl])
            TT(V, Fl, Fl, pm.bitcast(BF16), OP.subtract, [bl, bQX], [bl])
            TT(V, Fb, Fn, Fb, OP.add, [bn, bb], [bb])
            return TT(V, dst, Fb, Fl, OP.subtract, [bb, bl], list(outs))

        MS("vector", VST[:, 64:65], 2.0 ** -41, [], [bVST])
        TS2("vector", VST[:, 64:65], VST[:, 64:65], -1.0, 1.0, OP.mult, OP.add, [bVST], [bVST])

        fl_wide = [(WST[:, 0:1024], bWST), (WST[:, 1024:2048], bWST), (TRASH, bTRASH),
                   (SC0, bSC0), (SCF, bSCF), (PB[:, 0:1024], bPB), (PB[:, 1024:2048], bPB)]
        fl_ffn = [(GQ1, bGQ1), (TRASH, bTRASH), (SC1, bSC1), (SC2, bSC2),
                  (SC3, bSC3), (SC4, bSC4), (SCH2, bSH2)]
        fl_tail = [(GQ1, bGQ1), (TRASH, bTRASH), (SC1, bSC1), (SC2, bSC2),
                   (SC3, bSC3), (SC4, bSC4), (SCH3, bSH3)]
        fl_wq = [(PB[:, 2048 + i * 1024:3072 + i * 1024], bPB) for i in range(7)]

        def qpair_h(dst, src, W, ins, outs):
            ti, t2, pm = SCI[:, :W], SCI2[:, :W], GQI[:, :W]
            f8, l8 = F8Q[:, :W], L8Q[:, :W]
            T1, bT1 = PB[:, 18432:18432 + W], bPB
            T2, bT2 = PB[:, 19456:19456 + W], bPB
            T3, bT3 = TRASH[:, :W], bTRASH
            Tc, bTc = SC1[:, :W], bSC1
            Tn, bTn = SC2[:, :W], bSC2
            Tx, bTx = SC3[:, :W], bSC3
            Ty, bTy = SC4[:, :W], bSC4
            V = "vector"
            TS1(V, ti, src.bitcast(I16), 0x7C00, OP.bitwise_and, list(ins), [bQX])
            TS1(V, ti, ti, 1 << 10, OP.max, [bQX], [bQX])
            TS1(V, Tn, ti, 2 << 10, OP.is_ge, [bQX], [bTn])
            TS1(V, ti, ti, 3, OP.logical_shift_right, [bQX], [bQX])
            TS2(V, pm, ti, -1, 19456, OP.mult, OP.add, [bQX], [bQX])
            TS1(V, t2, ti, 13056, OP.add, [bQX], [bQX])
            TT(V, T1, src, pm.bitcast(BF16), OP.mult, list(ins) + [bQX], [bT1])
            TS1(V, Tc, T1, 2047.9, OP.is_ge, [bT1], [bTc])
            TS1(V, Tx, T1, 1023.9, OP.is_ge, [bT1], [bTx])
            TT(V, Tc, Tc, Tn, OP.mult, [bTc, bTn], [bTc])
            TS2(V, Tx, Tx, -1.0, 1.0, OP.mult, OP.add, [bTx], [bTx])
            TT(V, Tx, Tx, Tn, OP.mult, [bTx, bTn], [bTx])
            STT(Ty, Tc, -0.5, Tx, OP.mult, OP.add, [bTc, bTx], [bTy])
            STT(T1, Ty, 1.0, T1, OP.add, OP.mult, [bTy, bT1], [bT1])
            TS2(V, T1, T1, VST[:, 64:65], 0.0, OP.mult, OP.add, [bT1, bVST], [bT1])
            TS2(V, T1, T1, M64C, M64C, OP.add, OP.subtract, [bT1], [bT1])
            TT(V, T1, T1, t2.bitcast(BF16), OP.mult, [bT1, bQX], [bT1])
            STT(Tx, Tx, -0.5, Tc, OP.mult, OP.add, [bTx, bTc], [bTx])
            STT(T1, Tx, 1.0, T1, OP.add, OP.mult, [bTx, bT1], [bT1])
            TS2(V, t2, t2, 896, 15104, OP.add, OP.max, [bQX], [bQX])
            TS2(V, pm, t2, -1, 32512, OP.mult, OP.add, [bQX], [bQX])
            TT(V, T1, T1, pm.bitcast(BF16), OP.mult, [bT1, bQX], [bT1])
            STT(T1, Ty, 1.0, T1, OP.add, OP.mult, [bTy, bT1], [bT1])
            TS2(V, T1, T1, VST[:, 64:65], 0.0, OP.mult, OP.add, [bT1, bVST], [bT1])
            TS2(V, T1, T1, M64C, M64C, OP.add, OP.subtract, [bT1], [bT1])
            TT(V, T1, T1, t2.bitcast(BF16), OP.mult, [bT1, bQX], [bT1])
            STT(T1, Tx, 1.0, T1, OP.add, OP.mult, [bTx, bT1], [bT1])
            CP(V, f8, src, list(ins), [bQX])
            CP(V, l8, f8.bitcast(I8), [bQX], [bQX])
            CP(V, T2, l8.bitcast(F8E4), [bQX], [bT2])
            TS1(V, pm, t2, 6720, OP.add, [bQX], [bQX])
            TT(V, T3, T2, pm.bitcast(BF16), OP.add, [bT2, bQX], [bT3])
            TT(V, T3, T3, pm.bitcast(BF16), OP.subtract, [bT3, bQX], [bT3])
            TT(V, T2, T1, T2, OP.add, [bT1, bT2], [bT2])
            return TT(V, dst, T2, T3, OP.subtract, [bT2, bT3], list(outs))

        def qy510(dst, src, W, ins, outs):
            ti, t2, pm = SCI[:, :W], SCI2[:, :W], GQI[:, :W]
            T1, bT1 = TRASH[:, :W], bTRASH
            T2, bT2 = SCH4[:, :W], bSH4
            T3, bT3 = GQ1[:, :W], bGQ1
            Tc, bTc = SC1[:, :W], bSC1
            Tn, bTn = SC2[:, :W], bSC2
            Tx, bTx = SC3[:, :W], bSC3
            Ty, bTy = SC4[:, :W], bSC4
            V = "vector"
            TS1(V, ti, src.bitcast(I16), 0x7C00, OP.bitwise_and, list(ins), [bQX])
            TS1(V, ti, ti, 1 << 10, OP.max, [bQX], [bQX])
            TS1(V, Tn, ti, 2 << 10, OP.is_ge, [bQX], [bTn])
            TS1(V, ti, ti, 3, OP.logical_shift_right, [bQX], [bQX])
            TS2(V, pm, ti, -1, 19456, OP.mult, OP.add, [bQX], [bQX])
            TS1(V, t2, ti, 13056, OP.add, [bQX], [bQX])
            TT(V, T1, src, pm.bitcast(BF16), OP.mult, list(ins) + [bQX], [bT1])
            TT(V, Tc, T1, T1, OP.mult, [bT1], [bTc])
            TS1(V, Tx, Tc, 1048000.0, OP.is_ge, [bTc], [bTx])
            TS1(V, Tc, Tc, 4194000.0, OP.is_ge, [bTc], [bTc])
            TT(V, Tc, Tc, Tn, OP.mult, [bTc, bTn], [bTc])
            TS2(V, Tx, Tx, -1.0, 1.0, OP.mult, OP.add, [bTx], [bTx])
            TT(V, Tx, Tx, Tn, OP.mult, [bTx, bTn], [bTx])
            STT(Ty, Tc, -0.5, Tx, OP.mult, OP.add, [bTc, bTx], [bTy])
            STT(T1, Ty, 1.0, T1, OP.add, OP.mult, [bTy, bT1], [bT1])
            TS2(V, T1, T1, VST[:, 64:65], 0.0, OP.mult, OP.add, [bT1, bVST], [bT1])
            TS2(V, T1, T1, M64C, M64C, OP.add, OP.subtract, [bT1], [bT1])
            TT(V, T1, T1, t2.bitcast(BF16), OP.mult, [bT1, bQX], [bT1])
            STT(Tx, Tx, -0.5, Tc, OP.mult, OP.add, [bTx, bTc], [bTx])
            STT(T1, Tx, 1.0, T1, OP.add, OP.mult, [bTx, bT1], [bT1])
            CP(V, QL16[:, :W], src.bitcast(I16), list(ins), [bQX])
            CP(V, T2, QL16[:, :W].bitcast(F16), [bQX], [bT2])
            TS1(V, pm, t2, 6720, OP.add, [bQX], [bQX])
            TT(V, T3, T2, pm.bitcast(BF16), OP.add, [bT2, bQX], [bT3])
            TT(V, T3, T3, pm.bitcast(BF16), OP.subtract, [bT3, bQX], [bT3])
            TT(V, T2, T1, T2, OP.add, [bT1, bT2], [bT2])
            return TT(V, dst, T2, T3, OP.subtract, [bT2, bT3], list(outs))

        DMA("sync", IDT[:, :], ident_h, "d_c", [], [bIDT])
        MS("vector", ONES[:, :], 1.0, [], [bONES])
        DMA("sync", CST[:, :], trel_h, "d_c", [], [bCST])
        for h in range(NH):
            hi, lo = _pc(h)
            TS1("vector", SC0[:, :128], CST[:, :], hi, OP.mult, [bCST], [bSC0])
            TS1("vector", SCF[:, :128], CST[:, :], lo, OP.mult, [bCST], [bSCF])
            TT("vector", SC0[:, :128], SC0[:, :128], SCF[:, :128], OP.add, [bSC0, bSCF], [bSC0])
            ACT(DKT[:, h * 128:(h + 1) * 128], SC0[:, :128], AF.Exp, [bSC0], [bDKT])
        DMA("sync", CST[:, :], tip1_h, "d_c", [bCST], [bCST])
        for p in range(NPAIR):
            for half, h in ((0, 2 * p), (1, 2 * p + 1)):
                hi, lo = _pc(h)
                sl = slice(64 * half, 64 * half + 64)
                TS1("vector", SC0[sl, :128], CST[sl, :], hi, OP.mult, [bCST], [bSC0])
                TS1("vector", SCF[sl, :128], CST[sl, :], lo, OP.mult, [bCST], [bSCF])
            TT("vector", SC0[:, :128], SC0[:, :128], SCF[:, :128], OP.add, [bSC0, bSCF], [bSC0])
            ACT(GMT[:, p * 128:(p + 1) * 128], SC0[:, :128], AF.Exp, [bSC0], [bGMT])
        DMA("sync", CST[:, 0:1], trev_h, "d_c", [bCST], [bCST])
        for h in range(NH):
            hi, lo = _pc(h)
            TS1("vector", SC0[:, 0:1], CST[:, 0:1], hi, OP.mult, [bCST], [bSC0])
            TS1("vector", SCF[:, 0:1], CST[:, 0:1], lo, OP.mult, [bCST], [bSCF])
            TT("vector", SC0[:, 0:1], SC0[:, 0:1], SCF[:, 0:1], OP.add, [bSC0, bSCF], [bSC0])
            ACT(KGV[:, h:h + 1], SC0[:, 0:1], AF.Exp, [bSC0], [bKGV])
        for p in range(NPAIR):
            for half, h in ((0, 2 * p), (1, 2 * p + 1)):
                hi, lo = _pc(h)
                sl = slice(64 * half, 64 * half + 64)
                MS("vector", SC0[sl, 0:1], float(np.float32(hi)) * 128.0, [], [bSC0])
                MS("vector", SCF[sl, 0:1], float(np.float32(lo)) * 128.0, [], [bSCF])
            TT("vector", SC0[:, 0:1], SC0[:, 0:1], SCF[:, 0:1], OP.add, [bSC0, bSCF], [bSC0])
            ACT(GCV[:, p:p + 1], SC0[:, 0:1], AF.Exp, [bSC0], [bGCV])
        MS("vector", SBLK[:, :], 0.0, [], [bSBLK])
        bSBLK16 = B_("sblk16")
        MS("vector", PW[:, 16384:17408], 0.0, [], [bSBLK16])
        for v, name in enumerate(("ln1w", "ln1b", "ln2w", "ln2b")):
            DMA("sync", ROWS[0:1, 0:D], ln_h[name].rearrange("(o d) -> o d", o=1), "d_c", [bROWS], [bROWS])
            for cc in range(ND):
                TRANS(ps[7][:, 0:1], ROWS[0:1, cc * 128:cc * 128 + 128], [bROWS], [bPS[7]])
                ACT(LNV[:, v * ND + cc:v * ND + cc + 1], ps[7][:, 0:1], AF.Copy, [bPS[7]], [bLNV])
        LN1W, LN1B, LN2W, LN2B = 0, ND, 2 * ND, 3 * ND
        for half in range(2):
            for k in range(NKH):
                w = HID_T[k]
                DMA("sync", ROWS[0:1, 0:w], bin_h[half * HID + k * 128:half * HID + k * 128 + w].rearrange("(o d) -> o d", o=1),
                    "d_c", [bROWS], [bROWS])
                TRANS(ps[7][0:w, 0:1], ROWS[0:1, 0:w], [bROWS], [bPS[7]])
                ACT(BIV[0:w, half * NKH + k:half * NKH + k + 1], ps[7][0:w, 0:1], AF.Copy, [bPS[7]], [bBIV])
        DMA("sync", ROWS[0:1, 0:D], bout_h.rearrange("(o d) -> o d", o=1), "d_c", [bROWS], [bROWS])
        CP("vector", ROWB[0:1, 0:D], ROWS[0:1, 0:D], [bROWS], [bROWB])
        for cchunk in range(4):
            DMA("sync", ROWS[0:1, 0:1024], b1_h[cchunk * 1024:(cchunk + 1) * 1024].rearrange("(o d) -> o d", o=1),
                "d_c", [bROWS], [bROWS])
            CP("vector", SC4[0:1, 0:1024], ROWS[0:1, 0:1024], [bROWS], [bSC4])
            qx(SC4[0:1, 0:1024], SC4[0:1, 0:1024], 1024, fl_wide, "43bf", [bSC4], [bSC4], n=1)
            CP("vector", SCF[0:1, 0:1024], SC4[0:1, 0:1024], [bSC4], [bSCF])
            for k in range(8):
                j = cchunk * 8 + k
                TRANS(ps[7][:, 0:1], SCF[0:1, k * 128:(k + 1) * 128], [bSCF], [bPS[7]])
                ACT(QB1[:, j:j + 1], ps[7][:, 0:1], AF.Copy, [bPS[7]], [bQB1])
        DMA("sync", ROWS[0:1, 0:D], b2_h.rearrange("(o d) -> o d", o=1), "d_c", [bROWS], [bROWS])
        CP("vector", SC4[0:1, 0:1024], ROWS[0:1, 0:1024], [bROWS], [bSC4])
        qx(ROWB[0:1, 1024:2048], SC4[0:1, 0:1024], 1024, fl_wide, "43bf", [bSC4], [bROWB], n=1)

        def stats_batch(tiles_fn, n, eps, cmu=0, ce2=16, ct1=32, ct2=48):
            for t in range(n):
                ap, bufs = tiles_fn(t)
                ACT(TRASH[:, :D], ap, AF.Identity, bufs, [bTRASH, bVST], accum=VST[:, cmu + t:cmu + t + 1])
                ACT(TRASH[:, :D], ap, AF.Square, bufs, [bTRASH, bVST], accum=VST[:, ce2 + t:ce2 + t + 1])
            TS1("vector", VST[:, cmu:cmu + n], VST[:, cmu:cmu + n], 1.0 / D, OP.mult, [bVST], [bVST])
            TS1("vector", VST[:, ce2:ce2 + n], VST[:, ce2:ce2 + n], 1.0 / D, OP.mult, [bVST], [bVST])
            TT("vector", VST[:, ct1:ct1 + n], VST[:, cmu:cmu + n], VST[:, cmu:cmu + n], OP.mult, [bVST], [bVST])
            TT("vector", VST[:, ce2:ce2 + n], VST[:, ce2:ce2 + n], VST[:, ct1:ct1 + n], OP.subtract, [bVST], [bVST])
            TS1("vector", VST[:, ce2:ce2 + n], VST[:, ce2:ce2 + n], eps, OP.add, [bVST], [bVST])
            ACT(VST[:, ct2:ct2 + n], VST[:, ce2:ce2 + n], AF.Sqrt, [bVST], [bVST])
            RCP(VST[:, ce2:ce2 + n], VST[:, ct2:ct2 + n], [bVST], [bVST])

        bxn_d, by_d, bxm_d, bxq_d, bx1_d = (B_(x) for x in ("xn_d", "y_d", "xm_d", "xq_d", "x1_d"))

        def ln_stats(src_ap, W, src_bufs, mu_sl, rs_sl, eps):
            ACT(TRASH[:, :W], src_ap, AF.Identity, src_bufs, [bTRASH, bVST], accum=VST[:, mu_sl:mu_sl + 1])
            ACT(TRASH[:, :W], src_ap, AF.Square, src_bufs, [bTRASH, bVST], accum=VST[:, rs_sl:rs_sl + 1])
            TS1("vector", VST[:, mu_sl:mu_sl + 1], VST[:, mu_sl:mu_sl + 1], 1.0 / W, OP.mult, [bVST], [bVST])
            TS1("vector", VST[:, rs_sl:rs_sl + 1], VST[:, rs_sl:rs_sl + 1], 1.0 / W, OP.mult, [bVST], [bVST])
            TT("vector", VST[:, 60:61], VST[:, mu_sl:mu_sl + 1], VST[:, mu_sl:mu_sl + 1], OP.mult, [bVST], [bVST])
            TT("vector", VST[:, rs_sl:rs_sl + 1], VST[:, rs_sl:rs_sl + 1], VST[:, 60:61], OP.subtract, [bVST], [bVST])
            TS1("vector", VST[:, rs_sl:rs_sl + 1], VST[:, rs_sl:rs_sl + 1], eps, OP.add, [bVST], [bVST])
            ACT(VST[:, 61:62], VST[:, rs_sl:rs_sl + 1], AF.Sqrt, [bVST], [bVST])
            RCP(VST[:, rs_sl:rs_sl + 1], VST[:, 61:62], [bVST], [bVST])

        for t in range(NT):
            DMA("gpsimd", R0[:, t * D:(t + 1) * D], x_h[t * 128:(t + 1) * 128, :], "d_x",
                [bR0[t]], [bR0[t]])
        stats_batch(lambda t: (R0[:, t * D:(t + 1) * D], [bR0[t]]), NT, EPS1)
        for t in range(NT):
            TS2("vector", R0[:, t * D:(t + 1) * D], R0[:, t * D:(t + 1) * D],
                VST[:, t:t + 1], VST[:, 16 + t:17 + t], OP.subtract, OP.mult, [bR0[t], bVST], [bR0[t]])
        stats_batch(lambda t: (R0[:, t * D:(t + 1) * D], [bR0[t]]), NT, EPS1)
        for t in range(NT):
            dst, bdst = (SC2, bSC2) if t % 2 == 0 else (SC3, bSC3)
            TS2("vector", dst[:, :], R0[:, t * D:(t + 1) * D], VST[:, t:t + 1], VST[:, 16 + t:17 + t],
                OP.subtract, OP.mult, [bR0[t], bVST], [bdst])
            DMA("sync", xn_d[t * 128:(t + 1) * 128, :], dst[:, :], "d_b", [bdst], [bxn_d])

        for cc in range(ND):
            DMA("sync", TFM[:, cc * S:(cc + 1) * S], xn_d[:, cc * 128:(cc + 1) * 128], "d_t",
                [bxn_d, bTFM], [bTFM], transpose=True)
            TS2("vector", TFM[:, cc * S:(cc + 1) * S], TFM[:, cc * S:(cc + 1) * S],
                LNV[:, LN1W + cc:LN1W + cc + 1], LNV[:, LN1B + cc:LN1B + cc + 1],
                OP.mult, OP.add, [bTFM, bLNV], [bTFM])

        bw1q_d, bw2q_d = B_("w1q_d"), B_("w2q_d")
        bWQ = [B_("wq0"), B_("wq1")]

        def qw43f32(dst, src, W, ins, outs):
            ti, t2, pm = SCI[:, :W], SCI2[:, :W], GQI[:, :W]
            f8, l8 = F8Q[:, :W], L8Q[:, :W]
            T1, bT1 = SC1[:, :W], bSC1
            T2, bT2 = SC2[:, :W], bSC2
            T3, bT3 = SC3[:, :W], bSC3
            V = "vector"
            hi16 = src.bitcast(I16).rearrange("p (w two) -> p two w", two=2)[:, 1:2, :].rearrange("p o w -> p (o w)")
            TS1(V, ti, hi16, 0x7F80, OP.bitwise_and, list(ins), [bQX])
            TS2(V, t2, ti, 121 << 7, 384, OP.max, OP.subtract, [bQX], [bQX])
            TS2(V, pm, t2, -1, 32512, OP.mult, OP.add, [bQX], [bQX])
            TT(V, T1, src, pm.bitcast(BF16), OP.mult, list(ins) + [bQX], [bT1])
            TS2(V, T1, T1, VST[:, 64:65], 0.0, OP.mult, OP.add, [bT1, bVST], [bT1])
            TS2(V, T1, T1, M64C, M64C, OP.add, OP.subtract, [bT1], [bT1])
            TT(V, T1, T1, t2.bitcast(BF16), OP.mult, [bT1, bQX], [bT1])
            CP(V, f8, src, list(ins), [bQX])
            CP(V, l8, f8.bitcast(I8), [bQX], [bQX])
            CP(V, T2, l8.bitcast(F8E4), [bQX], [bT2])
            TS1(V, pm, t2, 6720, OP.add, [bQX], [bQX])
            TT(V, T3, T2, pm.bitcast(BF16), OP.add, [bT2, bQX], [bT3])
            TT(V, T3, T3, pm.bitcast(BF16), OP.subtract, [bT3, bQX], [bT3])
            TT(V, T2, T1, T2, OP.add, [bT1, bT2], [bT2])
            return TT(V, dst, T2, T3, OP.subtract, [bT2, bT3], list(outs))

        def wq_src_dst(j):
            if j < 32:
                return (w1_h[j * 128:(j + 1) * 128, :],
                        w1q_d[j * 128:(j + 1) * 128, :], bw1q_d)
            j2 = j - 32
            return (w2_h[(j2 // 4) * 128:(j2 // 4 + 1) * 128, (j2 % 4) * 1024:(j2 % 4) * 1024 + 1024],
                    w2q_d[(j2 // 4) * 128:(j2 // 4 + 1) * 128, (j2 % 4) * 1024:(j2 % 4) * 1024 + 1024], bw2q_d)

        bw1qT_d, bw2qT_d = B_("w1qT_d"), B_("w2qT_d")
        bTBO = B_("tbounce")

        def emit_w1qT(hh):
            for cc in range(ND):
                DMA("sync", PB[:, 16384:18432], w1q_d[hh * 2048:(hh + 1) * 2048, cc * 128:(cc + 1) * 128],
                    "d_t", [bw1q_d, bTBO], [bTBO], transpose=True)
                DMA("sync", w1qT_d[cc * 128:(cc + 1) * 128, hh * 2048:(hh + 1) * 2048], PB[:, 16384:18432],
                    "d_t", [bTBO], [bw1qT_d])

        def emit_w2qT(jj):
            DMA("sync", PB[:, 16384:17408], w2q_d[:, jj * 128:(jj + 1) * 128], "d_t",
                [bw2q_d, bTBO], [bTBO], transpose=True)
            DMA("sync", w2qT_d[jj * 128:(jj + 1) * 128, :], PB[:, 16384:17408], "d_t", [bTBO], [bw2qT_d])

        def emit_wq_load(j, extra_outs=()):
            src, _, _ = wq_src_dst(j)
            slab, bs = (SC0, bSC0) if j % 2 == 0 else (SCF, bSCF)
            DMA("sync", slab[:, :], src, "d_wq", [bs], [bs] + list(extra_outs))

        def emit_wq_quant(j):
            _, dst, bqd = wq_src_dst(j)
            slab, bs = (SC0, bSC0) if j % 2 == 0 else (SCF, bSCF)
            qw43f32(SC4[:, :], slab[:, :], 1024, [bs], [bSC4])
            DMA("sync", dst, SC4[:, :], "d_wq", [bSC4, bs], [bqd])

        def load_wslot(s, wh):
            DMA("gpsimd", PW[:, s * 8192:(s + 1) * 8192].rearrange("p (kd n) -> p kd n", kd=ND),
                wh.rearrange("(kd p) n -> p kd n", p=128), "d_w", [bSLOT[s]], [bSLOT[s]])

        def wslot_tile(s, k, m):
            return PW[:, s * 8192 + k * D + m * 128:s * 8192 + k * D + (m + 1) * 128]

        def wslot_mov(s, k, n):
            return PW[:, s * 8192 + k * D + n * 512:s * 8192 + k * D + n * 512 + 512]

        def xnT(k, lo, w):
            return TFM[:, k * S + lo:k * S + lo + w]

        QBo, KBo, KGo, VBo, GBo = 0, 4096, 8192, 12288, 16384
        bQB, bKB, bKG, bVB, bGB = B_("qb"), B_("kb"), B_("kg"), B_("vb"), B_("gb")

        for Bi in range(NBLK):
            lo = Bi * BLK
            load_wslot(0, wq_h)
            for m in range(ND):
                pb_ = bPS[m % 2]
                psn = ps[m % 2]
                for kk in range(ND):
                    MM(psn[:, :], wslot_tile(0, kk, m), xnT(kk, lo, BLK),
                       start=(kk == 0), stop=(kk == ND - 1),
                       ins=[bSLOT[0], bTFM], outs=[pb_], inc=(kk == ND - 1))
                ACT(PB[:, QBo + m * BLK:QBo + (m + 1) * BLK], psn[:, :], AF.Copy,
                    [pb_], [bQB], scale=0.125)
            load_wslot(1, wk_h)
            for m in range(ND):
                pb_ = bPS[2 + m % 2]
                psn = ps[2 + m % 2]
                for kk in range(ND):
                    MM(psn[:, :], wslot_tile(1, kk, m), xnT(kk, lo, BLK),
                       start=(kk == 0), stop=(kk == ND - 1),
                       ins=[bSLOT[1], bTFM], outs=[pb_], inc=(kk == ND - 1))
                ACT(PB[:, KBo + m * BLK:KBo + (m + 1) * BLK], psn[:, :], AF.Copy, [pb_], [bKB])
            for mt in range(4):
                for nn in range(2):
                    pb_ = bPS[(mt * 2 + nn) % 2]
                    psn = ps[(mt * 2 + nn) % 2]
                    for kk in range(ND):
                        MM(psn[:, :], xnT(kk, lo + mt * 128, 128), wslot_mov(1, kk, nn),
                           start=(kk == 0), stop=(kk == ND - 1),
                           ins=[bSLOT[1], bTFM], outs=[pb_], inc=(kk == ND - 1))
                    for hh in range(8):
                        h = nn * 8 + hh
                        TS1("vector",
                            PB[:, KGo + mt * 1024 + nn * 512 + hh * 64:KGo + mt * 1024 + nn * 512 + hh * 64 + 64],
                            psn[:, hh * 64:hh * 64 + 64], KGV[:, h:h + 1], OP.mult,
                            [pb_, bKGV], [bKG])
            load_wslot(0, wv_h)
            for mt in range(4):
                for nn in range(2):
                    pb_ = bPS[2 + (mt * 2 + nn) % 2]
                    psn = ps[2 + (mt * 2 + nn) % 2]
                    for kk in range(ND):
                        MM(psn[:, :], xnT(kk, lo + mt * 128, 128), wslot_mov(0, kk, nn),
                           start=(kk == 0), stop=(kk == ND - 1),
                           ins=[bSLOT[0], bTFM], outs=[pb_], inc=(kk == ND - 1))
                    ACT(PB[:, VBo + mt * 1024 + nn * 512:VBo + mt * 1024 + nn * 512 + 512],
                        psn[:, :], AF.Copy, [pb_], [bVB])
            load_wslot(1, wg_h)
            for mt in range(4):
                for nn in range(2):
                    pb_ = bPS[(mt * 2 + nn) % 2]
                    psn = ps[(mt * 2 + nn) % 2]
                    for kk in range(ND):
                        MM(psn[:, :], xnT(kk, lo + mt * 128, 128), wslot_mov(1, kk, nn),
                           start=(kk == 0), stop=(kk == ND - 1),
                           ins=[bSLOT[1], bTFM], outs=[pb_], inc=(kk == ND - 1))
                    ACT(PB[:, GBo + mt * 1024 + nn * 512:GBo + mt * 1024 + nn * 512 + 512],
                        psn[:, :], AF.Silu, [pb_], [bGB])

            for cc4 in range(4):
                g = Bi * 4 + cc4
                clo = cc4 * 128
                for p in range(NPAIR):
                    h1, h2 = 2 * p, 2 * p + 1
                    MM(ps[4][:, 0:128], PB[0:64, KBo + p * BLK + clo:KBo + p * BLK + clo + 128],
                       PB[0:64, QBo + p * BLK + clo:QBo + p * BLK + clo + 128],
                       start=True, stop=True, ins=[bKB, bQB], outs=[bPS[4]], inc=True)
                    MM(ps[7][:, 0:128], PB[64:128, KBo + p * BLK + clo:KBo + p * BLK + clo + 128],
                       PB[64:128, QBo + p * BLK + clo:QBo + p * BLK + clo + 128],
                       start=True, stop=True, ins=[bKB, bQB], outs=[bPS[7]], inc=True)
                    TT("vector", SC1[:, 0:128], ps[4][:, 0:128], DKT[:, h1 * 128:(h1 + 1) * 128],
                       OP.mult, [bPS[4], bDKT], [bSC1])
                    TT("vector", SC1[:, 128:256], ps[7][:, 0:128], DKT[:, h2 * 128:(h2 + 1) * 128],
                       OP.mult, [bPS[7], bDKT], [bSC1])
                    TT("vector", SC2[:, 0:128], PB[:, QBo + p * BLK + clo:QBo + p * BLK + clo + 128],
                       GMT[:, p * 128:(p + 1) * 128], OP.mult, [bQB, bGMT], [bSC2])
                    MM(ps[5][:, 0:64], SC1[:, 0:128],
                       PB[:, VBo + cc4 * 1024 + h1 * 64:VBo + cc4 * 1024 + h1 * 64 + 64],
                       start=True, stop=False, ins=[bSC1, bVB], outs=[bPS[5]], inc=False)
                    MM(ps[5][:, 64:128], SC1[:, 128:256],
                       PB[:, VBo + cc4 * 1024 + h2 * 64:VBo + cc4 * 1024 + h2 * 64 + 64],
                       start=False, stop=(g == 0), ins=[bSC1, bVB], outs=[bPS[5]], inc=(g == 0))
                    if g > 0:
                        MM(ps[5][:, 0:128], SC2[:, 0:128], PW[:, 16384 + p * 128:16384 + (p + 1) * 128],
                           start=False, stop=True, ins=[bSC2, bSBLK16], outs=[bPS[5]], inc=True)
                    MM(ps[6][:, 0:128], PB[:, KGo + cc4 * 1024 + p * 128:KGo + cc4 * 1024 + (p + 1) * 128],
                       PB[:, VBo + cc4 * 1024 + p * 128:VBo + cc4 * 1024 + (p + 1) * 128],
                       start=True, stop=True, ins=[bKG, bVB], outs=[bPS[6]], inc=True)
                    ACT(SC3[:, p * 128:(p + 1) * 128], ps[5][:, 0:128], AF.Copy, [bPS[5]], [bSC3])
                    STT(SBLK[0:64, p * 128:p * 128 + 64], SBLK[0:64, p * 128:p * 128 + 64],
                        GCV[0:64, p:p + 1], ps[6][0:64, 0:64], OP.mult, OP.add,
                        [bSBLK, bPS[6], bGCV], [bSBLK])
                    STT(SBLK[64:128, p * 128 + 64:p * 128 + 128], SBLK[64:128, p * 128 + 64:p * 128 + 128],
                        GCV[64:128, p:p + 1], ps[6][64:128, 64:128], OP.mult, OP.add,
                        [bSBLK, bPS[6], bGCV], [bSBLK])
                    CP("vector", PW[:, 16384 + p * 128:16384 + (p + 1) * 128],
                       SBLK[:, p * 128:(p + 1) * 128], [bSBLK], [bSBLK16])
                ACT(SC0[:, :], SC3[:, :], AF.Square, [bSC3], [bSC0])
                RED(VST[:, 0:16], SC3[:, :].rearrange("p (h d) -> p h d", h=16), OP.add, [bSC3], [bVST])
                RED(VST[:, 16:32], SC0[:, :].rearrange("p (h d) -> p h d", h=16), OP.add, [bSC0], [bVST])
                TS1("vector", VST[:, 0:16], VST[:, 0:16], 1.0 / 64, OP.mult, [bVST], [bVST])
                TS1("vector", VST[:, 16:32], VST[:, 16:32], 1.0 / 64, OP.mult, [bVST], [bVST])
                TT("vector", VST[:, 32:48], VST[:, 0:16], VST[:, 0:16], OP.mult, [bVST], [bVST])
                TT("vector", VST[:, 16:32], VST[:, 16:32], VST[:, 32:48], OP.subtract, [bVST], [bVST])
                TS1("vector", VST[:, 16:32], VST[:, 16:32], EPS2, OP.add, [bVST], [bVST])
                ACT(VST[:, 32:48], VST[:, 16:32], AF.Sqrt, [bVST], [bVST])
                RCP(VST[:, 16:32], VST[:, 32:48], [bVST], [bVST])
                for h in range(NH):
                    TS2("vector", SC3[:, h * 64:(h + 1) * 64], SC3[:, h * 64:(h + 1) * 64],
                        VST[:, h:h + 1], VST[:, 16 + h:17 + h], OP.subtract, OP.mult, [bSC3, bVST], [bSC3])
                TT("vector", SC3[:, :], SC3[:, :], PB[:, GBo + cc4 * 1024:GBo + (cc4 + 1) * 1024],
                   OP.mult, [bSC3, bGB], [bSC3])
                DMA("sync", y_d[g * 128:(g + 1) * 128, :], SC3[:, :], "d_b", [bSC3], [by_d])

        for cc in range(ND):
            DMA("sync", TFM[:, cc * S:(cc + 1) * S], y_d[:, cc * 128:(cc + 1) * 128], "d_t",
                [by_d, bTFM], [bTFM], transpose=True)
        load_wslot(0, wo_h)
        for mt in range(NT):
            for nn in range(2):
                pb_ = bPS[(mt * 2 + nn) % 4]
                psn = ps[(mt * 2 + nn) % 4]
                for kk in range(ND):
                    MM(psn[:, :], xnT(kk, mt * 128, 128), wslot_mov(0, kk, nn),
                       start=(kk == 0), stop=(kk == ND - 1),
                       ins=[bSLOT[0], bTFM], outs=[pb_], inc=(kk == ND - 1))
                TT("vector", R0[:, mt * D + nn * 512:mt * D + nn * 512 + 512],
                   psn[:, :], R0[:, mt * D + nn * 512:mt * D + nn * 512 + 512],
                   OP.add, [pb_, bR0[mt]], [bR0[mt]])

        stats_batch(lambda t: (R0[:, t * D:(t + 1) * D], [bR0[t]]), NT, EPS1)
        for t in range(NT):
            dst, bdst = (SC2, bSC2) if t % 2 == 0 else (SC3, bSC3)
            TS2("vector", dst[:, :], R0[:, t * D:(t + 1) * D], VST[:, t:t + 1], VST[:, 16 + t:17 + t],
                OP.subtract, OP.mult, [bR0[t], bVST], [bdst])
            DMA("sync", xm_d[t * 128:(t + 1) * 128, :], dst[:, :], "d_b", [bdst], [bxm_d])
        for cc in range(ND):
            DMA("sync", TFM[:, cc * S:(cc + 1) * S], xm_d[:, cc * 128:(cc + 1) * 128], "d_t",
                [bxm_d, bTFM], [bTFM], transpose=True)
            TS2("vector", TFM[:, cc * S:(cc + 1) * S], TFM[:, cc * S:(cc + 1) * S],
                LNV[:, LN2W + cc:LN2W + cc + 1], LNV[:, LN2B + cc:LN2B + cc + 1],
                OP.mult, OP.add, [bTFM, bLNV], [bTFM])

        for k in range(NKH):
            w = HID_T[k]
            if k == 0:
                DMA("gpsimd", PW[0:w, k * D:(k + 1) * D], wout_h[k * 128:k * 128 + w, :],
                    "d_w2", [bPW], [bPW, bSLOT[0], bSLOT[1], bSBLK16])
            else:
                DMA("gpsimd", PW[0:w, k * D:(k + 1) * D], wout_h[k * 128:k * 128 + w, :],
                    "d_w2", [bPW], [bPW])
        emit_wq_load(0)
        for tb in range(4):
            tlo = tb * 512
            for k in range(NKH):
                wqj = tb * NKH + k
                if wqj < 64:
                    if wqj + 1 < 64:
                        emit_wq_load(wqj + 1)
                    emit_wq_quant(wqj)
                if wqj == 17:
                    emit_w1qT(0)
                elif wqj == 34:
                    emit_w1qT(1)
                elif 64 <= wqj < 64 + (NJ + 1) // 2:
                    emit_w2qT((wqj - 64) * 2)
                    emit_w2qT((wqj - 64) * 2 + 1)
                w = HID_T[k]
                for half, hofs in ((0, 0), (1, HID)):
                    DMA("gpsimd", WST[:, half * 1024:half * 1024 + 8 * w].rearrange("p (kd n) -> p kd n", kd=ND),
                        win_h[:, hofs + k * 128:hofs + k * 128 + w].rearrange("(kd p) n -> p kd n", p=128),
                        "d_w", [bWST], [bWST])
                for half, pbank in ((0, 0), (1, 1)):
                    for kk in range(ND):
                        MM(ps[pbank][0:w, :], WST[:, half * 1024 + kk * w:half * 1024 + (kk + 1) * w],
                           xnT(kk, tlo, 512), start=(kk == 0), stop=(kk == ND - 1),
                           ins=[bWST, bTFM], outs=[bPS[pbank]], inc=(kk == ND - 1))
                ACT(SC1[0:w, 0:512], ps[0][0:w, :], AF.Identity, [bPS[0], bBIV], [bSC1], bias=BIV[0:w, k:k + 1])
                ACT(SC2[0:w, 0:512], ps[1][0:w, :], AF.Silu, [bPS[1], bBIV], [bSC2], bias=BIV[0:w, NKH + k:NKH + k + 1])
                TT("vector", PB[0:w, k * 512:k * 512 + 512], SC1[0:w, 0:512], SC2[0:w, 0:512],
                   OP.mult, [bSC1, bSC2], [bPB])
            for mt in range(4):
                t_glob = tb * 4 + mt
                for nn in range(2):
                    pb_ = bPS[2 + (mt * 2 + nn) % 2]
                    psn = ps[2 + (mt * 2 + nn) % 2]
                    for k in range(NKH):
                        w = HID_T[k]
                        MM(psn[:, :], PB[0:w, k * 512 + mt * 128:k * 512 + mt * 128 + 128],
                           PW[0:w, k * D + nn * 512:k * D + nn * 512 + 512],
                           start=(k == 0), stop=False, ins=[bPB, bPW], outs=[pb_], inc=False)
                    MM(psn[:, :], ONES[0:1, 0:128], ROWB[0:1, nn * 512:nn * 512 + 512],
                       start=False, stop=True, ins=[bONES, bROWB], outs=[pb_], inc=True)
                    TT("vector", R0[:, t_glob * D + nn * 512:t_glob * D + nn * 512 + 512],
                       psn[:, :], R0[:, t_glob * D + nn * 512:t_glob * D + nn * 512 + 512],
                       OP.add, [pb_, bR0[t_glob]], [bR0[t_glob]])


        stats_batch(lambda t: (R0[:, t * D:(t + 1) * D], [bR0[t]]), NT, EPS1)
        for t in range(NT):
            dst, bdst = (SC2, bSC2) if t % 2 == 0 else (SC3, bSC3)
            xsrc, bxsrc = (SC4, bSC4) if t % 2 == 0 else (SC1, bSC1)
            DMA("gpsimd", xsrc[:, :], x_h[t * 128:(t + 1) * 128, :], "d_x", [bxsrc], [bxsrc])
            TS2("vector", dst[:, :], R0[:, t * D:(t + 1) * D], VST[:, t:t + 1], VST[:, 16 + t:17 + t],
                OP.subtract, OP.mult, [bR0[t], bVST], [bdst])
            TT("vector", dst[:, :], dst[:, :], xsrc[:, :], OP.add, [bdst, bxsrc], [bdst])
            DMA("sync", x1_d[t * 128:(t + 1) * 128, :], dst[:, :], "d_b", [bdst], [bx1_d])
            ACT(TRASH[:, :D], dst[:, :], AF.Identity, [bdst], [bTRASH, bVST], accum=VST[:, 32 + t:33 + t])
            ACT(TRASH[:, :D], dst[:, :], AF.Square, [bdst], [bTRASH, bVST], accum=VST[:, 48 + t:49 + t])
        TS1("vector", VST[:, 32:48], VST[:, 32:48], 1.0 / D, OP.mult, [bVST], [bVST])
        TS1("vector", VST[:, 48:64], VST[:, 48:64], 1.0 / D, OP.mult, [bVST], [bVST])
        TT("vector", VST[:, 0:16], VST[:, 32:48], VST[:, 32:48], OP.mult, [bVST], [bVST])
        TT("vector", VST[:, 48:64], VST[:, 48:64], VST[:, 0:16], OP.subtract, [bVST], [bVST])
        TS1("vector", VST[:, 48:64], VST[:, 48:64], EPS1, OP.add, [bVST], [bVST])
        ACT(VST[:, 0:16], VST[:, 48:64], AF.Sqrt, [bVST], [bVST])
        RCP(VST[:, 48:64], VST[:, 0:16], [bVST], [bVST])
        for t in range(NT):
            xsrc, bxsrc = (SC4, bSC4) if t % 2 == 0 else (SC1, bSC1)
            DMA("gpsimd", xsrc[:, :], x1_d[t * 128:(t + 1) * 128, :], "d_x", [bx1_d, bxsrc], [bxsrc])
            dst, bdst = (SC2, bSC2) if t % 2 == 0 else (SC3, bSC3)
            TS2("vector", dst[:, :], xsrc[:, :], VST[:, 32 + t:33 + t], VST[:, 48 + t:49 + t],
                OP.subtract, OP.mult, [bxsrc, bVST], [bdst])
            if "xf" in dbg:
                DMA("gpsimd", dbg_h["xf"][t * 128:(t + 1) * 128, :], dst[:, :], "d_g",
                    [bdst], [B_("xfdbg")])
            qx(dst[:, :], dst[:, :], 1024, fl_wide, "43bf", [bdst], [bdst])
            DMA("sync", xq_d[t * 128:(t + 1) * 128, :], dst[:, :], "d_b", [bdst], [bxq_d])

        bout_dma = B_("out")
        bhq = [B_("hq0"), B_("hq1")]
        bxqT = [B_("xqT0"), B_("xqT1")]

        def h_phase(tb):
            par = tb % 2
            hqo = par * 8192
            tlo = tb * FTB
            for cc in range(ND):
                DMA("sync", TFM[:, cc * S + par * FTB:cc * S + par * FTB + FTB],
                    xq_d[tlo:tlo + FTB, cc * 128:(cc + 1) * 128],
                    "d_t", [bxq_d, bxqT[par]], [bxqT[par]], transpose=True)
            for js in range(NJ):
                if js % 2 == 0:
                    wbuf, bwbuf, wch = WST[:, 0:1024], B_("wstA"), "d_fa"
                else:
                    wbuf, bwbuf, wch = PB[:, 16384:17408], B_("pbw1"), "d_fb"
                extra_outs = [bwbuf] if not (tb == 0 and js < 2) else [bwbuf, bPB if js == 1 else bWST]
                DMA("sync", wbuf.rearrange("p (kd n) -> p kd n", kd=ND),
                    w1qT_d[:, js * 128:(js + 1) * 128].rearrange("(kd p) n -> p kd n", p=128),
                    wch, [bw1qT_d], extra_outs)
                pb_ = bPS[4 + js % 2]
                psn = ps[4 + js % 2]
                for kk in range(ND):
                    MM(psn[:, 0:FTB], wbuf[:, kk * 128:(kk + 1) * 128],
                       TFM[:, kk * S + par * FTB:kk * S + par * FTB + FTB],
                       start=(kk == 0), stop=(kk == ND - 1), ins=[bwbuf, bxqT[par]],
                       outs=[pb_], inc=(kk == ND - 1))
                ACT(SCH1[:, (js % 4) * FTB:(js % 4 + 1) * FTB], psn[:, 0:FTB], AF.Relu,
                    [pb_, bQB1], [bSH1], bias=QB1[:, js:js + 1])
                if js % 4 == 3:
                    qpair_h(PB[:, hqo + (js - 3) * FTB:hqo + (js + 1) * FTB], SCH1[:, 0:1024],
                            1024, [bSH1], [bhq[par]])
                    if "hq" in dbg:
                        DMA("gpsimd",
                            dbg_h["hq"][(js - 3) * 128:(js + 1) * 128, tlo:tlo + FTB].rearrange("(a p) n -> p a n", p=128),
                            PB[:, hqo + (js - 3) * FTB:hqo + (js + 1) * FTB].rearrange("p (a n) -> p a n", a=4),
                            "d_g", [bhq[par]], [B_("dbgout")])

        def y_phase(tb):
            par = tb % 2
            hqo = par * 8192
            for js in range(NJ):
                if js % 2 == 0:
                    ybuf, bybuf, ych = WST[:, 1024:2048], B_("wstB"), "d_fc"
                else:
                    ybuf, bybuf, ych = PB[:, 17408:18432], B_("pbw2"), "d_fd"
                DMA("sync", ybuf, w2qT_d[js * 128:(js + 1) * 128, :], ych, [bw2qT_d], [bybuf])
                for sub in range(2):
                    for ih in range(2):
                        bidx = sub * 2 + ih
                        MM(ps[bidx][:, :], PB[:, hqo + js * FTB + sub * 128:hqo + js * FTB + sub * 128 + 128],
                           ybuf[:, ih * 512:ih * 512 + 512],
                           start=(js == 0), stop=False, ins=[bhq[par], bybuf], outs=[bPS[bidx]],
                           inc=(sub == 1 and ih == 1))
            mm_last = None
            for sub in range(2):
                for ih in range(2):
                    bidx = sub * 2 + ih
                    mm_last = MM(ps[bidx][:, :], ONES[0:1, 0:128],
                                 ROWB[0:1, 1024 + ih * 512:1024 + ih * 512 + 512],
                                 start=False, stop=True, ins=[bONES, bROWB], outs=[bPS[bidx]], inc=True)
            mm_bar = em.do("tensor", lambda e, c: e.nop(), extra=[mm_last], inc=True)
            for b_ in range(4):
                bPS[b_].w = mm_bar

        def tail_phase(tb):
            for sub in range(2):
                t_glob = tb * 2 + sub
                for ih in range(2):
                    bidx = sub * 2 + ih
                    o = ih * 512
                    ACT(SYE[:, o:o + 512], ps[bidx][:, :], AF.Copy, [bPS[bidx]], [bSYE])
                    qy510(SYE[:, o:o + 512], SYE[:, o:o + 512], 512, [bSYE], [bSYE])
                DMA("sync", SCH2[:, :], mask_h[t_glob * 128:(t_glob + 1) * 128, 0:512], "d_m", [bSH2], [bSH2])
                DMA("sync", SCH3[:, :], mask_h[t_glob * 128:(t_glob + 1) * 128, 512:1024], "d_m", [bSH3], [bSH3])
                TT("vector", SYE[:, 0:512], SYE[:, 0:512], SCH2[:, :], OP.mult, [bSYE, bSH2], [bSYE])
                TT("vector", SYE[:, 512:1024], SYE[:, 512:1024], SCH3[:, :], OP.mult, [bSYE, bSH3], [bSYE])
                if "ym" in dbg:
                    DMA("gpsimd", dbg_h["ym"][t_glob * 128:(t_glob + 1) * 128, :], SYE[:, :],
                        "d_g", [bSYE], [B_("dbgout")])
                ACT(TRASH[:, :D], SYE[:, :], AF.Identity, [bSYE], [bTRASH, bVST], accum=VST[:, 0:1])
                ACT(TRASH[:, :D], SYE[:, :], AF.Square, [bSYE], [bTRASH, bVST], accum=VST[:, 1:2])
                TS1("vector", VST[:, 0:1], VST[:, 0:1], 1.0 / D, OP.mult, [bVST], [bVST])
                TS1("vector", VST[:, 1:2], VST[:, 1:2], 1.0 / D, OP.mult, [bVST], [bVST])
                TT("vector", VST[:, 2:3], VST[:, 0:1], VST[:, 0:1], OP.mult, [bVST], [bVST])
                TT("vector", VST[:, 1:2], VST[:, 1:2], VST[:, 2:3], OP.subtract, [bVST], [bVST])
                TS1("vector", VST[:, 1:2], VST[:, 1:2], EPS1, OP.add, [bVST], [bVST])
                ACT(VST[:, 2:3], VST[:, 1:2], AF.Sqrt, [bVST], [bVST])
                RCP(VST[:, 1:2], VST[:, 2:3], [bVST], [bVST])
                DMA("gpsimd", SC1[:, :], x1_d[t_glob * 128:(t_glob + 1) * 128, :], "d_x",
                    [bx1_d, bSC1], [bSC1])
                TS2("vector", SC2[:, :], SYE[:, :], VST[:, 0:1], VST[:, 1:2],
                    OP.subtract, OP.mult, [bSYE, bVST], [bSC2])
                TT("vector", SC0[:, :], SC2[:, :], SC1[:, :], OP.add, [bSC2, bSC1], [bSC0])
                DMA("sync", out_h[t_glob * 128:(t_glob + 1) * 128, :], SC0[:, :], "d_o", [bSC0], [bout_dma])

        h_phase(0)
        for tb in range(NFTB):
            y_phase(tb)
            if tb + 1 < NFTB:
                h_phase(tb + 1)
            tail_phase(tb)

        for n in dbg:
            src = {"xn": xn_d, "y": y_d, "xm": xm_d, "xq": xq_d, "x1": x1_d}.get(n)
            if src is None:
                continue
            DMA("gpsimd", dbg_h[n], src, "d_g", [B_(n + "_d")], [B_("dbgout")])

        em.do("tensor", lambda e, c: e.nop(), extra=[(ESEM["tensor"], em.cnt["tensor"])], inc=True)
        em.do("vector", lambda e, c: e.nop(), extra=[(ESEM["vector"], em.cnt["vector"])], inc=True)
        em.do("scalar", lambda e, c: e.nop(), extra=[(ESEM["scalar"], em.cnt["scalar"])], inc=True)
        em.do("gpsimd", lambda e, c: e.nop(), extra=[(ESEM["gpsimd"], em.cnt["gpsimd"])], inc=True)
        fin_extra = [("d_o", em.dcnt["d_o"])]
        if em.dcnt["d_g"]:
            fin_extra.append(("d_g", em.dcnt["d_g"]))
        em.do("sync", lambda e, c: e.nop(), extra=fin_extra, inc=False)

        ctx = {"waited": {}, "sem": sems, "eng": {}}

        def run_queue(name, handle):
            ctx["eng"][name] = handle
            for f in em.q[name]:
                f(ctx)

        @block.sync
        def _(e):
            run_queue("sync", e)

        @block.vector
        def _(e):
            run_queue("vector", e)

        @block.scalar
        def _(e):
            run_queue("scalar", e)

        @block.gpsimd
        def _(e):
            run_queue("gpsimd", e)

        @block.tensor
        def _(e):
            run_queue("tensor", e)

    return nc


def _host_constants():
    i = np.arange(128, dtype=np.float32)
    trel = i[None, :] - i[:, None]
    trel = np.where(trel >= 0, trel, np.float32(1e9)).astype(np.float32)
    tip1 = np.broadcast_to(i[None, :] + 1.0, (128, 128)).astype(np.float32)
    trev = (127.0 - i)[:, None].astype(np.float32)
    ident = np.eye(128, dtype=np.float32)
    return trel, tip1, trev, ident


def _dropout_mask():
    import os
    import jax
    ctx = (jax.experimental.enable_x64() if os.environ.get("BASS_SIM_FP64") == "1"
           else jax.experimental.disable_x64())
    with jax.default_device(jax.devices("cpu")[0]), ctx:
        keep = jax.random.bernoulli(jax.random.key(42, impl="threefry2x32"), 1.0 - 0.2, (8, S, D))
        keep = np.asarray(keep)
    return np.where(keep, np.float16(1.25), np.float16(0.0)).astype(np.float16)


def kernel(L0_attn_x, L0_attn_ln1_w, L0_attn_ln1_b, L0_attn_wq, L0_attn_wk, L0_attn_wv,
           L0_attn_wg, L0_attn_wo, L0_attn_ln2_w, L0_attn_ln2_b, L0_attn_w_in, L0_attn_b_in,
           L0_attn_w_out, L0_attn_b_out, L0_ffn_w1, L0_ffn_b1, L0_ffn_w2, L0_ffn_b2):
    x = np.ascontiguousarray(np.asarray(L0_attn_x, dtype=np.float32))
    trel, tip1, trev, ident = _host_constants()
    mask = _dropout_mask()
    common = {
        "wq": np.ascontiguousarray(L0_attn_wq, np.float32),
        "wk": np.ascontiguousarray(L0_attn_wk, np.float32),
        "wv": np.ascontiguousarray(L0_attn_wv, np.float32),
        "wg": np.ascontiguousarray(L0_attn_wg, np.float32),
        "wo": np.ascontiguousarray(L0_attn_wo, np.float32),
        "ln1w": np.ascontiguousarray(L0_attn_ln1_w, np.float32),
        "ln1b": np.ascontiguousarray(L0_attn_ln1_b, np.float32),
        "ln2w": np.ascontiguousarray(L0_attn_ln2_w, np.float32),
        "ln2b": np.ascontiguousarray(L0_attn_ln2_b, np.float32),
        "win": np.ascontiguousarray(L0_attn_w_in, np.float32),
        "bin": np.ascontiguousarray(L0_attn_b_in, np.float32),
        "wout": np.ascontiguousarray(L0_attn_w_out, np.float32),
        "bout": np.ascontiguousarray(L0_attn_b_out, np.float32),
        "w1": np.ascontiguousarray(L0_ffn_w1, np.float32),
        "b1": np.ascontiguousarray(L0_ffn_b1, np.float32),
        "w2": np.ascontiguousarray(L0_ffn_w2, np.float32),
        "b2": np.ascontiguousarray(L0_ffn_b2, np.float32),
        "trel": trel, "tip1": tip1, "trev": trev, "ident": ident,
    }
    nc = build_nc()
    in_maps = []
    for core in range(8):
        m = dict(common)
        m["x"] = np.ascontiguousarray(x[core])
        m["mask"] = np.ascontiguousarray(mask[core])
        in_maps.append(m)
    res = run_bass_kernel_spmd(nc, in_maps, core_ids=list(range(8)))
    return np.stack([r["out"] for r in res.results], axis=0)


# version 16 (final)
# speedup vs baseline: 1.5171x; 1.0121x over previous
import contextlib
import math
import sys

import numpy as np
import ml_dtypes

import concourse.bass as bass
import concourse.mybir as mybir
from concourse.bass_utils import run_bass_kernel_spmd

F32 = mybir.dt.float32
BF16 = mybir.dt.bfloat16
F16 = mybir.dt.float16
I16 = mybir.dt.int16
I8 = mybir.dt.int8
F8E4 = mybir.dt.float8e4
M64C = 6755399441055744.0
OP = mybir.AluOpType
AF = mybir.ActivationFunctionType
BF = ml_dtypes.bfloat16
AX = mybir.AxisListType.X

S, D = 2048, 1024
NT, ND = S // 128, D // 128
NH, NPAIR = 16, 8
BLK, NBLK = 512, 4
HID = 2730
HID_T = [128] * 21 + [42]
NKH = len(HID_T)
H4 = 4096
NJ = H4 // 128
FTB, NFTB = 256, 8
EPS1, EPS2 = 1e-5, 1e-6

ENGS = ("sync", "vector", "scalar", "gpsimd", "tensor")
ESEM = {e: "E_" + e for e in ENGS}
DMACH = ("d_x", "d_w", "d_w2", "d_t", "d_b", "d_m", "d_o", "d_c", "d_q", "d_g", "d_fa", "d_fb", "d_fc", "d_fd", "d_wq")


class Buf:
    __slots__ = ("w", "r")

    def __init__(self):
        self.w = None
        self.r = []


class Em:
    def __init__(self):
        self.q = {e: [] for e in ENGS}
        self.cnt = {e: 0 for e in ENGS}
        self.dcnt = {ch: 0 for ch in DMACH}

    def _wait(self, eng, sem_name, val):
        def f(ctx, eng=eng, sem_name=sem_name, val=val):
            key = (eng, sem_name)
            if ctx["waited"].get(key, -1) >= val:
                return
            ctx["waited"][key] = val
            ctx["eng"][eng].wait_ge(ctx["sem"][sem_name], val)

        self.q[eng].append(f)

    def do(self, eng, build, ins=(), outs=(), ch=None, extra=(), inc=True):
        fr = sys._getframe(1)
        site = f"{fr.f_code.co_name}:{fr.f_lineno}"
        fr2 = fr.f_back
        if fr2 is not None:
            site += f" < {fr2.f_code.co_name}:{fr2.f_lineno}"
        _build0 = build

        def build(e, c, _b=_build0, _site=site):
            try:
                return _b(e, c)
            except Exception as ex:
                raise RuntimeError(f"emit failed at {_site}: {ex}") from ex

        waits = []
        for b in ins:
            if b.w is not None:
                waits.append(b.w)
        for b in outs:
            if b.w is not None:
                waits.append(b.w)
            waits.extend(b.r)
        waits.extend(t for t in extra if t is not None)
        seen = {}
        for s_, v_ in waits:
            if s_ in self.dcnt:
                v_ = self.dcnt[s_]
            if seen.get(s_, -1) < v_:
                seen[s_] = v_
        for s_, v_ in sorted(seen.items()):
            if s_ == ESEM[eng] and v_ > self.cnt[eng]:
                continue
            self._wait(eng, s_, v_)
        if ch is not None:
            if self.dcnt[ch] > 0:
                self._wait(eng, ch, self.dcnt[ch])
            self.dcnt[ch] += 16
            val = self.dcnt[ch]

            def f(ctx, build=build, eng=eng, ch=ch):
                build(ctx["eng"][eng], ctx).then_inc(ctx["sem"][ch], 16)

            self.q[eng].append(f)
            tok = (ch, val)
        elif inc:
            self.cnt[eng] += 1
            val = self.cnt[eng]

            def f(ctx, build=build, eng=eng, sem_name=ESEM[eng]):
                build(ctx["eng"][eng], ctx).then_inc(ctx["sem"][sem_name], 1)

            self.q[eng].append(f)
            tok = (ESEM[eng], val)
        else:
            def f(ctx, build=build, eng=eng):
                build(ctx["eng"][eng], ctx)

            self.q[eng].append(f)
            tok = None
        if tok is None and ch is None:
            tok_eff = (ESEM[eng], self.cnt[eng] + 1)
        else:
            tok_eff = tok
        if tok_eff is not None:
            for b in ins:
                b.r.append(tok_eff)
            for b in outs:
                b.w = tok_eff
                b.r = []
        return tok


def _pc(h):
    g = 1.0 - 2.0 ** (-5.0 - h)
    lng = math.log(g)
    hi = np.float32(lng)
    lo = np.float32(lng - float(hi))
    return float(hi), float(lo)


def build_nc(dbg=()):
    nc = bass.Bass("TRN2", target_bir_lowering=False, debug=False)
    em = Em()

    I = lambda n, sh, dt_: nc.dram_tensor(n, sh, dt_, kind="ExternalInput").ap()
    x_h = I("x", [S, D], F32)
    wq_h = I("wq", [D, D], F32)
    wk_h = I("wk", [D, D], F32)
    wv_h = I("wv", [D, D], F32)
    wg_h = I("wg", [D, D], F32)
    wo_h = I("wo", [D, D], F32)
    ln_h = {n: I(n, [D], F32) for n in ("ln1w", "ln1b", "ln2w", "ln2b")}
    win_h = I("win", [D, 2 * HID], F32)
    bin_h = I("bin", [2 * HID], F32)
    wout_h = I("wout", [HID, D], F32)
    bout_h = I("bout", [D], F32)
    w1_h = I("w1", [H4, D], F32)
    b1_h = I("b1", [H4], F32)
    w2_h = I("w2", [D, H4], F32)
    b2_h = I("b2", [D], F32)
    mask_h = I("mask", [S, D], F16)
    trel_h = I("trel", [128, 128], F32)
    tip1_h = I("tip1", [128, 128], F32)
    trev_h = I("trev", [128, 1], F32)
    ident_h = I("ident", [128, 128], F32)
    out_h = nc.dram_tensor("out", [S, D], F32, kind="ExternalOutput").ap()
    _dbg_shapes = {"hq": [H4, S], "ym": [S, D]}
    dbg_h = {n: nc.dram_tensor("dbg_" + n, _dbg_shapes.get(n, [S, D]), F32, kind="ExternalOutput").ap() for n in dbg}

    T = lambda n, sh, dt_: nc.dram_tensor(n, sh, dt_, kind="Internal").ap()
    xn_d = T("xn_d", [S, D], BF16)
    y_d = T("y_d", [S, D], BF16)
    xm_d = T("xm_d", [S, D], BF16)
    xq_d = T("xq_d", [S, D], BF16)
    x1_d = T("x1_d", [S, D], BF16)
    w1q_d = T("w1q_d", [H4, D], BF16)
    w1qT_d = T("w1qT_d", [D, H4], BF16)
    w2q_d = T("w2q_d", [D, H4], BF16)
    w2qT_d = T("w2qT_d", [H4, D], BF16)

    es = contextlib.ExitStack()
    with es:
        SBt = lambda n, sh, dt_: es.enter_context(nc.sbuf_tensor(n, sh, dt_))
        TFM = SBt("TFM", [128, ND * S], BF16)
        R0 = SBt("R0", [128, NT * D], BF16)
        PW = SBt("PW", [128, 22528], BF16)
        PB = SBt("PB", [128, 20480], BF16)
        WST = SBt("WST", [128, 2048], BF16)
        SC0 = SBt("SC0", [128, 1024], F32)
        SCF = SBt("SCF", [128, 1024], F32)
        SC1 = SBt("SC1", [128, 1024], BF16)
        SC2 = SBt("SC2", [128, 1024], BF16)
        SC3 = SBt("SC3", [128, 1024], BF16)
        SC4 = SBt("SC4", [128, 1024], BF16)
        SCI = SBt("SCI", [128, 1024], I16)
        SCI2 = SBt("SCI2", [128, 1024], I16)
        SCH1 = SBt("SCH1", [128, 1024], F16)
        SCH2 = SBt("SCH2", [128, 512], F16)
        SCH3 = SBt("SCH3", [128, 512], F16)
        SCH4 = SBt("SCH4", [128, 512], F16)
        SYE = SBt("SYE", [128, 1024], F16)
        QL16 = SBt("QL16", [128, 512], I16)
        GQ1 = SBt("GQ1", [128, 512], BF16)
        F8Q = SBt("F8Q", [128, 1024], F8E4)
        L8Q = SBt("L8Q", [128, 1024], I8)
        GQI = SBt("GQI", [128, 1024], I16)
        VST = SBt("VST", [128, 66], F32)
        TRASH = SBt("TRASH", [128, 1024], BF16)
        DKT = SBt("DKT", [128, NH * 128], BF16)
        GMT = SBt("GMT", [128, NPAIR * 128], BF16)
        KGV = SBt("KGV", [128, NH], F32)
        GCV = SBt("GCV", [128, NPAIR], F32)
        SBLK = SBt("SBLK", [128, NPAIR * 128], F32)
        CST = SBt("CST", [128, 128], F32)
        IDT = SBt("IDT", [128, 128], F32)
        LNV = SBt("LNV", [128, 4 * ND], F32)
        BIV = SBt("BIV", [128, 2 * NKH], F32)
        QB1 = SBt("QB1", [128, NJ], F32)
        ROWS = SBt("ROWS", [1, 1024], F32)
        ROWB = SBt("ROWB", [1, 2048], BF16)
        ONES = SBt("ONES", [1, 512], BF16)
        ps = [es.enter_context(nc.psum_tensor(f"ps{i}", [128, 512], F32)) for i in range(8)]
        sems = {n: es.enter_context(nc.semaphore(n)) for n in list(ESEM.values()) + list(DMACH)}
        block = es.enter_context(nc.Block())

        _bufs = {}

        def B_(key):
            if key not in _bufs:
                _bufs[key] = Buf()
            return _bufs[key]

        bPS = [B_(("ps", i)) for i in range(8)]
        bSC0, bSCF, bSC1, bSC2, bSC3, bSC4 = (B_(x) for x in ("SC0", "SCF", "SC1", "SC2", "SC3", "SC4"))
        bSCI, bSCI2 = B_("SCI"), B_("SCI2")
        bSH1, bSH2, bSH3, bSH4, bSYE = B_("SCH1"), B_("SCH2"), B_("SCH3"), B_("SCH4"), B_("SYE")
        bVST, bTRASH, bCST, bROWS, bROWB = B_("VST"), B_("TRASH"), B_("CST"), B_("ROWS"), B_("ROWB")
        bWST, bTFM, bPW, bPB = B_("WST"), B_("TFM"), B_("PW"), B_("PB")
        bSBLK, bIDT, bONES = B_("SBLK"), B_("IDT"), B_("ONES")
        bLNV, bBIV, bQB1 = B_("LNV"), B_("BIV"), B_("QB1")
        bDKT, bGMT, bKGV, bGCV = B_("DKT"), B_("GMT"), B_("KGV"), B_("GCV")
        bQX, bGQ1 = B_("qx"), B_("gq1")
        bR0 = [B_(("R0", t)) for t in range(NT)]
        bSLOT = [B_(("slot", s)) for s in range(2)]

        def TT(eng, out, a, b, op, ins, outs, extra=()):
            return em.do(eng, lambda e, c: e.tensor_tensor(out, a, b, op=op), ins, outs, extra=extra)

        def TS1(eng, out, a, s1, op, ins, outs, extra=()):
            return em.do(eng, lambda e, c: e.tensor_single_scalar(out, a, s1, op=op), ins, outs, extra=extra)

        def TS2(eng, out, a, s1, s2, op0, op1, ins, outs, extra=()):
            if eng == "gpsimd":
                t = em.do(eng, lambda e, c: e.tensor_single_scalar(out, a, s1, op=op0), ins, outs, extra=extra)
                return em.do(eng, lambda e, c: e.tensor_single_scalar(out, out, s2, op=op1), list(outs), list(outs))
            return em.do(eng, lambda e, c: e.tensor_scalar(out, a, s1, s2, op0=op0, op1=op1), ins, outs, extra=extra)

        def STT(out, a, s, b, op0, op1, ins, outs, extra=()):
            return em.do("vector", lambda e, c: e.scalar_tensor_tensor(out, a, s, b, op0=op0, op1=op1), ins, outs, extra=extra)

        def CP(eng, out, a, ins, outs, extra=()):
            return em.do(eng, lambda e, c: e.tensor_copy(out, a), ins, outs, extra=extra)

        def ACT(out, a, func, ins, outs, bias=0.0, scale=1.0, accum=None, extra=()):
            return em.do("scalar", lambda e, c: e.activation(out, a, func, bias=bias, scale=scale, accum_out=accum), ins, outs, extra=extra)

        def RED(out, a, op, ins, outs, extra=()):
            return em.do("vector", lambda e, c: e.tensor_reduce(out, a, axis=AX, op=op), ins, outs, extra=extra)

        def MS(eng, ap, v, ins, outs, extra=()):
            return em.do(eng, lambda e, c: e.memset(ap, v), ins, outs, extra=extra)

        def MM(po, lhsT, rhs, start, stop, ins, outs, extra=(), inc=True, tp=None):
            return em.do("tensor", lambda e, c: e.matmul(po, lhsT, rhs, start=start, stop=stop, tile_position=tp), ins, outs, extra=extra, inc=True)

        def TRANS(po, in_, ins, outs, extra=(), P=1):
            return em.do("tensor", lambda e, c: e.transpose(po, in_, IDT[0:P, 0:P]), list(ins) + [bIDT], outs, extra=extra, inc=True)

        def DMA(eng, out, in_, ch, ins, outs, extra=(), transpose=False):
            if transpose:
                return em.do(eng, lambda e, c: e.dma_start(out=out, in_=in_, transpose=True), ins, outs, ch=ch, extra=extra)
            return em.do(eng, lambda e, c: e.dma_start(out=out, in_=in_), ins, outs, ch=ch, extra=extra)

        def RCP(out, a, ins, outs, extra=()):
            return em.do("vector", lambda e, c: e.reciprocal(out, a), ins, outs, extra=extra)

        def qx(dst, src, W, fl, kind, ins, outs, n=128):
            (Fr, br), (Fg, bg), (Fc, bc), (Fl, bl), (Fn, bn), (Fx, bx), (Fb, bb) = \
                [(a[0:n, 0:W], b) for a, b in fl]
            ti, t2, pm = SCI[0:n, :W], SCI2[0:n, :W], GQI[0:n, :W]
            f8, l8 = F8Q[0:n, :W], L8Q[0:n, :W]
            V = "vector"
            if kind == "43bf":
                TS1(V, ti, src.bitcast(I16), 0x7F80, OP.bitwise_and, list(ins), [bQX])
                TS2(V, t2, ti, 121 << 7, 384, OP.max, OP.subtract, [bQX], [bQX])
                TS1(V, Fn, ti, 15616, OP.is_ge, [bQX], [bn])
                TS2(V, pm, t2, -1, 32512, OP.mult, OP.add, [bQX], [bQX])
                CP(V, f8, src, list(ins), [bQX])
            else:
                TS1(V, ti, src.bitcast(I16), 0x7C00, OP.bitwise_and, list(ins), [bQX])
                clampv = (9 << 10) if kind == "43f16" else (1 << 10)
                TS1(V, ti, ti, clampv, OP.max, [bQX], [bQX])
                TS1(V, Fn, ti, clampv + (1 << 10), OP.is_ge, [bQX], [bn])
                TS1(V, ti, ti, 3, OP.logical_shift_right, [bQX], [bQX])
                if kind == "43f16":
                    TS2(V, pm, ti, -1, (127 + 15 + 3) << 7, OP.mult, OP.add, [bQX], [bQX])
                    TS1(V, t2, ti, (127 - 15 - 3) << 7, OP.add, [bQX], [bQX])
                    CP(V, f8, src, list(ins), [bQX])
                else:
                    TS2(V, pm, ti, -1, (127 + 15 + 10) << 7, OP.mult, OP.add, [bQX], [bQX])
                    TS1(V, t2, ti, (127 - 15 - 10) << 7, OP.add, [bQX], [bQX])
                    CP(V, QL16[0:n, :W], src.bitcast(I16), list(ins), [bQX])
            TT(V, Fr, src, pm.bitcast(BF16), OP.mult, list(ins) + [bQX], [br])
            if kind == "510f16":
                CP(V, Fb, QL16[0:n, :W].bitcast(F16), [bQX], [bb])
            else:
                CP(V, l8, f8.bitcast(I8), [bQX], [bQX])
                CP(V, Fb, l8.bitcast(F8E4), [bQX], [bb])
            TT(V, Fg, Fr, Fr, OP.mult, [br], [bg])
            hi_t = 4194000.0 if kind == "510f16" else 255.9
            lo_t = 1048000.0 if kind == "510f16" else 63.9
            TS1(V, Fc, Fg, hi_t, OP.is_ge, [bg], [bc])
            TS1(V, Fl, Fg, lo_t, OP.is_le, [bg], [bl])
            TT(V, Fl, Fl, Fn, OP.mult, [bl, bn], [bl])
            STT(Fx, Fc, -0.5, Fl, OP.mult, OP.add, [bc, bl], [bx])
            STT(Fg, Fx, 1.0, Fr, OP.add, OP.mult, [bx, br], [bg])
            TS2(V, Fg, Fg, VST[0:n, 64:65], 0.0, OP.mult, OP.add, [bg, bVST], [bg])
            TS2(V, Fr, Fg, M64C, M64C, OP.add, OP.subtract, [bg], [br])
            STT(Fx, Fl, -0.5, Fc, OP.mult, OP.add, [bl, bc], [bx])
            TT(V, Fn, Fr, t2.bitcast(BF16), OP.mult, [br, bQX], [bn])
            STT(Fn, Fx, 1.0, Fn, OP.add, OP.mult, [bx, bn], [bn])
            TS1(V, pm, t2, 6720, OP.add, [bQX], [bQX])
            TT(V, Fl, Fb, pm.bitcast(BF16), OP.add, [bb, bQX], [bl])
            TT(V, Fl, Fl, pm.bitcast(BF16), OP.subtract, [bl, bQX], [bl])
            TT(V, Fb, Fn, Fb, OP.add, [bn, bb], [bb])
            return TT(V, dst, Fb, Fl, OP.subtract, [bb, bl], list(outs))

        MS("vector", VST[:, 64:65], 2.0 ** -41, [], [bVST])
        TS2("vector", VST[:, 64:65], VST[:, 64:65], -1.0, 1.0, OP.mult, OP.add, [bVST], [bVST])

        fl_wide = [(WST[:, 0:1024], bWST), (WST[:, 1024:2048], bWST), (TRASH, bTRASH),
                   (SC0, bSC0), (SCF, bSCF), (PB[:, 0:1024], bPB), (PB[:, 1024:2048], bPB)]
        fl_ffn = [(GQ1, bGQ1), (TRASH, bTRASH), (SC1, bSC1), (SC2, bSC2),
                  (SC3, bSC3), (SC4, bSC4), (SCH2, bSH2)]
        fl_tail = [(GQ1, bGQ1), (TRASH, bTRASH), (SC1, bSC1), (SC2, bSC2),
                   (SC3, bSC3), (SC4, bSC4), (SCH3, bSH3)]
        fl_wq = [(PB[:, 2048 + i * 1024:3072 + i * 1024], bPB) for i in range(7)]

        def qpair_h(dst, src, W, ins, outs):
            ti, t2, pm = SCI[:, :W], SCI2[:, :W], GQI[:, :W]
            f8, l8 = F8Q[:, :W], L8Q[:, :W]
            T1, bT1 = PB[:, 18432:18432 + W], bPB
            T2, bT2 = PB[:, 19456:19456 + W], bPB
            T3, bT3 = TRASH[:, :W], bTRASH
            Tc, bTc = SC1[:, :W], bSC1
            Tn, bTn = SC2[:, :W], bSC2
            Tx, bTx = SC3[:, :W], bSC3
            Ty, bTy = SC4[:, :W], bSC4
            V = "vector"
            TS1(V, ti, src.bitcast(I16), 0x7C00, OP.bitwise_and, list(ins), [bQX])
            TS1(V, ti, ti, 1 << 10, OP.max, [bQX], [bQX])
            TS1(V, Tn, ti, 2 << 10, OP.is_ge, [bQX], [bTn])
            TS1(V, ti, ti, 3, OP.logical_shift_right, [bQX], [bQX])
            TS2(V, pm, ti, -1, 19456, OP.mult, OP.add, [bQX], [bQX])
            TS1(V, t2, ti, 13056, OP.add, [bQX], [bQX])
            TT(V, T1, src, pm.bitcast(BF16), OP.mult, list(ins) + [bQX], [bT1])
            TS1(V, Tc, T1, 2047.9, OP.is_ge, [bT1], [bTc])
            TS1(V, Tx, T1, 1023.9, OP.is_ge, [bT1], [bTx])
            TT(V, Tc, Tc, Tn, OP.mult, [bTc, bTn], [bTc])
            TS2(V, Tx, Tx, -1.0, 1.0, OP.mult, OP.add, [bTx], [bTx])
            TT(V, Tx, Tx, Tn, OP.mult, [bTx, bTn], [bTx])
            STT(Ty, Tc, -0.5, Tx, OP.mult, OP.add, [bTc, bTx], [bTy])
            STT(T1, Ty, 1.0, T1, OP.add, OP.mult, [bTy, bT1], [bT1])
            TS2(V, T1, T1, VST[:, 64:65], 0.0, OP.mult, OP.add, [bT1, bVST], [bT1])
            TS2(V, T1, T1, M64C, M64C, OP.add, OP.subtract, [bT1], [bT1])
            TT(V, T1, T1, t2.bitcast(BF16), OP.mult, [bT1, bQX], [bT1])
            STT(Tx, Tx, -0.5, Tc, OP.mult, OP.add, [bTx, bTc], [bTx])
            STT(T1, Tx, 1.0, T1, OP.add, OP.mult, [bTx, bT1], [bT1])
            TS2(V, t2, t2, 896, 15104, OP.add, OP.max, [bQX], [bQX])
            TS2(V, pm, t2, -1, 32512, OP.mult, OP.add, [bQX], [bQX])
            TT(V, T1, T1, pm.bitcast(BF16), OP.mult, [bT1, bQX], [bT1])
            STT(T1, Ty, 1.0, T1, OP.add, OP.mult, [bTy, bT1], [bT1])
            TS2(V, T1, T1, VST[:, 64:65], 0.0, OP.mult, OP.add, [bT1, bVST], [bT1])
            TS2(V, T1, T1, M64C, M64C, OP.add, OP.subtract, [bT1], [bT1])
            TT(V, T1, T1, t2.bitcast(BF16), OP.mult, [bT1, bQX], [bT1])
            STT(T1, Tx, 1.0, T1, OP.add, OP.mult, [bTx, bT1], [bT1])
            CP(V, f8, src, list(ins), [bQX])
            CP(V, l8, f8.bitcast(I8), [bQX], [bQX])
            CP(V, T2, l8.bitcast(F8E4), [bQX], [bT2])
            TS1(V, pm, t2, 6720, OP.add, [bQX], [bQX])
            TT(V, T3, T2, pm.bitcast(BF16), OP.add, [bT2, bQX], [bT3])
            TT(V, T3, T3, pm.bitcast(BF16), OP.subtract, [bT3, bQX], [bT3])
            TT(V, T2, T1, T2, OP.add, [bT1, bT2], [bT2])
            return TT(V, dst, T2, T3, OP.subtract, [bT2, bT3], list(outs))

        def qy510(dst, src, W, ins, outs):
            ti, t2, pm = SCI[:, :W], SCI2[:, :W], GQI[:, :W]
            T1, bT1 = TRASH[:, :W], bTRASH
            T2, bT2 = SCH4[:, :W], bSH4
            T3, bT3 = GQ1[:, :W], bGQ1
            Tc, bTc = SC1[:, :W], bSC1
            Tn, bTn = SC2[:, :W], bSC2
            Tx, bTx = SC3[:, :W], bSC3
            Ty, bTy = SC4[:, :W], bSC4
            V = "vector"
            TS1(V, ti, src.bitcast(I16), 0x7C00, OP.bitwise_and, list(ins), [bQX])
            TS1(V, ti, ti, 1 << 10, OP.max, [bQX], [bQX])
            TS1(V, Tn, ti, 2 << 10, OP.is_ge, [bQX], [bTn])
            TS1(V, ti, ti, 3, OP.logical_shift_right, [bQX], [bQX])
            TS2(V, pm, ti, -1, 19456, OP.mult, OP.add, [bQX], [bQX])
            TS1(V, t2, ti, 13056, OP.add, [bQX], [bQX])
            TT(V, T1, src, pm.bitcast(BF16), OP.mult, list(ins) + [bQX], [bT1])
            TT(V, Tc, T1, T1, OP.mult, [bT1], [bTc])
            TS1(V, Tx, Tc, 1048000.0, OP.is_ge, [bTc], [bTx])
            TS1(V, Tc, Tc, 4194000.0, OP.is_ge, [bTc], [bTc])
            TT(V, Tc, Tc, Tn, OP.mult, [bTc, bTn], [bTc])
            TS2(V, Tx, Tx, -1.0, 1.0, OP.mult, OP.add, [bTx], [bTx])
            TT(V, Tx, Tx, Tn, OP.mult, [bTx, bTn], [bTx])
            STT(Ty, Tc, -0.5, Tx, OP.mult, OP.add, [bTc, bTx], [bTy])
            STT(T1, Ty, 1.0, T1, OP.add, OP.mult, [bTy, bT1], [bT1])
            TS2(V, T1, T1, VST[:, 64:65], 0.0, OP.mult, OP.add, [bT1, bVST], [bT1])
            TS2(V, T1, T1, M64C, M64C, OP.add, OP.subtract, [bT1], [bT1])
            TT(V, T1, T1, t2.bitcast(BF16), OP.mult, [bT1, bQX], [bT1])
            STT(Tx, Tx, -0.5, Tc, OP.mult, OP.add, [bTx, bTc], [bTx])
            STT(T1, Tx, 1.0, T1, OP.add, OP.mult, [bTx, bT1], [bT1])
            CP(V, QL16[:, :W], src.bitcast(I16), list(ins), [bQX])
            CP(V, T2, QL16[:, :W].bitcast(F16), [bQX], [bT2])
            TS1(V, pm, t2, 6720, OP.add, [bQX], [bQX])
            TT(V, T3, T2, pm.bitcast(BF16), OP.add, [bT2, bQX], [bT3])
            TT(V, T3, T3, pm.bitcast(BF16), OP.subtract, [bT3, bQX], [bT3])
            TT(V, T2, T1, T2, OP.add, [bT1, bT2], [bT2])
            return TT(V, dst, T2, T3, OP.subtract, [bT2, bT3], list(outs))

        DMA("sync", IDT[:, :], ident_h, "d_c", [], [bIDT])
        MS("vector", ONES[:, :], 1.0, [], [bONES])
        DMA("sync", CST[:, :], trel_h, "d_c", [], [bCST])
        for h in range(NH):
            hi, lo = _pc(h)
            TS1("vector", SC0[:, :128], CST[:, :], hi, OP.mult, [bCST], [bSC0])
            TS1("vector", SCF[:, :128], CST[:, :], lo, OP.mult, [bCST], [bSCF])
            TT("vector", SC0[:, :128], SC0[:, :128], SCF[:, :128], OP.add, [bSC0, bSCF], [bSC0])
            ACT(DKT[:, h * 128:(h + 1) * 128], SC0[:, :128], AF.Exp, [bSC0], [bDKT])
        DMA("sync", CST[:, :], tip1_h, "d_c", [bCST], [bCST])
        for p in range(NPAIR):
            for half, h in ((0, 2 * p), (1, 2 * p + 1)):
                hi, lo = _pc(h)
                sl = slice(64 * half, 64 * half + 64)
                TS1("vector", SC0[sl, :128], CST[sl, :], hi, OP.mult, [bCST], [bSC0])
                TS1("vector", SCF[sl, :128], CST[sl, :], lo, OP.mult, [bCST], [bSCF])
            TT("vector", SC0[:, :128], SC0[:, :128], SCF[:, :128], OP.add, [bSC0, bSCF], [bSC0])
            ACT(GMT[:, p * 128:(p + 1) * 128], SC0[:, :128], AF.Exp, [bSC0], [bGMT])
        DMA("sync", CST[:, 0:1], trev_h, "d_c", [bCST], [bCST])
        for h in range(NH):
            hi, lo = _pc(h)
            TS1("vector", SC0[:, 0:1], CST[:, 0:1], hi, OP.mult, [bCST], [bSC0])
            TS1("vector", SCF[:, 0:1], CST[:, 0:1], lo, OP.mult, [bCST], [bSCF])
            TT("vector", SC0[:, 0:1], SC0[:, 0:1], SCF[:, 0:1], OP.add, [bSC0, bSCF], [bSC0])
            ACT(KGV[:, h:h + 1], SC0[:, 0:1], AF.Exp, [bSC0], [bKGV])
        for p in range(NPAIR):
            for half, h in ((0, 2 * p), (1, 2 * p + 1)):
                hi, lo = _pc(h)
                sl = slice(64 * half, 64 * half + 64)
                MS("vector", SC0[sl, 0:1], float(np.float32(hi)) * 128.0, [], [bSC0])
                MS("vector", SCF[sl, 0:1], float(np.float32(lo)) * 128.0, [], [bSCF])
            TT("vector", SC0[:, 0:1], SC0[:, 0:1], SCF[:, 0:1], OP.add, [bSC0, bSCF], [bSC0])
            ACT(GCV[:, p:p + 1], SC0[:, 0:1], AF.Exp, [bSC0], [bGCV])
        MS("vector", SBLK[:, :], 0.0, [], [bSBLK])
        bSBLK16 = B_("sblk16")
        MS("vector", PW[:, 16384:17408], 0.0, [], [bSBLK16])
        for v, name in enumerate(("ln1w", "ln1b", "ln2w", "ln2b")):
            DMA("sync", ROWS[0:1, 0:D], ln_h[name].rearrange("(o d) -> o d", o=1), "d_c", [bROWS], [bROWS])
            for cc in range(ND):
                TRANS(ps[7][:, 0:1], ROWS[0:1, cc * 128:cc * 128 + 128], [bROWS], [bPS[7]])
                ACT(LNV[:, v * ND + cc:v * ND + cc + 1], ps[7][:, 0:1], AF.Copy, [bPS[7]], [bLNV])
        LN1W, LN1B, LN2W, LN2B = 0, ND, 2 * ND, 3 * ND
        for half in range(2):
            for k in range(NKH):
                w = HID_T[k]
                DMA("sync", ROWS[0:1, 0:w], bin_h[half * HID + k * 128:half * HID + k * 128 + w].rearrange("(o d) -> o d", o=1),
                    "d_c", [bROWS], [bROWS])
                TRANS(ps[7][0:w, 0:1], ROWS[0:1, 0:w], [bROWS], [bPS[7]])
                ACT(BIV[0:w, half * NKH + k:half * NKH + k + 1], ps[7][0:w, 0:1], AF.Copy, [bPS[7]], [bBIV])
        DMA("sync", ROWS[0:1, 0:D], bout_h.rearrange("(o d) -> o d", o=1), "d_c", [bROWS], [bROWS])
        CP("vector", ROWB[0:1, 0:D], ROWS[0:1, 0:D], [bROWS], [bROWB])
        for cchunk in range(4):
            DMA("sync", ROWS[0:1, 0:1024], b1_h[cchunk * 1024:(cchunk + 1) * 1024].rearrange("(o d) -> o d", o=1),
                "d_c", [bROWS], [bROWS])
            CP("vector", SC4[0:1, 0:1024], ROWS[0:1, 0:1024], [bROWS], [bSC4])
            qx(SC4[0:1, 0:1024], SC4[0:1, 0:1024], 1024, fl_wide, "43bf", [bSC4], [bSC4], n=1)
            CP("vector", SCF[0:1, 0:1024], SC4[0:1, 0:1024], [bSC4], [bSCF])
            for k in range(8):
                j = cchunk * 8 + k
                TRANS(ps[7][:, 0:1], SCF[0:1, k * 128:(k + 1) * 128], [bSCF], [bPS[7]])
                ACT(QB1[:, j:j + 1], ps[7][:, 0:1], AF.Copy, [bPS[7]], [bQB1])
        DMA("sync", ROWS[0:1, 0:D], b2_h.rearrange("(o d) -> o d", o=1), "d_c", [bROWS], [bROWS])
        CP("vector", SC4[0:1, 0:1024], ROWS[0:1, 0:1024], [bROWS], [bSC4])
        qx(ROWB[0:1, 1024:2048], SC4[0:1, 0:1024], 1024, fl_wide, "43bf", [bSC4], [bROWB], n=1)

        def stats_batch(tiles_fn, n, eps, cmu=0, ce2=16, ct1=32, ct2=48):
            for t in range(n):
                ap, bufs = tiles_fn(t)
                ACT(TRASH[:, :D], ap, AF.Identity, bufs, [bTRASH, bVST], accum=VST[:, cmu + t:cmu + t + 1])
                ACT(TRASH[:, :D], ap, AF.Square, bufs, [bTRASH, bVST], accum=VST[:, ce2 + t:ce2 + t + 1])
            TS1("vector", VST[:, cmu:cmu + n], VST[:, cmu:cmu + n], 1.0 / D, OP.mult, [bVST], [bVST])
            TS1("vector", VST[:, ce2:ce2 + n], VST[:, ce2:ce2 + n], 1.0 / D, OP.mult, [bVST], [bVST])
            TT("vector", VST[:, ct1:ct1 + n], VST[:, cmu:cmu + n], VST[:, cmu:cmu + n], OP.mult, [bVST], [bVST])
            TT("vector", VST[:, ce2:ce2 + n], VST[:, ce2:ce2 + n], VST[:, ct1:ct1 + n], OP.subtract, [bVST], [bVST])
            TS1("vector", VST[:, ce2:ce2 + n], VST[:, ce2:ce2 + n], eps, OP.add, [bVST], [bVST])
            ACT(VST[:, ct2:ct2 + n], VST[:, ce2:ce2 + n], AF.Sqrt, [bVST], [bVST])
            RCP(VST[:, ce2:ce2 + n], VST[:, ct2:ct2 + n], [bVST], [bVST])

        bxn_d, by_d, bxm_d, bxq_d, bx1_d = (B_(x) for x in ("xn_d", "y_d", "xm_d", "xq_d", "x1_d"))

        def ln_stats(src_ap, W, src_bufs, mu_sl, rs_sl, eps):
            ACT(TRASH[:, :W], src_ap, AF.Identity, src_bufs, [bTRASH, bVST], accum=VST[:, mu_sl:mu_sl + 1])
            ACT(TRASH[:, :W], src_ap, AF.Square, src_bufs, [bTRASH, bVST], accum=VST[:, rs_sl:rs_sl + 1])
            TS1("vector", VST[:, mu_sl:mu_sl + 1], VST[:, mu_sl:mu_sl + 1], 1.0 / W, OP.mult, [bVST], [bVST])
            TS1("vector", VST[:, rs_sl:rs_sl + 1], VST[:, rs_sl:rs_sl + 1], 1.0 / W, OP.mult, [bVST], [bVST])
            TT("vector", VST[:, 60:61], VST[:, mu_sl:mu_sl + 1], VST[:, mu_sl:mu_sl + 1], OP.mult, [bVST], [bVST])
            TT("vector", VST[:, rs_sl:rs_sl + 1], VST[:, rs_sl:rs_sl + 1], VST[:, 60:61], OP.subtract, [bVST], [bVST])
            TS1("vector", VST[:, rs_sl:rs_sl + 1], VST[:, rs_sl:rs_sl + 1], eps, OP.add, [bVST], [bVST])
            ACT(VST[:, 61:62], VST[:, rs_sl:rs_sl + 1], AF.Sqrt, [bVST], [bVST])
            RCP(VST[:, rs_sl:rs_sl + 1], VST[:, 61:62], [bVST], [bVST])

        for t in range(NT):
            DMA("gpsimd", R0[:, t * D:(t + 1) * D], x_h[t * 128:(t + 1) * 128, :], "d_x",
                [bR0[t]], [bR0[t]])
        stats_batch(lambda t: (R0[:, t * D:(t + 1) * D], [bR0[t]]), NT, EPS1)
        for t in range(NT):
            TS2("vector", R0[:, t * D:(t + 1) * D], R0[:, t * D:(t + 1) * D],
                VST[:, t:t + 1], VST[:, 16 + t:17 + t], OP.subtract, OP.mult, [bR0[t], bVST], [bR0[t]])
        stats_batch(lambda t: (R0[:, t * D:(t + 1) * D], [bR0[t]]), NT, EPS1)
        for t in range(NT):
            dst, bdst = (SC2, bSC2) if t % 2 == 0 else (SC3, bSC3)
            TS2("vector", dst[:, :], R0[:, t * D:(t + 1) * D], VST[:, t:t + 1], VST[:, 16 + t:17 + t],
                OP.subtract, OP.mult, [bR0[t], bVST], [bdst])
            DMA("sync", xn_d[t * 128:(t + 1) * 128, :], dst[:, :], "d_b", [bdst], [bxn_d])

        for cc in range(ND):
            DMA("sync", TFM[:, cc * S:(cc + 1) * S], xn_d[:, cc * 128:(cc + 1) * 128], "d_t",
                [bxn_d, bTFM], [bTFM], transpose=True)
            TS2("vector", TFM[:, cc * S:(cc + 1) * S], TFM[:, cc * S:(cc + 1) * S],
                LNV[:, LN1W + cc:LN1W + cc + 1], LNV[:, LN1B + cc:LN1B + cc + 1],
                OP.mult, OP.add, [bTFM, bLNV], [bTFM])

        bw1q_d, bw2q_d = B_("w1q_d"), B_("w2q_d")
        bWQ = [B_("wq0"), B_("wq1")]

        def qw43f32(dst, src, W, ins, outs):
            ti, t2, pm = SCI[:, :W], SCI2[:, :W], GQI[:, :W]
            f8, l8 = F8Q[:, :W], L8Q[:, :W]
            T1, bT1 = SC1[:, :W], bSC1
            T2, bT2 = SC2[:, :W], bSC2
            T3, bT3 = SC3[:, :W], bSC3
            V = "vector"
            hi16 = src.bitcast(I16).rearrange("p (w two) -> p two w", two=2)[:, 1:2, :].rearrange("p o w -> p (o w)")
            TS1(V, ti, hi16, 0x7F80, OP.bitwise_and, list(ins), [bQX])
            TS2(V, t2, ti, 121 << 7, 384, OP.max, OP.subtract, [bQX], [bQX])
            TS2(V, pm, t2, -1, 32512, OP.mult, OP.add, [bQX], [bQX])
            TT(V, T1, src, pm.bitcast(BF16), OP.mult, list(ins) + [bQX], [bT1])
            TS2(V, T1, T1, VST[:, 64:65], 0.0, OP.mult, OP.add, [bT1, bVST], [bT1])
            TS2(V, T1, T1, M64C, M64C, OP.add, OP.subtract, [bT1], [bT1])
            TT(V, T1, T1, t2.bitcast(BF16), OP.mult, [bT1, bQX], [bT1])
            CP(V, f8, src, list(ins), [bQX])
            CP(V, l8, f8.bitcast(I8), [bQX], [bQX])
            CP(V, T2, l8.bitcast(F8E4), [bQX], [bT2])
            TS1(V, pm, t2, 6720, OP.add, [bQX], [bQX])
            TT(V, T3, T2, pm.bitcast(BF16), OP.add, [bT2, bQX], [bT3])
            TT(V, T3, T3, pm.bitcast(BF16), OP.subtract, [bT3, bQX], [bT3])
            TT(V, T2, T1, T2, OP.add, [bT1, bT2], [bT2])
            return TT(V, dst, T2, T3, OP.subtract, [bT2, bT3], list(outs))

        def wq_src_dst(j):
            if j < 32:
                return (w1_h[j * 128:(j + 1) * 128, :],
                        w1q_d[j * 128:(j + 1) * 128, :], bw1q_d)
            j2 = j - 32
            return (w2_h[(j2 // 4) * 128:(j2 // 4 + 1) * 128, (j2 % 4) * 1024:(j2 % 4) * 1024 + 1024],
                    w2q_d[(j2 // 4) * 128:(j2 // 4 + 1) * 128, (j2 % 4) * 1024:(j2 % 4) * 1024 + 1024], bw2q_d)

        bw1qT_d, bw2qT_d = B_("w1qT_d"), B_("w2qT_d")
        bTBO = B_("tbounce")

        def emit_w1qT(hh):
            for cc in range(ND):
                DMA("sync", PB[:, 16384:18432], w1q_d[hh * 2048:(hh + 1) * 2048, cc * 128:(cc + 1) * 128],
                    "d_t", [bw1q_d, bTBO], [bTBO], transpose=True)
                DMA("sync", w1qT_d[cc * 128:(cc + 1) * 128, hh * 2048:(hh + 1) * 2048], PB[:, 16384:18432],
                    "d_t", [bTBO], [bw1qT_d])

        def emit_w2qT(jj):
            DMA("sync", PB[:, 16384:17408], w2q_d[:, jj * 128:(jj + 1) * 128], "d_t",
                [bw2q_d, bTBO], [bTBO], transpose=True)
            DMA("sync", w2qT_d[jj * 128:(jj + 1) * 128, :], PB[:, 16384:17408], "d_t", [bTBO], [bw2qT_d])

        def emit_wq_load(j, extra_outs=()):
            src, _, _ = wq_src_dst(j)
            slab, bs = (SC0, bSC0) if j % 2 == 0 else (SCF, bSCF)
            DMA("sync", slab[:, :], src, "d_wq", [bs], [bs] + list(extra_outs))

        def emit_wq_quant(j):
            _, dst, bqd = wq_src_dst(j)
            slab, bs = (SC0, bSC0) if j % 2 == 0 else (SCF, bSCF)
            qw43f32(SC4[:, :], slab[:, :], 1024, [bs], [bSC4])
            DMA("sync", dst, SC4[:, :], "d_wq", [bSC4, bs], [bqd])

        def load_wslot(s, wh):
            DMA("gpsimd", PW[:, s * 8192:(s + 1) * 8192].rearrange("p (kd n) -> p kd n", kd=ND),
                wh.rearrange("(kd p) n -> p kd n", p=128), "d_w", [bSLOT[s]], [bSLOT[s]])

        def wslot_tile(s, k, m):
            return PW[:, s * 8192 + k * D + m * 128:s * 8192 + k * D + (m + 1) * 128]

        def wslot_mov(s, k, n):
            return PW[:, s * 8192 + k * D + n * 512:s * 8192 + k * D + n * 512 + 512]

        def xnT(k, lo, w):
            return TFM[:, k * S + lo:k * S + lo + w]

        QBo, KBo, KGo, VBo, GBo = 0, 4096, 8192, 12288, 16384
        bQB, bKB, bKG, bVB, bGB = B_("qb"), B_("kb"), B_("kg"), B_("vb"), B_("gb")

        for Bi in range(NBLK):
            lo = Bi * BLK
            load_wslot(0, wq_h)
            for m in range(ND):
                pb_ = bPS[m % 2]
                psn = ps[m % 2]
                for kk in range(ND):
                    MM(psn[:, :], wslot_tile(0, kk, m), xnT(kk, lo, BLK),
                       start=(kk == 0), stop=(kk == ND - 1),
                       ins=[bSLOT[0], bTFM], outs=[pb_], inc=(kk == ND - 1))
                ACT(PB[:, QBo + m * BLK:QBo + (m + 1) * BLK], psn[:, :], AF.Copy,
                    [pb_], [bQB], scale=0.125)
            load_wslot(1, wk_h)
            for m in range(ND):
                pb_ = bPS[2 + m % 2]
                psn = ps[2 + m % 2]
                for kk in range(ND):
                    MM(psn[:, :], wslot_tile(1, kk, m), xnT(kk, lo, BLK),
                       start=(kk == 0), stop=(kk == ND - 1),
                       ins=[bSLOT[1], bTFM], outs=[pb_], inc=(kk == ND - 1))
                ACT(PB[:, KBo + m * BLK:KBo + (m + 1) * BLK], psn[:, :], AF.Copy, [pb_], [bKB])
            for mt in range(4):
                for nn in range(2):
                    pb_ = bPS[(mt * 2 + nn) % 2]
                    psn = ps[(mt * 2 + nn) % 2]
                    for kk in range(ND):
                        MM(psn[:, :], xnT(kk, lo + mt * 128, 128), wslot_mov(1, kk, nn),
                           start=(kk == 0), stop=(kk == ND - 1),
                           ins=[bSLOT[1], bTFM], outs=[pb_], inc=(kk == ND - 1))
                    for hh in range(8):
                        h = nn * 8 + hh
                        TS1("vector",
                            PB[:, KGo + mt * 1024 + nn * 512 + hh * 64:KGo + mt * 1024 + nn * 512 + hh * 64 + 64],
                            psn[:, hh * 64:hh * 64 + 64], KGV[:, h:h + 1], OP.mult,
                            [pb_, bKGV], [bKG])
            load_wslot(0, wv_h)
            for mt in range(4):
                for nn in range(2):
                    pb_ = bPS[2 + (mt * 2 + nn) % 2]
                    psn = ps[2 + (mt * 2 + nn) % 2]
                    for kk in range(ND):
                        MM(psn[:, :], xnT(kk, lo + mt * 128, 128), wslot_mov(0, kk, nn),
                           start=(kk == 0), stop=(kk == ND - 1),
                           ins=[bSLOT[0], bTFM], outs=[pb_], inc=(kk == ND - 1))
                    ACT(PB[:, VBo + mt * 1024 + nn * 512:VBo + mt * 1024 + nn * 512 + 512],
                        psn[:, :], AF.Copy, [pb_], [bVB])
            load_wslot(1, wg_h)
            for mt in range(4):
                for nn in range(2):
                    pb_ = bPS[(mt * 2 + nn) % 2]
                    psn = ps[(mt * 2 + nn) % 2]
                    for kk in range(ND):
                        MM(psn[:, :], xnT(kk, lo + mt * 128, 128), wslot_mov(1, kk, nn),
                           start=(kk == 0), stop=(kk == ND - 1),
                           ins=[bSLOT[1], bTFM], outs=[pb_], inc=(kk == ND - 1))
                    ACT(PB[:, GBo + mt * 1024 + nn * 512:GBo + mt * 1024 + nn * 512 + 512],
                        psn[:, :], AF.Silu, [pb_], [bGB])

            for cc4 in range(4):
                g = Bi * 4 + cc4
                clo = cc4 * 128
                for p in range(NPAIR):
                    h1, h2 = 2 * p, 2 * p + 1
                    MM(ps[4][:, 0:128], PB[0:64, KBo + p * BLK + clo:KBo + p * BLK + clo + 128],
                       PB[0:64, QBo + p * BLK + clo:QBo + p * BLK + clo + 128],
                       start=True, stop=True, ins=[bKB, bQB], outs=[bPS[4]], inc=True)
                    MM(ps[7][:, 0:128], PB[64:128, KBo + p * BLK + clo:KBo + p * BLK + clo + 128],
                       PB[64:128, QBo + p * BLK + clo:QBo + p * BLK + clo + 128],
                       start=True, stop=True, ins=[bKB, bQB], outs=[bPS[7]], inc=True)
                    TT("vector", SC1[:, 0:128], ps[4][:, 0:128], DKT[:, h1 * 128:(h1 + 1) * 128],
                       OP.mult, [bPS[4], bDKT], [bSC1])
                    TT("vector", SC1[:, 128:256], ps[7][:, 0:128], DKT[:, h2 * 128:(h2 + 1) * 128],
                       OP.mult, [bPS[7], bDKT], [bSC1])
                    TT("vector", SC2[:, 0:128], PB[:, QBo + p * BLK + clo:QBo + p * BLK + clo + 128],
                       GMT[:, p * 128:(p + 1) * 128], OP.mult, [bQB, bGMT], [bSC2])
                    MM(ps[5][:, 0:64], SC1[:, 0:128],
                       PB[:, VBo + cc4 * 1024 + h1 * 64:VBo + cc4 * 1024 + h1 * 64 + 64],
                       start=True, stop=False, ins=[bSC1, bVB], outs=[bPS[5]], inc=False)
                    MM(ps[5][:, 64:128], SC1[:, 128:256],
                       PB[:, VBo + cc4 * 1024 + h2 * 64:VBo + cc4 * 1024 + h2 * 64 + 64],
                       start=False, stop=(g == 0), ins=[bSC1, bVB], outs=[bPS[5]], inc=(g == 0))
                    if g > 0:
                        MM(ps[5][:, 0:128], SC2[:, 0:128], PW[:, 16384 + p * 128:16384 + (p + 1) * 128],
                           start=False, stop=True, ins=[bSC2, bSBLK16], outs=[bPS[5]], inc=True)
                    MM(ps[6][:, 0:128], PB[:, KGo + cc4 * 1024 + p * 128:KGo + cc4 * 1024 + (p + 1) * 128],
                       PB[:, VBo + cc4 * 1024 + p * 128:VBo + cc4 * 1024 + (p + 1) * 128],
                       start=True, stop=True, ins=[bKG, bVB], outs=[bPS[6]], inc=True)
                    ACT(SC3[:, p * 128:(p + 1) * 128], ps[5][:, 0:128], AF.Copy, [bPS[5]], [bSC3])
                    STT(SBLK[0:64, p * 128:p * 128 + 64], SBLK[0:64, p * 128:p * 128 + 64],
                        GCV[0:64, p:p + 1], ps[6][0:64, 0:64], OP.mult, OP.add,
                        [bSBLK, bPS[6], bGCV], [bSBLK])
                    STT(SBLK[64:128, p * 128 + 64:p * 128 + 128], SBLK[64:128, p * 128 + 64:p * 128 + 128],
                        GCV[64:128, p:p + 1], ps[6][64:128, 64:128], OP.mult, OP.add,
                        [bSBLK, bPS[6], bGCV], [bSBLK])
                    CP("vector", PW[:, 16384 + p * 128:16384 + (p + 1) * 128],
                       SBLK[:, p * 128:(p + 1) * 128], [bSBLK], [bSBLK16])
                ACT(SC0[:, :], SC3[:, :], AF.Square, [bSC3], [bSC0])
                RED(VST[:, 0:16], SC3[:, :].rearrange("p (h d) -> p h d", h=16), OP.add, [bSC3], [bVST])
                RED(VST[:, 16:32], SC0[:, :].rearrange("p (h d) -> p h d", h=16), OP.add, [bSC0], [bVST])
                TS1("vector", VST[:, 0:16], VST[:, 0:16], 1.0 / 64, OP.mult, [bVST], [bVST])
                TS1("vector", VST[:, 16:32], VST[:, 16:32], 1.0 / 64, OP.mult, [bVST], [bVST])
                TT("vector", VST[:, 32:48], VST[:, 0:16], VST[:, 0:16], OP.mult, [bVST], [bVST])
                TT("vector", VST[:, 16:32], VST[:, 16:32], VST[:, 32:48], OP.subtract, [bVST], [bVST])
                TS1("vector", VST[:, 16:32], VST[:, 16:32], EPS2, OP.add, [bVST], [bVST])
                ACT(VST[:, 32:48], VST[:, 16:32], AF.Sqrt, [bVST], [bVST])
                RCP(VST[:, 16:32], VST[:, 32:48], [bVST], [bVST])
                for h in range(NH):
                    TS2("vector", SC3[:, h * 64:(h + 1) * 64], SC3[:, h * 64:(h + 1) * 64],
                        VST[:, h:h + 1], VST[:, 16 + h:17 + h], OP.subtract, OP.mult, [bSC3, bVST], [bSC3])
                TT("vector", SC3[:, :], SC3[:, :], PB[:, GBo + cc4 * 1024:GBo + (cc4 + 1) * 1024],
                   OP.mult, [bSC3, bGB], [bSC3])
                DMA("sync", y_d[g * 128:(g + 1) * 128, :], SC3[:, :], "d_b", [bSC3], [by_d])

        for cc in range(ND):
            DMA("sync", TFM[:, cc * S:(cc + 1) * S], y_d[:, cc * 128:(cc + 1) * 128], "d_t",
                [by_d, bTFM], [bTFM], transpose=True)
        load_wslot(0, wo_h)
        for mt in range(NT):
            for nn in range(2):
                pb_ = bPS[(mt * 2 + nn) % 4]
                psn = ps[(mt * 2 + nn) % 4]
                for kk in range(ND):
                    MM(psn[:, :], xnT(kk, mt * 128, 128), wslot_mov(0, kk, nn),
                       start=(kk == 0), stop=(kk == ND - 1),
                       ins=[bSLOT[0], bTFM], outs=[pb_], inc=(kk == ND - 1))
                TT("vector", R0[:, mt * D + nn * 512:mt * D + nn * 512 + 512],
                   psn[:, :], R0[:, mt * D + nn * 512:mt * D + nn * 512 + 512],
                   OP.add, [pb_, bR0[mt]], [bR0[mt]])

        stats_batch(lambda t: (R0[:, t * D:(t + 1) * D], [bR0[t]]), NT, EPS1)
        for t in range(NT):
            dst, bdst = (SC2, bSC2) if t % 2 == 0 else (SC3, bSC3)
            TS2("vector", dst[:, :], R0[:, t * D:(t + 1) * D], VST[:, t:t + 1], VST[:, 16 + t:17 + t],
                OP.subtract, OP.mult, [bR0[t], bVST], [bdst])
            DMA("sync", xm_d[t * 128:(t + 1) * 128, :], dst[:, :], "d_b", [bdst], [bxm_d])
        for cc in range(ND):
            DMA("sync", TFM[:, cc * S:(cc + 1) * S], xm_d[:, cc * 128:(cc + 1) * 128], "d_t",
                [bxm_d, bTFM], [bTFM], transpose=True)
            TS2("vector", TFM[:, cc * S:(cc + 1) * S], TFM[:, cc * S:(cc + 1) * S],
                LNV[:, LN2W + cc:LN2W + cc + 1], LNV[:, LN2B + cc:LN2B + cc + 1],
                OP.mult, OP.add, [bTFM, bLNV], [bTFM])

        for k in range(NKH):
            w = HID_T[k]
            if k == 0:
                DMA("gpsimd", PW[0:w, k * D:(k + 1) * D], wout_h[k * 128:k * 128 + w, :],
                    "d_w2", [bPW], [bPW, bSLOT[0], bSLOT[1], bSBLK16])
            else:
                DMA("gpsimd", PW[0:w, k * D:(k + 1) * D], wout_h[k * 128:k * 128 + w, :],
                    "d_w2", [bPW], [bPW])
        emit_wq_load(0)
        for tb in range(4):
            tlo = tb * 512
            for k in range(NKH):
                wqj = tb * NKH + k
                if wqj < 64:
                    if wqj + 1 < 64:
                        emit_wq_load(wqj + 1)
                    emit_wq_quant(wqj)
                if wqj == 17:
                    emit_w1qT(0)
                elif wqj == 34:
                    emit_w1qT(1)
                elif 64 <= wqj < 64 + (NJ + 1) // 2:
                    emit_w2qT((wqj - 64) * 2)
                    emit_w2qT((wqj - 64) * 2 + 1)
                w = HID_T[k]
                for half, hofs in ((0, 0), (1, HID)):
                    DMA("gpsimd", WST[:, half * 1024:half * 1024 + 8 * w].rearrange("p (kd n) -> p kd n", kd=ND),
                        win_h[:, hofs + k * 128:hofs + k * 128 + w].rearrange("(kd p) n -> p kd n", p=128),
                        "d_w", [bWST], [bWST])
                for half, pbank in ((0, 0), (1, 1)):
                    for kk in range(ND):
                        MM(ps[pbank][0:w, :], WST[:, half * 1024 + kk * w:half * 1024 + (kk + 1) * w],
                           xnT(kk, tlo, 512), start=(kk == 0), stop=(kk == ND - 1),
                           ins=[bWST, bTFM], outs=[bPS[pbank]], inc=(kk == ND - 1))
                ACT(SC1[0:w, 0:512], ps[0][0:w, :], AF.Identity, [bPS[0], bBIV], [bSC1], bias=BIV[0:w, k:k + 1])
                ACT(SC2[0:w, 0:512], ps[1][0:w, :], AF.Silu, [bPS[1], bBIV], [bSC2], bias=BIV[0:w, NKH + k:NKH + k + 1])
                TT("vector", PB[0:w, k * 512:k * 512 + 512], SC1[0:w, 0:512], SC2[0:w, 0:512],
                   OP.mult, [bSC1, bSC2], [bPB])
            for mt in range(4):
                t_glob = tb * 4 + mt
                for nn in range(2):
                    pb_ = bPS[2 + (mt * 2 + nn) % 2]
                    psn = ps[2 + (mt * 2 + nn) % 2]
                    for k in range(NKH):
                        w = HID_T[k]
                        MM(psn[:, :], PB[0:w, k * 512 + mt * 128:k * 512 + mt * 128 + 128],
                           PW[0:w, k * D + nn * 512:k * D + nn * 512 + 512],
                           start=(k == 0), stop=False, ins=[bPB, bPW], outs=[pb_], inc=False)
                    MM(psn[:, :], ONES[0:1, 0:128], ROWB[0:1, nn * 512:nn * 512 + 512],
                       start=False, stop=True, ins=[bONES, bROWB], outs=[pb_], inc=True)
                    TT("vector", R0[:, t_glob * D + nn * 512:t_glob * D + nn * 512 + 512],
                       psn[:, :], R0[:, t_glob * D + nn * 512:t_glob * D + nn * 512 + 512],
                       OP.add, [pb_, bR0[t_glob]], [bR0[t_glob]])


        stats_batch(lambda t: (R0[:, t * D:(t + 1) * D], [bR0[t]]), NT, EPS1)
        for t in range(NT):
            dst, bdst = (SC2, bSC2) if t % 2 == 0 else (SC3, bSC3)
            xf32, bxf32 = (SC0, bSC0) if t % 2 == 0 else (SCF, bSCF)
            DMA("sync", xf32[:, :], x_h[t * 128:(t + 1) * 128, :], "d_c", [bxf32], [bxf32])
            TS2("vector", dst[:, :], R0[:, t * D:(t + 1) * D], VST[:, t:t + 1], VST[:, 16 + t:17 + t],
                OP.subtract, OP.mult, [bR0[t], bVST], [bdst])
            TT("vector", R0[:, t * D:(t + 1) * D], dst[:, :], xf32[:, :], OP.add,
               [bdst, bxf32, bR0[t]], [bR0[t]])
            if dbg:
                DMA("sync", x1_d[t * 128:(t + 1) * 128, :], R0[:, t * D:(t + 1) * D], "d_b",
                    [bR0[t]], [bx1_d])
            ACT(TRASH[:, :D], R0[:, t * D:(t + 1) * D], AF.Identity, [bR0[t]], [bTRASH, bVST], accum=VST[:, 32 + t:33 + t])
            ACT(TRASH[:, :D], R0[:, t * D:(t + 1) * D], AF.Square, [bR0[t]], [bTRASH, bVST], accum=VST[:, 48 + t:49 + t])
        TS1("vector", VST[:, 32:48], VST[:, 32:48], 1.0 / D, OP.mult, [bVST], [bVST])
        TS1("vector", VST[:, 48:64], VST[:, 48:64], 1.0 / D, OP.mult, [bVST], [bVST])
        TT("vector", VST[:, 0:16], VST[:, 32:48], VST[:, 32:48], OP.mult, [bVST], [bVST])
        TT("vector", VST[:, 48:64], VST[:, 48:64], VST[:, 0:16], OP.subtract, [bVST], [bVST])
        TS1("vector", VST[:, 48:64], VST[:, 48:64], EPS1, OP.add, [bVST], [bVST])
        ACT(VST[:, 0:16], VST[:, 48:64], AF.Sqrt, [bVST], [bVST])
        RCP(VST[:, 48:64], VST[:, 0:16], [bVST], [bVST])
        for t in range(NT):
            dst, bdst = (SC2, bSC2) if t % 2 == 0 else (SC3, bSC3)
            TS2("vector", dst[:, :], R0[:, t * D:(t + 1) * D], VST[:, 32 + t:33 + t], VST[:, 48 + t:49 + t],
                OP.subtract, OP.mult, [bR0[t], bVST], [bdst])
            if "xf" in dbg:
                DMA("gpsimd", dbg_h["xf"][t * 128:(t + 1) * 128, :], dst[:, :], "d_g",
                    [bdst], [B_("xfdbg")])
            qx(dst[:, :], dst[:, :], 1024, fl_wide, "43bf", [bdst], [bdst])
            DMA("sync", xq_d[t * 128:(t + 1) * 128, :], dst[:, :], "d_b", [bdst], [bxq_d])

        bout_dma = B_("out")
        bhq = [B_("hq0"), B_("hq1")]
        bxqT = [B_("xqT0"), B_("xqT1")]

        def h_phase(tb):
            par = tb % 2
            hqo = par * 8192
            tlo = tb * FTB
            for cc in range(ND):
                DMA("sync", TFM[:, cc * S + par * FTB:cc * S + par * FTB + FTB],
                    xq_d[tlo:tlo + FTB, cc * 128:(cc + 1) * 128],
                    "d_t", [bxq_d, bxqT[par]], [bxqT[par]], transpose=True)
            for js in range(NJ):
                if js % 2 == 0:
                    wbuf, bwbuf, wch = WST[:, 0:1024], B_("wstA"), "d_fa"
                else:
                    wbuf, bwbuf, wch = PB[:, 16384:17408], B_("pbw1"), "d_fb"
                extra_outs = [bwbuf] if not (tb == 0 and js < 2) else [bwbuf, bPB if js == 1 else bWST]
                DMA("sync", wbuf.rearrange("p (kd n) -> p kd n", kd=ND),
                    w1qT_d[:, js * 128:(js + 1) * 128].rearrange("(kd p) n -> p kd n", p=128),
                    wch, [bw1qT_d], extra_outs)
                pb_ = bPS[4 + js % 2]
                psn = ps[4 + js % 2]
                for kk in range(ND):
                    MM(psn[:, 0:FTB], wbuf[:, kk * 128:(kk + 1) * 128],
                       TFM[:, kk * S + par * FTB:kk * S + par * FTB + FTB],
                       start=(kk == 0), stop=(kk == ND - 1), ins=[bwbuf, bxqT[par]],
                       outs=[pb_], inc=(kk == ND - 1))
                ACT(SCH1[:, (js % 4) * FTB:(js % 4 + 1) * FTB], psn[:, 0:FTB], AF.Relu,
                    [pb_, bQB1], [bSH1], bias=QB1[:, js:js + 1])
                if js % 4 == 3:
                    qpair_h(PB[:, hqo + (js - 3) * FTB:hqo + (js + 1) * FTB], SCH1[:, 0:1024],
                            1024, [bSH1], [bhq[par]])
                    if "hq" in dbg:
                        DMA("gpsimd",
                            dbg_h["hq"][(js - 3) * 128:(js + 1) * 128, tlo:tlo + FTB].rearrange("(a p) n -> p a n", p=128),
                            PB[:, hqo + (js - 3) * FTB:hqo + (js + 1) * FTB].rearrange("p (a n) -> p a n", a=4),
                            "d_g", [bhq[par]], [B_("dbgout")])

        def y_phase(tb):
            par = tb % 2
            hqo = par * 8192
            for js in range(NJ):
                if js % 2 == 0:
                    ybuf, bybuf, ych = WST[:, 1024:2048], B_("wstB"), "d_fc"
                else:
                    ybuf, bybuf, ych = PB[:, 17408:18432], B_("pbw2"), "d_fd"
                DMA("sync", ybuf, w2qT_d[js * 128:(js + 1) * 128, :], ych, [bw2qT_d], [bybuf])
                for sub in range(2):
                    for ih in range(2):
                        bidx = sub * 2 + ih
                        MM(ps[bidx][:, :], PB[:, hqo + js * FTB + sub * 128:hqo + js * FTB + sub * 128 + 128],
                           ybuf[:, ih * 512:ih * 512 + 512],
                           start=(js == 0), stop=False, ins=[bhq[par], bybuf], outs=[bPS[bidx]],
                           inc=(sub == 1 and ih == 1))
            mm_last = None
            for sub in range(2):
                for ih in range(2):
                    bidx = sub * 2 + ih
                    mm_last = MM(ps[bidx][:, :], ONES[0:1, 0:128],
                                 ROWB[0:1, 1024 + ih * 512:1024 + ih * 512 + 512],
                                 start=False, stop=True, ins=[bONES, bROWB], outs=[bPS[bidx]], inc=True)
            mm_bar = em.do("tensor", lambda e, c: e.nop(), extra=[mm_last], inc=True)
            for b_ in range(4):
                bPS[b_].w = mm_bar

        def tail_phase(tb):
            for sub in range(2):
                t_glob = tb * 2 + sub
                for ih in range(2):
                    bidx = sub * 2 + ih
                    o = ih * 512
                    ACT(SYE[:, o:o + 512], ps[bidx][:, :], AF.Copy, [bPS[bidx]], [bSYE])
                    qy510(SYE[:, o:o + 512], SYE[:, o:o + 512], 512, [bSYE], [bSYE])
                DMA("sync", SCH2[:, :], mask_h[t_glob * 128:(t_glob + 1) * 128, 0:512], "d_m", [bSH2], [bSH2])
                DMA("sync", SCH3[:, :], mask_h[t_glob * 128:(t_glob + 1) * 128, 512:1024], "d_m", [bSH3], [bSH3])
                TT("vector", SYE[:, 0:512], SYE[:, 0:512], SCH2[:, :], OP.mult, [bSYE, bSH2], [bSYE])
                TT("vector", SYE[:, 512:1024], SYE[:, 512:1024], SCH3[:, :], OP.mult, [bSYE, bSH3], [bSYE])
                if "ym" in dbg:
                    DMA("gpsimd", dbg_h["ym"][t_glob * 128:(t_glob + 1) * 128, :], SYE[:, :],
                        "d_g", [bSYE], [B_("dbgout")])
                ACT(TRASH[:, :D], SYE[:, :], AF.Identity, [bSYE], [bTRASH, bVST], accum=VST[:, 0:1])
                ACT(TRASH[:, :D], SYE[:, :], AF.Square, [bSYE], [bTRASH, bVST], accum=VST[:, 1:2])
                TS1("vector", VST[:, 0:1], VST[:, 0:1], 1.0 / D, OP.mult, [bVST], [bVST])
                TS1("vector", VST[:, 1:2], VST[:, 1:2], 1.0 / D, OP.mult, [bVST], [bVST])
                TT("vector", VST[:, 2:3], VST[:, 0:1], VST[:, 0:1], OP.mult, [bVST], [bVST])
                TT("vector", VST[:, 1:2], VST[:, 1:2], VST[:, 2:3], OP.subtract, [bVST], [bVST])
                TS1("vector", VST[:, 1:2], VST[:, 1:2], EPS1, OP.add, [bVST], [bVST])
                ACT(VST[:, 2:3], VST[:, 1:2], AF.Sqrt, [bVST], [bVST])
                RCP(VST[:, 1:2], VST[:, 2:3], [bVST], [bVST])
                TS2("vector", SC2[:, :], SYE[:, :], VST[:, 0:1], VST[:, 1:2],
                    OP.subtract, OP.mult, [bSYE, bVST], [bSC2])
                TT("vector", SC0[:, :], SC2[:, :], R0[:, t_glob * D:(t_glob + 1) * D], OP.add,
                   [bSC2, bR0[t_glob]], [bSC0])
                DMA("sync", out_h[t_glob * 128:(t_glob + 1) * 128, :], SC0[:, :], "d_o", [bSC0], [bout_dma])

        h_phase(0)
        for tb in range(NFTB):
            y_phase(tb)
            if tb + 1 < NFTB:
                h_phase(tb + 1)
            tail_phase(tb)

        for n in dbg:
            src = {"xn": xn_d, "y": y_d, "xm": xm_d, "xq": xq_d, "x1": x1_d}.get(n)
            if src is None:
                continue
            DMA("gpsimd", dbg_h[n], src, "d_g", [B_(n + "_d")], [B_("dbgout")])

        em.do("tensor", lambda e, c: e.nop(), extra=[(ESEM["tensor"], em.cnt["tensor"])], inc=True)
        em.do("vector", lambda e, c: e.nop(), extra=[(ESEM["vector"], em.cnt["vector"])], inc=True)
        em.do("scalar", lambda e, c: e.nop(), extra=[(ESEM["scalar"], em.cnt["scalar"])], inc=True)
        em.do("gpsimd", lambda e, c: e.nop(), extra=[(ESEM["gpsimd"], em.cnt["gpsimd"])], inc=True)
        fin_extra = [("d_o", em.dcnt["d_o"])]
        if em.dcnt["d_g"]:
            fin_extra.append(("d_g", em.dcnt["d_g"]))
        em.do("sync", lambda e, c: e.nop(), extra=fin_extra, inc=False)

        ctx = {"waited": {}, "sem": sems, "eng": {}}

        def run_queue(name, handle):
            ctx["eng"][name] = handle
            for f in em.q[name]:
                f(ctx)

        @block.sync
        def _(e):
            run_queue("sync", e)

        @block.vector
        def _(e):
            run_queue("vector", e)

        @block.scalar
        def _(e):
            run_queue("scalar", e)

        @block.gpsimd
        def _(e):
            run_queue("gpsimd", e)

        @block.tensor
        def _(e):
            run_queue("tensor", e)

    return nc


def _host_constants():
    i = np.arange(128, dtype=np.float32)
    trel = i[None, :] - i[:, None]
    trel = np.where(trel >= 0, trel, np.float32(1e9)).astype(np.float32)
    tip1 = np.broadcast_to(i[None, :] + 1.0, (128, 128)).astype(np.float32)
    trev = (127.0 - i)[:, None].astype(np.float32)
    ident = np.eye(128, dtype=np.float32)
    return trel, tip1, trev, ident


def _dropout_mask():
    import os
    import jax
    ctx = (jax.experimental.enable_x64() if os.environ.get("BASS_SIM_FP64") == "1"
           else jax.experimental.disable_x64())
    with jax.default_device(jax.devices("cpu")[0]), ctx:
        keep = jax.random.bernoulli(jax.random.key(42, impl="threefry2x32"), 1.0 - 0.2, (8, S, D))
        keep = np.asarray(keep)
    return np.where(keep, np.float16(1.25), np.float16(0.0)).astype(np.float16)


def kernel(L0_attn_x, L0_attn_ln1_w, L0_attn_ln1_b, L0_attn_wq, L0_attn_wk, L0_attn_wv,
           L0_attn_wg, L0_attn_wo, L0_attn_ln2_w, L0_attn_ln2_b, L0_attn_w_in, L0_attn_b_in,
           L0_attn_w_out, L0_attn_b_out, L0_ffn_w1, L0_ffn_b1, L0_ffn_w2, L0_ffn_b2):
    x = np.ascontiguousarray(np.asarray(L0_attn_x, dtype=np.float32))
    trel, tip1, trev, ident = _host_constants()
    mask = _dropout_mask()
    common = {
        "wq": np.ascontiguousarray(L0_attn_wq, np.float32),
        "wk": np.ascontiguousarray(L0_attn_wk, np.float32),
        "wv": np.ascontiguousarray(L0_attn_wv, np.float32),
        "wg": np.ascontiguousarray(L0_attn_wg, np.float32),
        "wo": np.ascontiguousarray(L0_attn_wo, np.float32),
        "ln1w": np.ascontiguousarray(L0_attn_ln1_w, np.float32),
        "ln1b": np.ascontiguousarray(L0_attn_ln1_b, np.float32),
        "ln2w": np.ascontiguousarray(L0_attn_ln2_w, np.float32),
        "ln2b": np.ascontiguousarray(L0_attn_ln2_b, np.float32),
        "win": np.ascontiguousarray(L0_attn_w_in, np.float32),
        "bin": np.ascontiguousarray(L0_attn_b_in, np.float32),
        "wout": np.ascontiguousarray(L0_attn_w_out, np.float32),
        "bout": np.ascontiguousarray(L0_attn_b_out, np.float32),
        "w1": np.ascontiguousarray(L0_ffn_w1, np.float32),
        "b1": np.ascontiguousarray(L0_ffn_b1, np.float32),
        "w2": np.ascontiguousarray(L0_ffn_w2, np.float32),
        "b2": np.ascontiguousarray(L0_ffn_b2, np.float32),
        "trel": trel, "tip1": tip1, "trev": trev, "ident": ident,
    }
    nc = build_nc()
    in_maps = []
    for core in range(8):
        m = dict(common)
        m["x"] = np.ascontiguousarray(x[core])
        m["mask"] = np.ascontiguousarray(mask[core])
        in_maps.append(m)
    res = run_bass_kernel_spmd(nc, in_maps, core_ids=list(range(8)))
    return np.stack([r["out"] for r in res.results], axis=0)
